# version 06
# baseline (speedup 1.0000x reference)
import numpy as np
import ml_dtypes
import concourse.bass as bass
import concourse.mybir as mybir
import bass_rust
from concourse.bass_utils import run_bass_kernel_spmd

F32 = mybir.dt.float32
BF16 = mybir.dt.bfloat16
AF = mybir.ActivationFunctionType
ALU = mybir.AluOpType
BF_NP = ml_dtypes.bfloat16

T = 2048
C = 1024
CT = 8
CH = 512
NCH = T // CH
H = 16
SHIFT = 64.0
EPS = 1e-5
ALIBI_GRAN = [128 if h < 2 else (256 if h < 4 else 512) for h in range(H)]


def host_constants():
    c = {}
    c["cJ"] = np.repeat((np.arange(CH, dtype=np.float32) + 1.0)[None, :], 128, 0)
    c["cIdentF"] = np.eye(128, dtype=np.float32)
    c["cIdentB"] = np.eye(128, dtype=BF_NP)
    c["cDiag"] = (np.arange(128)[None, :] >= np.arange(128)[:, None]).astype(BF_NP)
    c["cOnes"] = np.ones((128, 512), dtype=BF_NP)
    slopes = np.array([2.0 ** (-8.0 * (i + 1) / H) for i in range(H)], np.float64)
    bt = np.zeros((128, 256), np.float32)
    p = np.arange(128, dtype=np.float64)
    for h in range(H):
        gran = ALIBI_GRAN[h]
        dmin = -(16 - gran // 128)
        for idx in range(16):
            bt[:, h * 16 + idx] = (slopes[h] * p + slopes[h] * 128.0 * (dmin + idx)
                                   - SHIFT).astype(np.float32)
    c["cBias"] = bt
    cv = np.zeros((128, 64), np.float32)
    pq = np.arange(128, dtype=np.float64)
    for h in range(H):
        gran = ALIBI_GRAN[h]
        for i2 in range(4):
            fp = (128.0 * i2 + pq) % gran
            cv[:, h * 4 + i2] = np.exp(64.0 - slopes[h] * fp).astype(np.float32)
    c["cCVT"] = cv
    p2 = np.zeros((32, 512), BF_NP)
    for b in range(32):
        p2[b, b * 16:(b + 1) * 16] = 1
    c["cFP2"] = p2
    p2b = np.zeros((16, 512), BF_NP)
    for e in range(16):
        p2b[e, e::16] = 1
    c["cFP2B"] = p2b
    fsd = np.zeros((32, 512), BF_NP)
    for p in range(512):
        fsd[p // 16, p] = 1
    c["cFSD"] = fsd
    fsg = np.zeros((16, 512), BF_NP)
    for p in range(512):
        fsg[p % 16, p] = 1
    c["cFSG"] = fsg
    half = 512
    freqs = 1.0 / 10000.0 ** (np.arange(half, dtype=np.float32) / half)
    ang = np.arange(T, dtype=np.float32)[None, :] * freqs[:, None]
    c["cCos"] = np.cos(ang).astype(np.float32).reshape(4, 128, T).transpose(
        1, 0, 2).reshape(128, 4 * T).astype(BF_NP)
    c["cSin"] = np.sin(ang).astype(np.float32).reshape(4, 128, T).transpose(
        1, 0, 2).reshape(128, 4 * T).astype(BF_NP)
    return c


class Sem:
    def __init__(self, handle, name):
        self.h = handle
        self.name = name
        self.n = 0


class EQ:
    def __init__(self, name):
        self.name = name
        self.items = []
        self.auto_edge = False
        self.esem = None

    def wait(self, sem, val=None):
        v = sem.n if val is None else val
        if v > 0:
            self.items.append(("wait", sem, v))

    def op(self, fn, inc=None, inc_n=1):
        incs = []
        if inc is not None:
            inc.n += inc_n
            incs.append((inc, inc_n))
        if self.auto_edge and inc is not self.esem:
            self.esem.n += 1
            incs.append((self.esem, 1))
        self.items.append(("op", fn, incs))
        return incs

    def dma(self, fn, sem):
        return self.op(fn, inc=sem, inc_n=16)

    def opE(self, fn, esem):
        self.op(fn, inc=esem)
        self.wait(esem)

    def last_op_incs(self):
        for it in reversed(self.items):
            if it[0] == "op":
                return it[2]
        return None


def build(n_sublayers=4):
    nc = bass.Bass("TRN2", target_bir_lowering=False)
    import os as _os
    if not int(_os.environ.get("KRACE", "0")):
        nc.detect_race_conditions = False

    din = {}
    def dram_in(name, shape, dtype=F32):
        din[name] = nc.dram_tensor(name, shape, dtype, kind="ExternalInput").ap()
        return din[name]

    xin = dram_in("xin", [T, C])
    for nm in ["a_Wk", "a_Wv", "a_Wr", "a_Wo", "b_wq", "b_wk", "b_wv", "b_out_w"]:
        dram_in(nm, [C, C])
    dram_in("a_par", [40, 128])
    dram_in("b_out_b", [1, C])
    dram_in("f_G1R", [32, 128])
    dram_in("f_G2R", [16, 64])
    dram_in("f_G1I", [32, 128])
    dram_in("f_G2I", [16, 64])
    dram_in("f_bias", [1, C])
    dram_in("cJ", [128, CH])
    dram_in("cIdentF", [128, 128])
    dram_in("cIdentB", [128, 128], BF16)
    dram_in("cDiag", [128, 128], BF16)
    dram_in("cOnes", [128, 512], BF16)
    dram_in("cBias", [128, 256])
    dram_in("cCVT", [128, 64])
    dram_in("cCos", [128, 4 * T], BF16)
    dram_in("cFP2", [32, 512], BF16)
    dram_in("cFP2B", [16, 512], BF16)
    dram_in("cFSD", [32, 512], BF16)
    dram_in("cFSG", [16, 512], BF16)
    dram_in("cSin", [128, 4 * T], BF16)
    out = nc.dram_tensor("out", [T, C], F32, kind="ExternalOutput").ap()
    dbg = nc.dram_tensor("dbg", [128, 16384], BF16, kind="ExternalOutput").ap()
    dbgf = nc.dram_tensor("dbgf", [128, 512], F32, kind="ExternalOutput").ap()
    wstash = nc.dram_tensor("wstash", [128, 8192], BF16, kind="Internal").ap()

    tens = {}
    def sb(name, shape, dtype, offset):
        t = nc.alloc_sbuf_tensor_at(name, shape, dtype, offset=offset)
        tens[name] = t.ap()
        return tens[name]

    SB_BASE = 16512
    OFF = SB_BASE
    def place(name, shape, dtype):
        nonlocal OFF
        szb = {F32: 4, BF16: 2}[dtype]
        w = szb
        for d in shape[1:]:
            w *= d
        at = OFF
        OFF = at + ((w + 63) // 64) * 64
        return sb(name, shape, dtype, at), at

    X, _ = place("X", [128, CT * T], F32)
    cJ, _ = place("sJ", [128, CH], F32)
    cIdF, _ = place("sIdF", [128, 128], F32)
    cIdB, _ = place("sIdB", [128, 128], BF16)
    cDg, _ = place("sDiag", [128, 128], BF16)
    cOn, _ = place("sOnes", [128, 512], BF16)
    cBt, _ = place("sBias", [128, 256], F32)
    cCV, _ = place("sCVT", [128, 64], F32)
    PAR, _ = place("PAR", [128, 32], F32)
    PARB, _ = place("PARB", [128, 48], BF16)
    PARF, _ = place("PARF", [128, 48], F32)
    AUGA, _ = place("AUGA", [128, 1024], BF16)
    AUGB, _ = place("AUGB", [128, 1024], BF16)
    AUGC, _ = place("AUGC", [128, 1024], BF16)
    EPSB, _ = place("EPSB", [128, 1], F32)
    PRAW, _ = place("PRAW", [128, 40], F32)
    PPB, _ = place("PPB", [128, 8], F32)
    AAC, _ = place("AAC", [128, 8], F32)
    BBC, _ = place("BBC", [128, 8], F32)
    MRB, MRB_at = place("MRB", [128, T + 1], BF16)
    RSTD, RSTD_at = place("RSTD", [128, T], F32)
    XS, XS_at = place("XS", [128, CT * (CH + 1)], BF16)
    W0, W0_at = place("W0", [128, 8192], BF16)
    W1, W1_at = place("W1", [128, 8192], BF16)
    W2, W2_at = place("W2", [128, 8192], BF16)
    W3, W3_at = place("W3", [128, 8192], BF16)
    VAR_at = OFF
    SB_LIMIT = 229312
    assert VAR_at + 47000 <= SB_LIMIT, VAR_at

    V = VAR_at
    XSTG = sb("XSTG", [128, 2048], F32, V)
    WNAT = sb("WNAT", [128, 8192], BF16, V + 8192)
    XBF = sb("XBF", [128, CT * CH], BF16, V)
    XSQ = sb("XSQ", [128, CT * CH], BF16, V + 8192)
    MST = sb("MST", [128, CH], F32, V + 16384)
    VST = sb("VST", [128, CH], F32, V + 18432)
    VST2 = sb("VST2", [128, CH], F32, V + 20480)
    MIXA = sb("MIXA", [128, CT * CH], BF16, V)
    MIXB = sb("MIXB", [128, CT * CH], BF16, V + 8192)
    YG = sb("YG", [128, CT * CH], BF16, V + 16384)
    ZP = [sb(f"ZP{i}", [128, CH], F32, V + 24576 + i * 2048) for i in range(5)]
    CB = sb("CB", [128, CH + 1], F32, V + 34816)
    ABF = sb("ABF", [128, CH + 1], F32, V + 36992)
    BBF = sb("BBF", [128, CH + 1], F32, V + 39168)
    TMPA = sb("TMPA", [128, CH], BF16, V + 41344)
    TMPB = sb("TMPB", [128, CH], BF16, V + 42368)
    TMPG1 = sb("TMPG1", [128, CH], BF16, V + 43392)
    TMPG2 = sb("TMPG2", [128, CH], BF16, V + 44416)
    assert V + 45440 <= SB_LIMIT, V
    FO1 = sb("FO1", [128, 512], BF16, V)
    FEA = sb("FEA", [128, 512], F32, V + 1024)
    FG1R = sb("FG1R", [128, 128], BF16, V + 3072)
    FG2R = sb("FG2R", [128, 64], BF16, V + 3328)
    FG1I = sb("FG1I", [128, 128], BF16, V + 3456)
    FG2I = sb("FG2I", [128, 64], BF16, V + 3712)
    FP2 = sb("FP2", [128, 512], BF16, V + 3840)
    FP2B = sb("FP2B", [128, 512], BF16, V + 4864)
    FSD = sb("FSD", [128, 512], BF16, V + 5888)
    FSG = sb("FSG", [128, 512], BF16, V + 6912)
    FWT = [sb(f"FWT{i}", [128, 512], F32, V + 7936 + 2048 * i) for i in range(8)]
    QTXS = sb("QTXS", [128, CT * T], BF16, V)
    WINB = sb("WINB", [128, 8192], BF16, RSTD_at)
    VN = []
    for j in range(16):
        if j < 7:
            at = W2_at + j * 2112
        elif j < 14:
            at = W3_at + (j - 7) * 2112
        else:
            at = V + 32768 + (j - 14) * 2112
        VN.append(sb(f"VN{j}", [128, 1040], BF16, at))
    BW2 = V + 32768 + 4608
    ROPE = sb("ROPEB", [128, 1024], BF16, BW2)
    RTS = sb("RTS", [128, 1024], BF16, BW2 + 2048)
    QP8 = sb("QP8", [128, 2048], BF16, BW2 + 4096)
    TMPC = sb("TMPC", [128, 256], BF16, BW2 + 8192)
    TMPD = sb("TMPD", [128, 256], BF16, BW2 + 8704)
    OTB = sb("OTB", [128, CT * 512], BF16, BW2)
    WNAT2 = sb("WNAT2", [128, 4096], BF16, BW2)
    FO2 = sb("FO2", [128, 512], BF16, BW2 + 2048)
    PT0 = sb("PT0", [128, 512], BF16, MRB_at)
    PT1 = sb("PT1", [128, 512], BF16, MRB_at + 1024)
    PTD = [PT0, PT1]
    PSC = sb("PSC", [128, 128], BF16, MRB_at + 2048)
    STG64 = sb("STG64", [128, 256], BF16, MRB_at + 2304)
    RCB = sb("RCB", [128, 12], F32, MRB_at + 2816)
    LNZ = sb("LNZ", [128, 256], F32, MRB_at + 3072)
    OST = sb("OST", [128, 2 * 1024], F32, V)

    PS = [nc.alloc_psum_tensor(f"ps{i}", [128, 512], F32).ap() for i in range(7)]
    PST = nc.alloc_psum_tensor("pst", [128, 1024], BF16).ap()

    sem_names = ["sIN", "sGPD", "sPE", "sDV", "sAC", "sGE", "sSY", "sST",
                 "sCN", "sXA", "sXB", "sSA", "sSB"]
    sem_ctx = [nc.semaphore(n) for n in sem_names]
    handles = [ctx.__enter__() for ctx in sem_ctx]
    sems = {n: Sem(h, n) for n, h in zip(sem_names, handles)}
    sIN, sGPD, sPE, sDV, sAC, sGE, sSY, sST, sCN, sXA, sXB, sSA, sSB = (
        sems[n] for n in sem_names)
    all_sems = list(sems.values())
    esem = {"sync": sSY, "gpsimd": sGE, "vector": sDV, "scalar": sAC, "tensor": sPE}
    q = {n: EQ(n) for n in ("sync", "gpsimd", "vector", "scalar", "tensor")}
    for n in ("vector", "scalar"):
        q[n].auto_edge = True
        q[n].esem = esem[n]
    SY, GP, DV, SC, PE = (q[n] for n in ("sync", "gpsimd", "vector", "scalar", "tensor"))

    def barrier():
        for en, qq in q.items():
            incs = qq.last_op_incs()
            if incs is not None and len(incs) == 0:
                esem[en].n += 1
                incs.append((esem[en], 1))
        snap = [(s, s.n) for s in all_sems if s.n > 0]
        for qq in q.values():
            for s, v in snap:
                qq.wait(s, v)

    blk_ctx = nc.Block()
    blk = blk_ctx.__enter__()

    def cpy(eng, out, in_):
        if hasattr(eng, "tensor_copy"):
            return eng.tensor_copy(out=out, in_=in_)
        return eng.copy(out=out, in_=in_)

    def mm(psum, lhsT, rhs, start, stop, inc=None):
        PE.op(lambda eng, a=psum, b=lhsT, c=rhs, s1=start, s2=stop:
              eng.matmul(a, b, c, start=s1, stop=s2), inc=inc)

    def wsl(w, ct, dblk):
        return w[:, ct * 1024 + dblk * 128: ct * 1024 + dblk * 128 + 128]

    def proj_group(psum, wt, i, rhs_fn, aug1, aug2, inc=None):
        for ct in range(CT):
            mm(psum, wsl(wt, ct, i), rhs_fn(ct), start=(ct == 0), stop=False)
            if ct == 3 and aug1 is not None:
                mm(psum, aug1[0], aug1[1], start=False, stop=False)
        last = aug2 if aug2 is not None else aug1
        if last is None:
            raise ValueError
        mm(psum, last[0], last[1], start=False, stop=True, inc=inc)

    Xv = X.rearrange("p (ct t) -> p ct t", t=T)
    XSv = XS.rearrange("p (ct j) -> p ct j", j=CH + 1)
    def par_w(i): return PAR[:, i:i + 1]
    def par_upw(i): return PAR[:, 8 + i:9 + i]
    def par_cl(i): return PAR[:, 16 + i:17 + i]
    def par_w512(i): return PAR[:, 24 + i:25 + i]
    def parb(m, ct): return PARB[:, m * 8 + ct: m * 8 + ct + 1]
    def parb1m(m, ct): return PARB[:, 24 + m * 8 + ct: 25 + m * 8 + ct]
    def parf(m, ct): return PARF[:, m * 8 + ct: m * 8 + ct + 1]
    def parf1m(m, ct): return PARF[:, 24 + m * 8 + ct: 25 + m * 8 + ct]

    for dst, src in [(cJ, din["cJ"]), (cIdF, din["cIdentF"]), (cIdB, din["cIdentB"]),
                     (cDg, din["cDiag"]), (cOn, din["cOnes"]), (cBt, din["cBias"]),
                     (cCV, din["cCVT"])]:
        SY.dma(lambda eng, d=dst, s=src: eng.dma_start(out=d, in_=s), sCN)

    DV.op(lambda eng: eng.memset(EPSB, EPS), inc=sDV)
    xin_t = xin.rearrange("(tt p) c -> tt p c", p=128)
    PE.wait(sCN)
    for tt in range(16):
        par = tt % 2
        sX = sXA if par == 0 else sXB
        if tt >= 2:
            SY.wait(sAC)
        SY.dma(lambda eng, tt=tt, par=par: eng.dma_start(
            out=XSTG[:, par * 1024:(par + 1) * 1024], in_=xin_t[tt]), sX)
        PE.wait(sX)
        PE.wait(sAC)
        for cb in range(8):
            bank = PS[cb // 4]
            PE.op(lambda eng, cb=cb, par=par, bank=bank: eng.transpose(
                bank[:, (cb % 4) * 128:(cb % 4 + 1) * 128],
                XSTG[:, par * 1024 + cb * 128: par * 1024 + (cb + 1) * 128],
                cIdF), inc=(sPE if cb == 7 else None))
        SC.wait(sPE)
        for hf in range(2):
            dst = Xv[:, hf * 4:(hf + 1) * 4, tt * 128:(tt + 1) * 128]
            SC.op(lambda eng, hf=hf, dst=dst: cpy(eng,
                dst, PS[hf].rearrange("p (a r) -> p a r", a=4)),
                inc=(sAC if hf == 1 else None))

    SY.wait(sAC)
    SY.dma(lambda eng: eng.dma_start(out=XSTG[0:40, 0:128], in_=din["a_par"]), sIN)
    PE.wait(sIN)
    PE.op(lambda eng: eng.transpose(PS[2][:, 0:128], XSTG[:, 0:128], cIdF), inc=sPE)
    SC.wait(sPE)
    SC.op(lambda eng: cpy(eng, PRAW, PS[2][:, 0:40]), inc=sAC)
    SC.op(lambda eng: eng.activation(
        out=PAR[:, 0:8], in_=PRAW[:, 0:8], func=AF.Exp))
    SC.op(lambda eng: eng.drain(fusable=False), inc=sAC)
    DV.wait(sAC)
    DV.opE(lambda eng: eng.tensor_tensor(
        out=PAR[:, 8:16], in0=PRAW[:, 8:16], in1=PAR[:, 0:8], op=ALU.subtract), sDV)
    DV.opE(lambda eng: eng.tensor_scalar(
        out=PAR[:, 16:24], in0=PAR[:, 8:16], scalar1=43.0, scalar2=None,
        op0=ALU.subtract), sDV)
    DV.opE(lambda eng: eng.tensor_scalar(
        out=PAR[:, 24:32], in0=PAR[:, 0:8], scalar1=512.0, scalar2=None,
        op0=ALU.mult), sDV)
    DV.opE(lambda eng: eng.tensor_copy(out=PARB[:, 0:24], in_=PRAW[:, 16:40]), sDV)
    DV.opE(lambda eng: eng.tensor_scalar(
        out=PARB[:, 24:48], in0=PRAW[:, 16:40],
        scalar1=-1.0, scalar2=1.0, op0=ALU.mult, op1=ALU.add), sDV)
    DV.op(lambda eng: eng.tensor_copy(out=PARF, in_=PARB), inc=sDV)

    def prep_weight(src_name, slot):
        GP.wait(sPE)
        GP.dma(lambda eng, s=src_name: eng.dma_start(
            out=WNAT.rearrange("p (r c) -> p r c", c=1024),
            in_=din[s].rearrange("(r p) c -> p r c", p=128)), sGPD)
        PE.wait(sGPD)
        slot_v = slot.rearrange("p (ct d) -> p ct d", d=1024)
        for r in range(8):
            PE.wait(sDV)
            PE.wait(sAC)
            for dblk in range(8):
                PE.op(lambda eng, r=r, dblk=dblk: eng.transpose(
                    PST[:, dblk * 128:(dblk + 1) * 128],
                    WNAT[:, r * 1024 + dblk * 128: r * 1024 + (dblk + 1) * 128],
                    cIdB), inc=(sPE if dblk == 7 else None))
            evq, evsem = (DV, sDV) if (r % 2 == 0) else (SC, sAC)
            evq.wait(sPE)
            dst = slot_v[:, :, r * 128:(r + 1) * 128]
            evq.op(lambda eng, dst=dst: cpy(
                eng, dst, PST.rearrange("p (a r2) -> p a r2", a=8)), inc=evsem)

    def make_svec(slot_w, lhs_fn, dst_aug, dst_row, scale=-1.0):
        for hf in range(2):
            for ct in range(CT):
                mm(PS[6][0:1, 0:512], lhs_fn(ct),
                   slot_w[:, ct * 1024 + hf * 512: ct * 1024 + hf * 512 + 512],
                   start=(ct == 0), stop=(ct == CT - 1),
                   inc=(sPE if ct == CT - 1 else None))
            SC.wait(sPE)
            SC.op(lambda eng, hf=hf, dst_aug=dst_aug, dst_row=dst_row, scale=scale:
                  eng.activation(
                      out=dst_aug[dst_row:dst_row + 1, hf * 512: hf * 512 + 512],
                      in_=PS[6][0:1, 0:512], func=AF.Copy, scale=scale), inc=sAC)
            PE.wait(sAC)

    prep_weight("a_Wk", W0)
    prep_weight("a_Wv", W1)
    prep_weight("a_Wr", W2)
    prep_weight("a_Wo", W3)
    barrier()
    make_svec(W0, lambda ct: parb(0, ct), AUGA, 0)
    make_svec(W0, lambda ct: parb1m(0, ct), AUGA, 32)
    make_svec(W1, lambda ct: parb(1, ct), AUGA, 64)
    make_svec(W1, lambda ct: parb1m(1, ct), AUGB, 0)
    make_svec(W2, lambda ct: parb(2, ct), AUGB, 32)
    make_svec(W2, lambda ct: parb1m(2, ct), AUGB, 64)

    def ln_stats():
        barrier()
        DV.opE(lambda eng: eng.memset(MRB[:, 0:1], 0.0), sDV)
        for c in range(NCH):
            t0 = c * CH
            GP.wait(sAC)
            GP.dma(lambda eng, t0=t0: eng.dma_start(
                out=XBF.rearrange("p (ct j) -> p ct j", j=CH),
                in_=Xv[:, :, t0:t0 + CH]), sGPD)
            SC.wait(sGPD)
            SC.wait(sPE)
            SC.op(lambda eng: eng.activation(out=XSQ, in_=XBF, func=AF.Square),
                  inc=sAC)
            PE.wait(sGPD)
            PE.wait(sDV)
            for ct in range(CT):
                mm(PS[0], cOn[:, 0:128], XBF[:, ct * CH:(ct + 1) * CH],
                   start=(ct == 0), stop=(ct == CT - 1))
            PE.wait(sAC)
            for ct in range(CT):
                mm(PS[1], cOn[:, 0:128], XSQ[:, ct * CH:(ct + 1) * CH],
                   start=(ct == 0), stop=(ct == CT - 1),
                   inc=(sPE if ct == CT - 1 else None))
            DV.wait(sPE)
            DV.op(lambda eng: eng.tensor_scalar(
                out=MST, in0=PS[0], scalar1=1.0 / C, scalar2=None, op0=ALU.mult))
            DV.op(lambda eng: eng.tensor_tensor(
                out=VST, in0=MST, in1=MST, op=ALU.mult))
            DV.op(lambda eng: eng.scalar_tensor_tensor(
                out=VST, in0=PS[1], scalar=1.0 / C, in1=VST,
                op0=ALU.mult, op1=ALU.subtract), inc=sDV)
            SC.wait(sDV)
            SC.op(lambda eng: eng.activation(
                out=VST2, in_=VST, func=AF.Sqrt, bias=EPSB), inc=sAC)
            DV.wait(sAC)
            DV.op(lambda eng, t0=t0: eng.reciprocal(
                out=RSTD[:, t0:t0 + CH], in_=VST2))
            DV.op(lambda eng, t0=t0: eng.tensor_tensor(
                out=MRB[:, 1 + t0:1 + t0 + CH], in0=MST,
                in1=RSTD[:, t0:t0 + CH], op=ALU.mult), inc=sDV)
        barrier()

    def make_xs(c, q_=None, sem=None):
        q_ = q_ or GP
        sem = sem or sGE
        t0 = c * CH
        if c == 0:
            q_.opE(lambda eng: eng.memset(XSv[:, :, 0:1], 0.0), sem)
        else:
            q_.opE(lambda eng: eng.tensor_copy(
                out=XSv[:, :, 0:1], in_=XSv[:, :, CH:CH + 1]), sem)
        for ct in range(CT):
            q_.op(lambda eng, ct=ct, t0=t0: eng.tensor_tensor(
                out=XS[:, ct * (CH + 1) + 1: ct * (CH + 1) + 1 + CH],
                in0=X[:, ct * T + t0: ct * T + t0 + CH],
                in1=RSTD[:, t0:t0 + CH], op=ALU.mult),
                inc=(sem if ct == CT - 1 else None))

    def make_mix(m, dstbuf):
        for ct in range(CT):
            GP.op(lambda eng, ct=ct, m=m: eng.tensor_scalar(
                out=TMPG1, in0=XS[:, ct * (CH + 1) + 1: ct * (CH + 1) + 1 + CH],
                scalar1=parf(m, ct), scalar2=None, op0=ALU.mult))
            GP.op(lambda eng, ct=ct, m=m: eng.tensor_scalar(
                out=TMPG2, in0=XS[:, ct * (CH + 1): ct * (CH + 1) + CH],
                scalar1=parf1m(m, ct), scalar2=None, op0=ALU.mult))
            GP.op(lambda eng, ct=ct, dstbuf=dstbuf: eng.tensor_tensor(
                out=dstbuf[:, ct * CH:(ct + 1) * CH], in0=TMPG1, in1=TMPG2,
                op=ALU.add), inc=(sGE if ct == CT - 1 else None))

    import os
    DBG = int(os.environ.get("KDBG", "9"))
    KAUG = int(os.environ.get("KAUG", "1"))

    def attn_a():
        ln_stats()
        if DBG < 2:
            barrier()
            return
        DV.opE(lambda eng: eng.memset(PPB, -1e38), sDV)
        DV.opE(lambda eng: eng.memset(AAC, 0.0), sDV)
        DV.opE(lambda eng: eng.memset(BBC, 0.0), sDV)
        for c in range(NCH):
            t0 = c * CH
            GP.wait(sPE)
            GP.wait(sDV)
            make_xs(c)
            make_mix(2, MIXA)
            mix_r = sGE.n
            if DBG < 3:
                continue
            sig_done = []
            for i in range(CT):
                pr = PS[i % 2]
                PE.wait(sGE, mix_r)
                if i >= 2:
                    PE.wait(sAC, sig_done[i - 2])
                proj_group(pr, W2, i,
                           lambda ct: MIXA[:, ct * CH:(ct + 1) * CH],
                           (AUGB[32:33, i * 128:(i + 1) * 128],
                            MRB[32:33, 1 + t0:1 + t0 + CH]),
                           (AUGB[64:65, i * 128:(i + 1) * 128],
                            MRB[64:65, t0:t0 + CH]), inc=sPE)
                SC.wait(sPE)
                SC.op(lambda eng, i=i, pr=pr: eng.activation(
                    out=YG[:, i * CH:(i + 1) * CH], in_=pr, func=AF.Sigmoid),
                    inc=sAC)
                sig_done.append(sAC.n)
            r_all = sPE.n
            if DBG < 4:
                continue
            GP.wait(sPE, r_all)
            make_mix(0, MIXA)
            make_mix(1, MIXB)
            mix_kv = sGE.n
            wkv_end = [0, 0]
            for i in range(CT):
                pk, pv = PS[2 + (i % 2) * 2], PS[3 + (i % 2) * 2]
                PE.wait(sGE, mix_kv)
                if i >= 2:
                    PE.wait(sDV, wkv_end[i % 2])
                proj_group(pk, W0, i,
                           lambda ct: MIXA[:, ct * CH:(ct + 1) * CH],
                           (AUGA[0:1, i * 128:(i + 1) * 128],
                            MRB[0:1, 1 + t0:1 + t0 + CH]),
                           (AUGA[32:33, i * 128:(i + 1) * 128],
                            MRB[32:33, t0:t0 + CH]))
                proj_group(pv, W1, i,
                           lambda ct: MIXB[:, ct * CH:(ct + 1) * CH],
                           (AUGA[64:65, i * 128:(i + 1) * 128],
                            MRB[64:65, 1 + t0:1 + t0 + CH]),
                           (AUGB[0:1, i * 128:(i + 1) * 128],
                            MRB[0:1, t0:t0 + CH]), inc=sPE)
                kv_done = sPE.n
                if DBG < 5:
                    continue

                DV.wait(sPE, kv_done)
                DV.op(lambda eng, i=i, pk=pk: eng.scalar_tensor_tensor(
                    out=ZP[0], in0=cJ, scalar=par_w(i), in1=pk,
                    op0=ALU.mult, op1=ALU.add))
                DV.opE(lambda eng, i=i: eng.tensor_copy(
                    out=CB[:, 0:1], in_=PPB[:, i:i + 1]), sDV)
                DV.opE(lambda eng, i=i: eng.tensor_tensor_scan(
                    out=CB[:, 1:CH + 1], data0=ZP[0], data1=ZP[0],
                    initial=PPB[:, i:i + 1], op0=ALU.max, op1=ALU.max), sDV)
                DV.op(lambda eng: eng.tensor_tensor(
                    out=ZP[1], in0=CB[:, 0:CH], in1=ZP[0], op=ALU.subtract))
                DV.op(lambda eng, i=i: eng.tensor_scalar(
                    out=ZP[4], in0=ZP[1], scalar1=par_cl(i), scalar2=None,
                    op0=ALU.max))
                DV.op(lambda eng: eng.tensor_scalar(
                    out=ZP[2], in0=ZP[4], scalar1=0.0, scalar2=None,
                    op0=ALU.min))
                DV.op(lambda eng: eng.tensor_scalar(
                    out=ZP[3], in0=ZP[1], scalar1=-1.0, scalar2=0.0,
                    op0=ALU.mult, op1=ALU.min), inc=sDV)
                SC.wait(sDV)
                SC.op(lambda eng: eng.activation(out=ZP[1], in_=ZP[2], func=AF.Exp))
                SC.op(lambda eng: eng.activation(out=ZP[2], in_=ZP[3], func=AF.Exp))
                SC.op(lambda eng, i=i: eng.activation(
                    out=ZP[3], in_=ZP[4], func=AF.Exp, bias=par_upw(i),
                    scale=-1.0), inc=sAC)
                DV.wait(sAC)
                DV.op(lambda eng, pv=pv: eng.tensor_tensor(
                    out=ZP[0], in0=ZP[2], in1=pv, op=ALU.mult))
                DV.opE(lambda eng, i=i: eng.tensor_copy(
                    out=ABF[:, 0:1], in_=AAC[:, i:i + 1]), sDV)
                DV.opE(lambda eng, i=i: eng.tensor_tensor_scan(
                    out=ABF[:, 1:CH + 1], data0=ZP[1], data1=ZP[0],
                    initial=AAC[:, i:i + 1], op0=ALU.mult, op1=ALU.add), sDV)
                DV.opE(lambda eng, i=i: eng.tensor_copy(
                    out=BBF[:, 0:1], in_=BBC[:, i:i + 1]), sDV)
                DV.opE(lambda eng, i=i: eng.tensor_tensor_scan(
                    out=BBF[:, 1:CH + 1], data0=ZP[1], data1=ZP[2],
                    initial=BBC[:, i:i + 1], op0=ALU.mult, op1=ALU.add), sDV)
                DV.op(lambda eng, pv=pv: eng.tensor_tensor(
                    out=ZP[4], in0=ZP[3], in1=pv, op=ALU.mult))
                DV.op(lambda eng: eng.tensor_tensor(
                    out=ZP[0], in0=ABF[:, 0:CH], in1=ZP[4], op=ALU.add))
                DV.op(lambda eng: eng.tensor_tensor(
                    out=ZP[2], in0=BBF[:, 0:CH], in1=ZP[3], op=ALU.add),
                    inc=sDV)
                wkv_end[i % 2] = sDV.n
                DV.opE(lambda eng, i=i: eng.tensor_scalar(
                    out=PPB[:, i:i + 1], in0=CB[:, CH:CH + 1],
                    scalar1=par_w512(i), scalar2=None, op0=ALU.subtract), sDV)
                DV.opE(lambda eng, i=i: eng.tensor_copy(
                    out=AAC[:, i:i + 1], in_=ABF[:, CH:CH + 1]), sDV)
                DV.opE(lambda eng, i=i: eng.tensor_copy(
                    out=BBC[:, i:i + 1], in_=BBF[:, CH:CH + 1]), sDV)
                SC.wait(sDV)
                SC.op(lambda eng: eng.activation(out=ZP[1], in_=ZP[2], func=AF.Ln))
                SC.op(lambda eng: eng.activation(
                    out=ZP[4], in_=ZP[1], func=AF.Exp, scale=-1.0), inc=sAC)
                DV.wait(sAC)
                DV.op(lambda eng: eng.tensor_tensor(
                    out=TMPA, in0=ZP[0], in1=ZP[4], op=ALU.mult))
                DV.op(lambda eng, i=i: eng.tensor_tensor(
                    out=YG[:, i * CH:(i + 1) * CH], in0=TMPA,
                    in1=YG[:, i * CH:(i + 1) * CH], op=ALU.mult), inc=sDV)
            wkv_all = sDV.n
            if DBG < 6:
                continue
            PE.wait(sDV, wkv_all)
            for dp in range(CT):
                for ct in range(CT):
                    mm(PS[6], wsl(W3, ct, dp), YG[:, ct * CH:(ct + 1) * CH],
                       start=(ct == 0), stop=(ct == CT - 1),
                       inc=(sPE if ct == CT - 1 else None))
                DV.wait(sPE)
                DV.op(lambda eng, dp=dp, t0=t0: eng.tensor_tensor(
                    out=X[:, dp * T + t0: dp * T + t0 + CH],
                    in0=X[:, dp * T + t0: dp * T + t0 + CH], in1=PS[6],
                    op=ALU.add), inc=sDV)
                PE.wait(sDV)
        barrier()

    if n_sublayers >= 1:
        attn_a()

    def ffn_build():
        barrier()
        for dst, s in ((FP2, "cFP2"), (FP2B, "cFP2B"), (FSD, "cFSD"),
                       (FSG, "cFSG")):
            SY.dma(lambda eng, d=dst, s=s: eng.dma_start(
                out=d[0:din[s].shape[0], :], in_=din[s]), sIN)
        for dst, s in ((FG1R, "f_G1R"), (FG2R, "f_G2R"), (FG1I, "f_G1I"),
                       (FG2I, "f_G2I")):
            GP.dma(lambda eng, d=dst, s=s: eng.dma_start(
                out=d[0:din[s].shape[0], :], in_=din[s]), sGPD)
        GP.dma(lambda eng: eng.dma_start(
            out=AUGC[32:33, 0:1024], in_=din["f_bias"]), sGPD)
        PE.wait(sGPD)
        PE.wait(sIN)
        for part in range(2):
            G1 = FG1R if part == 0 else FG1I
            G2 = FG2R if part == 0 else FG2I
            for c4 in range(4):
                mm(PS[0][0:32, :], G1[0:32, c4 * 32:(c4 + 1) * 32],
                   FP2[0:32, :], start=True, stop=True, inc=sPE)
                SC.wait(sPE)
                SC.op(lambda eng, G1=G1: eng.activation(
                    out=FO1[0:32, :], in_=PS[0][0:32, :], func=AF.Copy))
                mm(PS[1][0:16, :], G2[0:16, c4 * 16:(c4 + 1) * 16],
                   FP2B[0:16, :], start=True, stop=True, inc=sPE)
                SC.wait(sPE)
                SC.op(lambda eng: eng.activation(
                    out=FO2[0:16, :], in_=PS[1][0:16, :], func=AF.Copy),
                    inc=sAC)
                PE.wait(sAC)
                for pt in range(4):
                    PE.wait(sDV)
                    mm(PS[2], FSD[0:32, pt * 128:(pt + 1) * 128], FO1[0:32, :],
                       start=True, stop=True)
                    mm(PS[3], FSG[0:16, pt * 128:(pt + 1) * 128], FO2[0:16, :],
                       start=True, stop=True, inc=sPE)
                    DV.wait(sPE)
                    DV.op(lambda eng: eng.tensor_copy(out=FEA, in_=PS[2]))
                    DV.op(lambda eng: eng.tensor_tensor(
                        out=FEA, in0=PS[3], in1=FEA, op=ALU.mult))
                    w = FWT[part * 4 + pt]
                    if c4 == 0:
                        DV.op(lambda eng, w=w: eng.tensor_copy(out=w, in_=FEA),
                              inc=sDV)
                    else:
                        DV.op(lambda eng, w=w: eng.tensor_tensor(
                            out=w, in0=w, in1=FEA, op=ALU.add), inc=sDV)
        SC.wait(sDV)
        for ct2 in range(4):
            SC.op(lambda eng, ct2=ct2: eng.activation(
                out=W0[:, ct2 * 1024: ct2 * 1024 + 512], in_=FWT[ct2],
                func=AF.Copy))
            SC.op(lambda eng, ct2=ct2: eng.activation(
                out=W0[:, ct2 * 1024 + 512: ct2 * 1024 + 1024], in_=FWT[4 + ct2],
                func=AF.Copy))
            SC.op(lambda eng, ct2=ct2: eng.activation(
                out=W0[:, (4 + ct2) * 1024: (4 + ct2) * 1024 + 512],
                in_=FWT[4 + ct2], func=AF.Copy, scale=-1.0))
            SC.op(lambda eng, ct2=ct2: eng.activation(
                out=W0[:, (4 + ct2) * 1024 + 512: (4 + ct2) * 1024 + 1024],
                in_=FWT[ct2], func=AF.Copy), inc=sAC)
        barrier()
        make_svec(W0, lambda ct: cOn[:, 0:1], AUGC, 0)
        barrier()
        SY.dma(lambda eng: eng.dma_start(out=wstash, in_=W0), sIN)

    def ffn(reload=False):
        if reload:
            barrier()
            SY.wait(sIN)
            SY.dma(lambda eng: eng.dma_start(out=W0, in_=wstash), sIN)
        ln_stats()
        if reload:
            PE.wait(sIN)
        res_done = [0, 0]
        for c in range(NCH):
            t0 = c * CH
            GP.wait(sPE)
            GP.wait(sDV)
            make_xs(c)
            xs_v = sGE.n
            PE.wait(sGE, xs_v)
            for dp in range(CT):
                psf = PS[dp % 2]
                if c * CT + dp >= 2:
                    PE.wait(sDV, res_done[dp % 2])
                proj_group(psf, W0, dp,
                           lambda ct: XS[:, ct * (CH + 1) + 1:
                                         ct * (CH + 1) + 1 + CH],
                           (AUGC[0:1, dp * 128:(dp + 1) * 128],
                            MRB[0:1, 1 + t0:1 + t0 + CH]),
                           (AUGC[32:33, dp * 128:(dp + 1) * 128],
                            cOn[32:33, 0:CH]), inc=sPE)
                DV.wait(sPE)
                DV.op(lambda eng, dp=dp, t0=t0, psf=psf: eng.tensor_tensor(
                    out=X[:, dp * T + t0: dp * T + t0 + CH],
                    in0=X[:, dp * T + t0: dp * T + t0 + CH], in1=psf,
                    op=ALU.add), inc=sDV)
                res_done[dp % 2] = sDV.n
        barrier()

    if n_sublayers >= 2:
        ffn_build()
        ffn()

    import os as _os_chk
    def load_win(src_name):
        win_v = WINB.rearrange("p (ct d) -> p ct d", d=1024)
        src_t = din[src_name].rearrange("(r p) cc -> p r cc", p=128)
        for half in range(2):
            GP.wait(sPE)
            GP.dma(lambda eng, half=half, src_t=src_t: eng.dma_start(
                out=WNAT2.rearrange("p (r cc) -> p r cc", cc=1024),
                in_=src_t[:, half * 4:(half + 1) * 4, :]), sGPD)
            PE.wait(sGPD)
            for r4 in range(4):
                r = half * 4 + r4
                PE.wait(sDV)
                PE.wait(sAC)
                for dblk in range(8):
                    PE.op(lambda eng, r4=r4, dblk=dblk: eng.transpose(
                        PST[:, dblk * 128:(dblk + 1) * 128],
                        WNAT2[:, r4 * 1024 + dblk * 128: r4 * 1024 + (dblk + 1) * 128],
                        cIdB), inc=(sPE if dblk == 7 else None))
                evq, evsem = (DV, sDV) if (r % 2 == 0) else (SC, sAC)
                evq.wait(sPE)
                dst = win_v[:, :, r * 128:(r + 1) * 128]
                evq.op(lambda eng, dst=dst: cpy(
                    eng, dst, PST.rearrange("p (a r2) -> p a r2", a=8)), inc=evsem)
        barrier()

    def kTs(dt):
        slot = W0 if dt < 4 else W1
        return slot[:, (dt % 4) * 2048:(dt % 4) * 2048 + 2048]

    def attn_b():
        ln_stats()
        GP.dma(lambda eng: eng.dma_start(
            out=AUGB[0:1, 0:1024], in_=din["b_out_b"]), sGPD)
        DV.wait(sGPD)
        for c in range(NCH):
            t0 = c * CH
            for ct in range(CT):
                DV.op(lambda eng, ct=ct, t0=t0: eng.tensor_tensor(
                    out=QTXS[:, ct * T + t0: ct * T + t0 + CH],
                    in0=X[:, ct * T + t0: ct * T + t0 + CH],
                    in1=RSTD[:, t0:t0 + CH], op=ALU.mult),
                    inc=(sDV if ct == CT - 1 else None))
        for j in range(16):
            oc = VN[j].rearrange("p (h d) -> p h d", d=65)[:, :, 64:65]
            DV.op(lambda eng, oc=oc: eng.tensor_copy(
                out=oc, in_=cOn[:, 0:16].unsqueeze(2)),
                inc=(sDV if j == 15 else None))
        barrier()
        load_win("b_wv")
        make_svec(WINB, lambda ct: cOn[:, 0:1], AUGA, 0)
        barrier()
        ev_done = [0, 0]
        for tt in range(16):
            for hf in range(2):
                idx = tt * 2 + hf
                ps_ = PS[idx % 2]
                if idx >= 2:
                    PE.wait(sAC, ev_done[idx % 2])
                for ct in range(CT):
                    mm(ps_, QTXS[:, ct * T + tt * 128: ct * T + (tt + 1) * 128],
                       WINB[:, ct * 1024 + hf * 512: ct * 1024 + hf * 512 + 512],
                       start=(ct == 0), stop=False)
                mm(ps_, MRB[0:1, 1 + tt * 128: 1 + (tt + 1) * 128],
                   AUGA[0:1, hf * 512: hf * 512 + 512],
                   start=False, stop=True, inc=sPE)
                SC.wait(sPE)
                dst = VN[tt].rearrange("p (h d) -> p h d",
                                       d=65)[:, hf * 8:(hf + 1) * 8, 0:64]
                SC.op(lambda eng, dst=dst, ps_=ps_: eng.activation(
                    out=dst, in_=ps_.rearrange("p (a b) -> p a b", a=8),
                    func=AF.Copy), inc=sAC)
                ev_done[idx % 2] = sAC.n
        barrier()
        load_win("b_wk")
        make_svec(WINB, lambda ct: cOn[:, 0:1], AUGA, 32)
        barrier()
        kv_ev = [0, 0, 0, 0]
        for c in range(NCH):
            t0 = c * CH
            for dt in range(CT):
                ps_ = PS[dt % 4]
                if kv_ev[dt % 4]:
                    PE.wait(sAC, kv_ev[dt % 4])
                for ct in range(CT):
                    mm(ps_, WINB[:, ct * 1024 + dt * 128: ct * 1024 + (dt + 1) * 128],
                       QTXS[:, ct * T + t0: ct * T + t0 + CH],
                       start=(ct == 0), stop=False)
                mm(ps_, AUGA[32:33, dt * 128:(dt + 1) * 128],
                   MRB[32:33, 1 + t0:1 + t0 + CH], start=False, stop=True, inc=sPE)
                SC.wait(sPE)
                SC.op(lambda eng, dt=dt, t0=t0, ps_=ps_: eng.activation(
                    out=kTs(dt)[:, t0:t0 + CH], in_=ps_, func=AF.Copy), inc=sAC)
                kv_ev[dt % 4] = sAC.n
            DV.wait(sAC)
            for hc in range(2):
                th = t0 + hc * 256
                GP.wait(sDV)
                GP.dma(lambda eng, th=th: eng.dma_start(
                    out=ROPE.rearrange("p (dt j) -> p dt j", j=256),
                    in_=din["cCos"].rearrange("p (dt t) -> p dt t", t=T)[:, :, th:th + 256]), sGPD)
                GP.dma(lambda eng, th=th: eng.dma_start(
                    out=RTS.rearrange("p (dt j) -> p dt j", j=256),
                    in_=din["cSin"].rearrange("p (dt t) -> p dt t", t=T)[:, :, th:th + 256]), sGPD)
                DV.wait(sGPD)
                for dt in range(4):
                    top = kTs(dt)[:, th:th + 256]
                    bot = kTs(dt + 4)[:, th:th + 256]
                    cosb = ROPE[:, dt * 256:(dt + 1) * 256]
                    sinb = RTS[:, dt * 256:(dt + 1) * 256]
                    DV.op(lambda eng, top=top, cosb=cosb: eng.tensor_tensor(
                        out=TMPC, in0=top, in1=cosb, op=ALU.mult))
                    DV.op(lambda eng, top=top, sinb=sinb: eng.tensor_tensor(
                        out=TMPD, in0=top, in1=sinb, op=ALU.mult))
                    DV.op(lambda eng, top=top, bot=bot, sinb=sinb: eng.tensor_tensor(
                        out=top, in0=bot, in1=sinb, op=ALU.mult))
                    DV.op(lambda eng, top=top: eng.tensor_tensor(
                        out=top, in0=TMPC, in1=top, op=ALU.subtract))
                    DV.op(lambda eng, bot=bot, cosb=cosb: eng.tensor_tensor(
                        out=TMPC, in0=bot, in1=cosb, op=ALU.mult))
                    DV.op(lambda eng, bot=bot: eng.tensor_tensor(
                        out=bot, in0=TMPD, in1=TMPC, op=ALU.add),
                        inc=(sDV if dt == 3 else None))
        barrier()
        load_win("b_wq")
        make_svec(WINB, lambda ct: cOn[:, 0:1], AUGA, 64)
        barrier()
        for c in range(NCH):
            t0 = c * CH
            for hc in range(2):
                th = t0 + hc * 256
                GP.wait(sDV)
                GP.dma(lambda eng, th=th: eng.dma_start(
                    out=ROPE.rearrange("p (dt j) -> p dt j", j=256),
                    in_=din["cCos"].rearrange("p (dt t) -> p dt t", t=T)[:, :, th:th + 256]), sGPD)
                GP.dma(lambda eng, th=th: eng.dma_start(
                    out=RTS.rearrange("p (dt j) -> p dt j", j=256),
                    in_=din["cSin"].rearrange("p (dt t) -> p dt t", t=T)[:, :, th:th + 256]), sGPD)
                qev = [0, 0]
                for dt in range(CT):
                    ps_ = PS[dt % 2][:, 0:256]
                    PE.wait(sDV)
                    if qev[dt % 2]:
                        PE.wait(sAC, qev[dt % 2])
                    for ct in range(CT):
                        mm(ps_, WINB[:, ct * 1024 + dt * 128: ct * 1024 + (dt + 1) * 128],
                           QTXS[:, ct * T + th: ct * T + th + 256],
                           start=(ct == 0), stop=False)
                    mm(ps_, AUGA[64:65, dt * 128:(dt + 1) * 128],
                       MRB[64:65, 1 + th:1 + th + 256], start=False, stop=True,
                       inc=sPE)
                    SC.wait(sPE)
                    SC.op(lambda eng, dt=dt, ps_=ps_: eng.activation(
                        out=QP8[:, dt * 256:(dt + 1) * 256], in_=ps_,
                        func=AF.Copy), inc=sAC)
                    qev[dt % 2] = sAC.n
                DV.wait(sAC)
                DV.wait(sGPD)
                for dt in range(4):
                    topo = QTXS[:, dt * T + th: dt * T + th + 256]
                    boto = QTXS[:, (dt + 4) * T + th:(dt + 4) * T + th + 256]
                    qpa = QP8[:, dt * 256:(dt + 1) * 256]
                    qpb = QP8[:, (dt + 4) * 256:(dt + 5) * 256]
                    cosb = ROPE[:, dt * 256:(dt + 1) * 256]
                    sinb = RTS[:, dt * 256:(dt + 1) * 256]
                    DV.op(lambda eng, qpa=qpa, cosb=cosb: eng.tensor_tensor(
                        out=TMPC, in0=qpa, in1=cosb, op=ALU.mult))
                    DV.op(lambda eng, qpb=qpb, sinb=sinb: eng.tensor_tensor(
                        out=TMPD, in0=qpb, in1=sinb, op=ALU.mult))
                    DV.op(lambda eng, topo=topo: eng.tensor_tensor(
                        out=topo, in0=TMPC, in1=TMPD, op=ALU.subtract))
                    DV.op(lambda eng, qpa=qpa, sinb=sinb: eng.tensor_tensor(
                        out=TMPC, in0=qpa, in1=sinb, op=ALU.mult))
                    DV.op(lambda eng, qpb=qpb, cosb=cosb: eng.tensor_tensor(
                        out=TMPD, in0=qpb, in1=cosb, op=ALU.mult))
                    DV.op(lambda eng, boto=boto: eng.tensor_tensor(
                        out=boto, in0=TMPC, in1=TMPD, op=ALU.add),
                        inc=(sDV if dt == 3 else None))
        barrier()
        if os.environ.get("KDUMP") == "qt":
            SY.dma(lambda eng: eng.dma_start(out=dbg, in_=QTXS), sIN)
            barrier()
        if os.environ.get("KDUMP") == "kt":
            SY.dma(lambda eng: eng.dma_start(out=dbg[:, 0:8192], in_=W0), sIN)
            SY.dma(lambda eng: eng.dma_start(out=dbg[:, 8192:16384], in_=W1), sIN)
            barrier()
        if os.environ.get("KDUMP") == "vn":
            for j in range(16):
                SY.dma(lambda eng, j=j: eng.dma_start(
                    out=dbg[:, j * 1024:(j + 1) * 1024], in_=VN[j][:, 0:1024]), sIN)
            barrier()
        load_win("b_out_w")
        barrier()
        for g in range(NCH):
            t1c = g * 512
            tr_pe = [0]
            ts_dv = [0]
            pending_norm = [None]
            for h in range(H):
                dtq = h // 2
                pbase = (h % 2) * 64
                qsl = QTXS[pbase:pbase + 64, dtq * T + t1c: dtq * T + t1c + 512]
                gran = ALIBI_GRAN[h]
                dmin = -(16 - gran // 128)
                nj = 4 * g + 4
                exp_ac = {}
                mask_dv = {}
                pv_arr = {}

                def emit_qk(j):
                    psc = PS[j % 2]
                    if j >= 2:
                        PE.wait(sAC, exp_ac[j - 2])
                    else:
                        PE.wait(sAC)
                    ksl = kTs(dtq)[pbase:pbase + 64, j * 128:(j + 1) * 128]
                    mm(psc, ksl, qsl, start=True, stop=True, inc=sPE)
                    return sPE.n

                def emit_exp(j, qk_v):
                    psc = PS[j % 2]
                    PT = PTD[j % 2]
                    SC.wait(sPE, qk_v)
                    if j >= 2:
                        SC.wait(sPE, pv_arr[j - 2])
                    if j - 1 in mask_dv:
                        SC.wait(sDV, mask_dv[j - 1])
                    nsub = 512 // gran
                    for s2 in range(nsub):
                        c0 = s2 * gran
                        c1 = c0 + gran
                        dlo = j * 128 - t1c
                        if c1 <= dlo:
                            SC.op(lambda eng, c0=c0, c1=c1, PT=PT: eng.memzero(
                                PT[:, c0:c1]))
                            continue
                        blk0 = (t1c + s2 * gran) // gran
                        didx = j - (gran // 128) * blk0 - dmin
                        bias = cBt[:, h * 16 + didx: h * 16 + didx + 1]
                        if c0 <= dlo < c1:
                            if dlo > c0:
                                SC.op(lambda eng, c0=c0, dlo=dlo, PT=PT:
                                      eng.memzero(PT[:, c0:dlo]))
                            SC.op(lambda eng, psc=psc, dlo=dlo, bias=bias:
                                  eng.activation(
                                      out=PSC, in_=psc[:, dlo:dlo + 128],
                                      func=AF.Exp, scale=0.125, bias=bias),
                                  inc=sAC)
                            DV.wait(sAC)
                            DV.op(lambda eng, dlo=dlo, PT=PT: eng.tensor_tensor(
                                out=PT[:, dlo:dlo + 128], in0=PSC, in1=cDg,
                                op=ALU.mult), inc=sDV)
                            if dlo + 128 < c1:
                                SC.op(lambda eng, psc=psc, dlo=dlo, c1=c1,
                                      bias=bias, PT=PT: eng.activation(
                                    out=PT[:, dlo + 128:c1],
                                    in_=psc[:, dlo + 128:c1],
                                    func=AF.Exp, scale=0.125, bias=bias))
                        elif c0 > dlo:
                            SC.op(lambda eng, psc=psc, c0=c0, c1=c1, bias=bias,
                                  PT=PT: eng.activation(
                                      out=PT[:, c0:c1], in_=psc[:, c0:c1],
                                      func=AF.Exp, scale=0.125, bias=bias))
                    exp_ac[j] = sAC.n
                    mask_dv[j] = sDV.n

                def emit_pv(j):
                    PT = PTD[j % 2]
                    if j == 0 and ts_dv[0]:
                        PE.wait(sDV, ts_dv[0])
                    PE.wait(sAC, exp_ac[j])
                    PE.wait(sDV, mask_dv[j])
                    for i2 in range(4):
                        mm(PS[6][:, i2 * 65:(i2 + 1) * 65],
                           PT[:, i2 * 128:(i2 + 1) * 128],
                           VN[j][:, 65 * h:65 * h + 65],
                           start=(j == 0 and i2 == 0),
                           stop=(j == nj - 1 and i2 == 3),
                           inc=(sPE if i2 == 3 else None))
                    pv_arr[j] = sPE.n

                for j in range(nj):
                    qkv = emit_qk(j)
                    emit_exp(j, qkv)
                    if j == 1 and pending_norm[0] is not None:
                        pending_norm[0]()
                        pending_norm[0] = None
                    if j >= 1:
                        emit_pv(j - 1)
                if pending_norm[0] is not None:
                    pending_norm[0]()
                    pending_norm[0] = None
                emit_pv(nj - 1)
                pv_pe = pv_arr[nj - 1]
                def make_norm(h=h, dtq=dtq, pbase=pbase,
                              pv_arr=pv_arr, nj=nj):
                    pv_pe = pv_arr[nj - 1]
                    DV.wait(sPE, pv_pe)
                    if ts_dv[0]:
                        DV.wait(sDV, ts_dv[0])
                    zcols = PS[6][:, 0:260].rearrange(
                        "p (i c) -> p i c", c=65)[:, :, 64:65]
                    DV.op(lambda eng, zcols=zcols, h=h: eng.tensor_tensor(
                        out=RCB[:, 0:4].unsqueeze(2), in0=zcols,
                        in1=cCV[:, h * 4:h * 4 + 4].unsqueeze(2), op=ALU.mult))
                    DV.op(lambda eng: eng.drain(fusable=False), inc=sDV)
                    SC.wait(sDV)
                    SC.op(lambda eng: eng.activation(
                        out=RCB[:, 4:8], in_=RCB[:, 0:4], func=AF.Ln))
                    SC.op(lambda eng: eng.activation(
                        out=RCB[:, 8:12], in_=RCB[:, 4:8], func=AF.Exp, scale=-1.0))
                    SC.op(lambda eng: eng.drain(fusable=False), inc=sAC)
                    DV.wait(sAC)
                    DV.op(lambda eng, h=h: eng.tensor_tensor(
                        out=RCB[:, 8:12], in0=RCB[:, 8:12],
                        in1=cCV[:, h * 4:h * 4 + 4], op=ALU.mult), inc=sDV)
                    DV.wait(sDV)
                    for i2 in range(4):
                        DV.op(lambda eng, i2=i2: eng.tensor_scalar(
                            out=STG64[:, i2 * 64:(i2 + 1) * 64],
                            in0=PS[6][:, i2 * 65: i2 * 65 + 64],
                            scalar1=RCB[:, 8 + i2:9 + i2], scalar2=None,
                            op0=ALU.mult), inc=(sDV if i2 == 3 else None))
                    if (os.environ.get("KDUMP") == "stg" and g == 0
                            and h == int(os.environ.get("KH", "2"))):
                        barrier()
                        SY.dma(lambda eng: eng.dma_start(
                            out=dbg[:, 0:256], in_=STG64), sIN)
                        SY.dma(lambda eng: eng.dma_start(
                            out=dbgf[:, 0:8], in_=RCB), sIN)
                        barrier()
                    PE.wait(sDV)
                    for i2 in range(4):
                        PE.op(lambda eng, i2=i2: eng.transpose(
                            PST[0:64, i2 * 128:(i2 + 1) * 128],
                            STG64[:, i2 * 64:(i2 + 1) * 64], cIdB),
                            inc=(sPE if i2 == 3 else None))
                    tr_pe[0] = sPE.n
                    SC.wait(sPE)
                    SC.op(lambda eng, pbase=pbase, dtq=dtq: eng.activation(
                        out=OTB[pbase:pbase + 64, dtq * 512:(dtq + 1) * 512],
                        in_=PST[0:64, 0:512], func=AF.Copy), inc=sAC)
                    PE.wait(sAC)
                pending_norm[0] = make_norm

            if pending_norm[0] is not None:
                pending_norm[0]()
                pending_norm[0] = None
            if g == 0 and os.environ.get("KDUMP") == "otb":
                barrier()
                SY.dma(lambda eng: eng.dma_start(out=dbg[:, 0:4096], in_=OTB), sIN)
                barrier()
            op_res = [0, 0]
            for dp in range(CT):
                pso = PS[2 + dp % 2]
                if dp >= 2:
                    PE.wait(sDV, op_res[dp % 2])
                for ct in range(CT):
                    mm(pso, WINB[:, ct * 1024 + dp * 128: ct * 1024 + (dp + 1) * 128],
                       OTB[:, ct * 512:(ct + 1) * 512],
                       start=(ct == 0), stop=False)
                mm(pso, AUGB[0:1, dp * 128:(dp + 1) * 128], cOn[0:1, 0:512],
                   start=False, stop=True, inc=sPE)
                DV.wait(sPE)
                DV.op(lambda eng, dp=dp, t1c=t1c, pso=pso: eng.tensor_tensor(
                    out=X[:, dp * T + t1c: dp * T + t1c + 512],
                    in0=X[:, dp * T + t1c: dp * T + t1c + 512], in1=pso,
                    op=ALU.add), inc=sDV)
                op_res[dp % 2] = sDV.n
        barrier()

    if n_sublayers >= 3:
        attn_b()
    if n_sublayers >= 4:
        ffn(reload=True)

    barrier()
    for tt in range(16):
        par = tt % 2
        sS = sSA if par == 0 else sSB
        if tt >= 2:
            PE.wait(sS, 16 * (tt // 2))
        PE.wait(sAC)
        for hf in range(2):
            bank = PS[hf]
            for cq in range(4):
                cb = hf * 4 + cq
                PE.op(lambda eng, cb=cb, cq=cq, tt=tt, bank=bank: eng.transpose(
                    bank[:, cq * 128:(cq + 1) * 128],
                    X[:, cb * T + tt * 128: cb * T + (tt + 1) * 128],
                    cIdF), inc=(sPE if cq == 3 else None))
            SC.wait(sPE)
            SC.op(lambda eng, hf=hf, par=par, bank=bank: cpy(eng,
                OST[:, par * 1024 + hf * 512: par * 1024 + hf * 512 + 512],
                bank), inc=sAC)
        SY.wait(sAC)
        SY.dma(lambda eng, tt=tt, par=par: eng.dma_start(
            out=out[tt * 128:(tt + 1) * 128, :],
            in_=OST[:, par * 1024:(par + 1) * 1024]), sS)
    SY.wait(sSA)
    SY.wait(sSB)

    def body(qname):
        def run(eng):
            for item in q[qname].items:
                if item[0] == "wait":
                    eng.wait_ge(item[1].h, item[2])
                else:
                    inst = item[1](eng)
                    for s, n in item[2]:
                        inst.then_inc(s.h, n)
        return run
    blk.sync(body("sync"))
    blk.gpsimd(body("gpsimd"))
    blk.vector(body("vector"))
    blk.scalar(body("scalar"))
    blk.tensor(body("tensor"))
    blk_ctx.__exit__(None, None, None)
    for ctx in sem_ctx:
        ctx.__exit__(None, None, None)
    return nc


_consts_cache = None

def kernel(L0_attn_a_x, L0_attn_a_time_decay, L0_attn_a_time_first,
           L0_attn_a_time_mix_k, L0_attn_a_time_mix_v, L0_attn_a_time_mix_r,
           L0_attn_a_Wk, L0_attn_a_Wv, L0_attn_a_Wr, L0_attn_a_Wo,
           L0_attn_b_wq, L0_attn_b_wk, L0_attn_b_wv, L0_attn_b_out_w,
           L0_attn_b_out_b, L0_ffn_G1_R, L0_ffn_G2_R, L0_ffn_G1_I, L0_ffn_G2_I,
           L0_ffn_bias_real, L0_ffn_bias_imag, _sublayers=4):
    global _consts_cache
    if _consts_cache is None:
        _consts_cache = host_constants()
    f32 = np.float32
    x = np.ascontiguousarray(np.asarray(L0_attn_a_x, f32))
    par = np.ascontiguousarray(np.stack([
        np.asarray(L0_attn_a_time_decay, f32).reshape(8, 128),
        np.asarray(L0_attn_a_time_first, f32).reshape(8, 128),
        np.asarray(L0_attn_a_time_mix_k, f32).reshape(8, 128),
        np.asarray(L0_attn_a_time_mix_v, f32).reshape(8, 128),
        np.asarray(L0_attn_a_time_mix_r, f32).reshape(8, 128)],
        axis=0).reshape(40, 128))
    g2r = np.ascontiguousarray(np.asarray(L0_ffn_G2_R, f32).reshape(4, 16, 16)
                               .transpose(1, 0, 2).reshape(16, 64))
    g2i = np.ascontiguousarray(np.asarray(L0_ffn_G2_I, f32).reshape(4, 16, 16)
                               .transpose(1, 0, 2).reshape(16, 64))
    base = {
        "a_Wk": np.ascontiguousarray(np.asarray(L0_attn_a_Wk, f32)),
        "a_Wv": np.ascontiguousarray(np.asarray(L0_attn_a_Wv, f32)),
        "a_Wr": np.ascontiguousarray(np.asarray(L0_attn_a_Wr, f32)),
        "a_Wo": np.ascontiguousarray(np.asarray(L0_attn_a_Wo, f32)),
        "b_wq": np.ascontiguousarray(np.asarray(L0_attn_b_wq, f32)),
        "b_wk": np.ascontiguousarray(np.asarray(L0_attn_b_wk, f32)),
        "b_wv": np.ascontiguousarray(np.asarray(L0_attn_b_wv, f32)),
        "b_out_w": np.ascontiguousarray(np.asarray(L0_attn_b_out_w, f32)),
        "a_par": par,
        "b_out_b": np.ascontiguousarray(np.asarray(L0_attn_b_out_b, f32))[None, :],
        "f_G1R": np.ascontiguousarray(np.asarray(L0_ffn_G1_R, f32).reshape(32, 128)),
        "f_G2R": g2r,
        "f_G1I": np.ascontiguousarray(np.asarray(L0_ffn_G1_I, f32).reshape(32, 128)),
        "f_G2I": g2i,
        "f_bias": np.concatenate([np.asarray(L0_ffn_bias_real, f32),
                                  np.asarray(L0_ffn_bias_imag, f32)])[None, :],
    }
    base.update(_consts_cache)
    nc = build(n_sublayers=_sublayers)
    in_maps = []
    for b in range(8):
        m = dict(base)
        m["xin"] = np.ascontiguousarray(x[b])
        in_maps.append(m)
    res = run_bass_kernel_spmd(nc, in_maps, core_ids=list(range(8)))
    return np.stack([np.asarray(res.results[b]["out"]) for b in range(8)], axis=0)


# version 07
# speedup vs baseline: 1.6913x; 1.6913x over previous
import numpy as np
import ml_dtypes
import concourse.bass as bass
import concourse.mybir as mybir
import bass_rust
from concourse.bass_utils import run_bass_kernel_spmd

F32 = mybir.dt.float32
BF16 = mybir.dt.bfloat16
AF = mybir.ActivationFunctionType
ALU = mybir.AluOpType
BF_NP = ml_dtypes.bfloat16

T = 2048
C = 1024
CT = 8
CH = 512
NCH = T // CH
H = 16
SHIFT = 64.0
EPS = 1e-5
ALIBI_GRAN = [128 if h < 2 else (256 if h < 4 else 512) for h in range(H)]


def host_constants():
    c = {}
    c["cJ"] = np.repeat((np.arange(CH, dtype=np.float32) + 1.0)[None, :], 128, 0)
    c["cIdentF"] = np.eye(128, dtype=np.float32)
    c["cIdentB"] = np.eye(128, dtype=BF_NP)
    c["cDiag"] = (np.arange(128)[None, :] >= np.arange(128)[:, None]).astype(BF_NP)
    c["cOnes"] = np.ones((128, 512), dtype=BF_NP)
    slopes = np.array([2.0 ** (-8.0 * (i + 1) / H) for i in range(H)], np.float64)
    bt = np.zeros((128, 256), np.float32)
    p = np.arange(128, dtype=np.float64)
    for h in range(H):
        gran = ALIBI_GRAN[h]
        dmin = -(16 - gran // 128)
        for idx in range(16):
            bt[:, h * 16 + idx] = (slopes[h] * p + slopes[h] * 128.0 * (dmin + idx)
                                   - SHIFT).astype(np.float32)
    c["cBias"] = bt
    cv = np.zeros((128, 64), np.float32)
    pq = np.arange(128, dtype=np.float64)
    for h in range(H):
        gran = ALIBI_GRAN[h]
        for i2 in range(4):
            fp = (128.0 * i2 + pq) % gran
            cv[:, h * 4 + i2] = np.exp(64.0 - slopes[h] * fp).astype(np.float32)
    c["cCVT"] = cv
    p2 = np.zeros((32, 512), BF_NP)
    for b in range(32):
        p2[b, b * 16:(b + 1) * 16] = 1
    c["cFP2"] = p2
    p2b = np.zeros((16, 512), BF_NP)
    for e in range(16):
        p2b[e, e::16] = 1
    c["cFP2B"] = p2b
    fsd = np.zeros((32, 512), BF_NP)
    for p in range(512):
        fsd[p // 16, p] = 1
    c["cFSD"] = fsd
    fsg = np.zeros((16, 512), BF_NP)
    for p in range(512):
        fsg[p % 16, p] = 1
    c["cFSG"] = fsg
    half = 512
    freqs = 1.0 / 10000.0 ** (np.arange(half, dtype=np.float32) / half)
    ang = np.arange(T, dtype=np.float32)[None, :] * freqs[:, None]
    c["cCos"] = np.cos(ang).astype(np.float32).reshape(4, 128, T).transpose(
        1, 0, 2).reshape(128, 4 * T).astype(BF_NP)
    c["cSin"] = np.sin(ang).astype(np.float32).reshape(4, 128, T).transpose(
        1, 0, 2).reshape(128, 4 * T).astype(BF_NP)
    return c


class Sem:
    def __init__(self, handle, name):
        self.h = handle
        self.name = name
        self.n = 0


class EQ:
    def __init__(self, name):
        self.name = name
        self.items = []
        self.auto_edge = False
        self.esem = None

    def wait(self, sem, val=None):
        v = sem.n if val is None else val
        if v > 0:
            self.items.append(("wait", sem, v))

    def op(self, fn, inc=None, inc_n=1):
        incs = []
        if inc is not None:
            inc.n += inc_n
            incs.append((inc, inc_n))
        if self.auto_edge and inc is not self.esem:
            self.esem.n += 1
            incs.append((self.esem, 1))
        self.items.append(("op", fn, incs))
        return incs

    def dma(self, fn, sem):
        return self.op(fn, inc=sem, inc_n=16)

    def opE(self, fn, esem):
        self.op(fn, inc=esem)
        self.wait(esem)

    def last_op_incs(self):
        for it in reversed(self.items):
            if it[0] == "op":
                return it[2]
        return None


def build(n_sublayers=4):
    nc = bass.Bass("TRN2", target_bir_lowering=False)
    import os as _os
    if not int(_os.environ.get("KRACE", "0")):
        nc.detect_race_conditions = False

    din = {}
    def dram_in(name, shape, dtype=F32):
        din[name] = nc.dram_tensor(name, shape, dtype, kind="ExternalInput").ap()
        return din[name]

    xin = dram_in("xin", [T, C])
    for nm in ["a_Wk", "a_Wv", "a_Wr", "a_Wo", "b_wq", "b_wk", "b_wv", "b_out_w"]:
        dram_in(nm, [C, C])
    dram_in("a_par", [40, 128])
    dram_in("b_out_b", [1, C])
    dram_in("f_G1R", [32, 128])
    dram_in("f_G2R", [16, 64])
    dram_in("f_G1I", [32, 128])
    dram_in("f_G2I", [16, 64])
    dram_in("f_bias", [1, C])
    dram_in("cJ", [128, CH])
    dram_in("cIdentF", [128, 128])
    dram_in("cIdentB", [128, 128], BF16)
    dram_in("cDiag", [128, 128], BF16)
    dram_in("cOnes", [128, 512], BF16)
    dram_in("cBias", [128, 256])
    dram_in("cCVT", [128, 64])
    dram_in("cCos", [128, 4 * T], BF16)
    dram_in("cFP2", [32, 512], BF16)
    dram_in("cFP2B", [16, 512], BF16)
    dram_in("cFSD", [32, 512], BF16)
    dram_in("cFSG", [16, 512], BF16)
    dram_in("cSin", [128, 4 * T], BF16)
    out = nc.dram_tensor("out", [T, C], F32, kind="ExternalOutput").ap()
    dbg = nc.dram_tensor("dbg", [128, 16384], BF16, kind="ExternalOutput").ap()
    dbgf = nc.dram_tensor("dbgf", [128, 512], F32, kind="ExternalOutput").ap()
    wstash = nc.dram_tensor("wstash", [128, 8192], BF16, kind="Internal").ap()

    tens = {}
    def sb(name, shape, dtype, offset):
        t = nc.alloc_sbuf_tensor_at(name, shape, dtype, offset=offset)
        tens[name] = t.ap()
        return tens[name]

    SB_BASE = 16512
    OFF = SB_BASE
    def place(name, shape, dtype):
        nonlocal OFF
        szb = {F32: 4, BF16: 2}[dtype]
        w = szb
        for d in shape[1:]:
            w *= d
        at = OFF
        OFF = at + ((w + 63) // 64) * 64
        return sb(name, shape, dtype, at), at

    X, _ = place("X", [128, CT * T], F32)
    cJ, _ = place("sJ", [128, CH], F32)
    cIdF, _ = place("sIdF", [128, 128], F32)
    cIdB, _ = place("sIdB", [128, 128], BF16)
    cDg, _ = place("sDiag", [128, 128], BF16)
    cOn, _ = place("sOnes", [128, 512], BF16)
    cBt, _ = place("sBias", [128, 256], F32)
    cCV, _ = place("sCVT", [128, 64], F32)
    PAR, _ = place("PAR", [128, 32], F32)
    PARB, _ = place("PARB", [128, 48], BF16)
    PARF, _ = place("PARF", [128, 48], F32)
    AUGA, _ = place("AUGA", [128, 1024], BF16)
    AUGB, _ = place("AUGB", [128, 1024], BF16)
    AUGC, _ = place("AUGC", [128, 1024], BF16)
    EPSB, _ = place("EPSB", [128, 1], F32)
    PRAW, _ = place("PRAW", [128, 40], F32)
    PPB, _ = place("PPB", [128, 8], F32)
    AAC, _ = place("AAC", [128, 8], F32)
    BBC, _ = place("BBC", [128, 8], F32)
    MRB, MRB_at = place("MRB", [128, T + 1], BF16)
    RSTD, RSTD_at = place("RSTD", [128, T], F32)
    XS, XS_at = place("XS", [128, CT * (CH + 1)], BF16)
    W0, W0_at = place("W0", [128, 8192], BF16)
    W1, W1_at = place("W1", [128, 8192], BF16)
    W2, W2_at = place("W2", [128, 8192], BF16)
    W3, W3_at = place("W3", [128, 8192], BF16)
    VAR_at = OFF
    SB_LIMIT = 229312
    assert VAR_at + 47000 <= SB_LIMIT, VAR_at

    V = VAR_at
    XSTG = sb("XSTG", [128, 2048], F32, V)
    WNAT = sb("WNAT", [128, 8192], BF16, V + 8192)
    XBF = sb("XBF", [128, CT * CH], BF16, V)
    XSQ = sb("XSQ", [128, CT * CH], BF16, V + 8192)
    MST = sb("MST", [128, CH], F32, V + 16384)
    VST = sb("VST", [128, CH], F32, V + 18432)
    VST2 = sb("VST2", [128, CH], F32, V + 20480)
    MIXA = sb("MIXA", [128, CT * CH], BF16, V)
    MIXB = sb("MIXB", [128, CT * CH], BF16, V + 8192)
    YG = sb("YG", [128, CT * CH], BF16, V + 16384)
    ZP = [sb(f"ZP{i}", [128, CH], F32, V + 24576 + i * 2048) for i in range(5)]
    CB = sb("CB", [128, CH + 1], F32, V + 34816)
    ABF = sb("ABF", [128, CH + 1], F32, V + 36992)
    BBF = sb("BBF", [128, CH + 1], F32, V + 39168)
    TMPA = sb("TMPA", [128, CH], BF16, V + 41344)
    TMPB = sb("TMPB", [128, CH], BF16, V + 42368)
    TMPG1 = sb("TMPG1", [128, CH], BF16, V + 43392)
    TMPG2 = sb("TMPG2", [128, CH], BF16, V + 44416)
    assert V + 45440 <= SB_LIMIT, V
    FO1 = sb("FO1", [128, 512], BF16, V)
    FEA = sb("FEA", [128, 512], F32, V + 1024)
    FG1R = sb("FG1R", [128, 128], BF16, V + 3072)
    FG2R = sb("FG2R", [128, 64], BF16, V + 3328)
    FG1I = sb("FG1I", [128, 128], BF16, V + 3456)
    FG2I = sb("FG2I", [128, 64], BF16, V + 3712)
    FP2 = sb("FP2", [128, 512], BF16, V + 3840)
    FP2B = sb("FP2B", [128, 512], BF16, V + 4864)
    FSD = sb("FSD", [128, 512], BF16, V + 5888)
    FSG = sb("FSG", [128, 512], BF16, V + 6912)
    FWT = [sb(f"FWT{i}", [128, 512], F32, V + 7936 + 2048 * i) for i in range(8)]
    QTXS = sb("QTXS", [128, CT * T], BF16, V)
    WINB = sb("WINB", [128, 8192], BF16, RSTD_at)
    VN = []
    for j in range(16):
        if j < 7:
            at = W2_at + j * 2112
        elif j < 14:
            at = W3_at + (j - 7) * 2112
        else:
            at = V + 32768 + (j - 14) * 2112
        VN.append(sb(f"VN{j}", [128, 1040], BF16, at))
    BW2 = V + 32768 + 4608
    ROPE = sb("ROPEB", [128, 1024], BF16, BW2)
    RTS = sb("RTS", [128, 1024], BF16, BW2 + 2048)
    QP8 = sb("QP8", [128, 2048], BF16, BW2 + 4096)
    TMPC = sb("TMPC", [128, 256], BF16, BW2 + 8192)
    TMPD = sb("TMPD", [128, 256], BF16, BW2 + 8704)
    OTB = sb("OTB", [128, CT * 512], BF16, BW2)
    WNAT2 = sb("WNAT2", [128, 4096], BF16, BW2)
    FO2 = sb("FO2", [128, 512], BF16, BW2 + 2048)
    PT0 = sb("PT0", [128, 512], BF16, MRB_at)
    PT1 = sb("PT1", [128, 512], BF16, MRB_at + 1024)
    PTD = [PT0, PT1]
    PSC = sb("PSC", [128, 128], BF16, MRB_at + 2048)
    STG64 = sb("STG64", [128, 256], BF16, MRB_at + 2304)
    RCB = sb("RCB", [128, 12], F32, MRB_at + 2816)
    LNZ = sb("LNZ", [128, 256], F32, MRB_at + 3072)
    OST = sb("OST", [128, 2 * 1024], F32, V)

    PS = [nc.alloc_psum_tensor(f"ps{i}", [128, 512], F32).ap() for i in range(7)]
    PST = nc.alloc_psum_tensor("pst", [128, 1024], BF16).ap()

    sem_names = ["sIN", "sGPD", "sPE", "sDV", "sAC", "sGE", "sSY", "sST",
                 "sCN", "sXA", "sXB", "sSA", "sSB"]
    sem_ctx = [nc.semaphore(n) for n in sem_names]
    handles = [ctx.__enter__() for ctx in sem_ctx]
    sems = {n: Sem(h, n) for n, h in zip(sem_names, handles)}
    sIN, sGPD, sPE, sDV, sAC, sGE, sSY, sST, sCN, sXA, sXB, sSA, sSB = (
        sems[n] for n in sem_names)
    all_sems = list(sems.values())
    esem = {"sync": sSY, "gpsimd": sGE, "vector": sDV, "scalar": sAC, "tensor": sPE}
    q = {n: EQ(n) for n in ("sync", "gpsimd", "vector", "scalar", "tensor")}
    for n in ("vector", "scalar"):
        q[n].auto_edge = True
        q[n].esem = esem[n]
    SY, GP, DV, SC, PE = (q[n] for n in ("sync", "gpsimd", "vector", "scalar", "tensor"))

    def barrier():
        for en, qq in q.items():
            incs = qq.last_op_incs()
            if incs is not None and len(incs) == 0:
                esem[en].n += 1
                incs.append((esem[en], 1))
        snap = [(s, s.n) for s in all_sems if s.n > 0]
        for qq in q.values():
            for s, v in snap:
                qq.wait(s, v)

    blk_ctx = nc.Block()
    blk = blk_ctx.__enter__()

    def cpy(eng, out, in_):
        if hasattr(eng, "tensor_copy"):
            return eng.tensor_copy(out=out, in_=in_)
        return eng.copy(out=out, in_=in_)

    def mm(psum, lhsT, rhs, start, stop, inc=None):
        PE.op(lambda eng, a=psum, b=lhsT, c=rhs, s1=start, s2=stop:
              eng.matmul(a, b, c, start=s1, stop=s2), inc=inc)

    def wsl(w, ct, dblk):
        return w[:, ct * 1024 + dblk * 128: ct * 1024 + dblk * 128 + 128]

    def proj_group(psum, wt, i, rhs_fn, aug1, aug2, inc=None):
        for ct in range(CT):
            mm(psum, wsl(wt, ct, i), rhs_fn(ct), start=(ct == 0), stop=False)
            if ct == 3 and aug1 is not None:
                mm(psum, aug1[0], aug1[1], start=False, stop=False)
        last = aug2 if aug2 is not None else aug1
        if last is None:
            raise ValueError
        mm(psum, last[0], last[1], start=False, stop=True, inc=inc)

    Xv = X.rearrange("p (ct t) -> p ct t", t=T)
    XSv = XS.rearrange("p (ct j) -> p ct j", j=CH + 1)
    def par_w(i): return PAR[:, i:i + 1]
    def par_upw(i): return PAR[:, 8 + i:9 + i]
    def par_cl(i): return PAR[:, 16 + i:17 + i]
    def par_w512(i): return PAR[:, 24 + i:25 + i]
    def parb(m, ct): return PARB[:, m * 8 + ct: m * 8 + ct + 1]
    def parb1m(m, ct): return PARB[:, 24 + m * 8 + ct: 25 + m * 8 + ct]
    def parf(m, ct): return PARF[:, m * 8 + ct: m * 8 + ct + 1]
    def parf1m(m, ct): return PARF[:, 24 + m * 8 + ct: 25 + m * 8 + ct]

    for dst, src in [(cJ, din["cJ"]), (cIdF, din["cIdentF"]), (cIdB, din["cIdentB"]),
                     (cDg, din["cDiag"]), (cOn, din["cOnes"]), (cBt, din["cBias"]),
                     (cCV, din["cCVT"])]:
        SY.dma(lambda eng, d=dst, s=src: eng.dma_start(out=d, in_=s), sCN)

    DV.op(lambda eng: eng.memset(EPSB, EPS), inc=sDV)
    xin_t = xin.rearrange("(tt p) c -> tt p c", p=128)
    PE.wait(sCN)
    for tt in range(16):
        par = tt % 2
        sX = sXA if par == 0 else sXB
        if tt >= 2:
            SY.wait(sAC)
        SY.dma(lambda eng, tt=tt, par=par: eng.dma_start(
            out=XSTG[:, par * 1024:(par + 1) * 1024], in_=xin_t[tt]), sX)
        PE.wait(sX)
        PE.wait(sAC)
        for cb in range(8):
            bank = PS[cb // 4]
            PE.op(lambda eng, cb=cb, par=par, bank=bank: eng.transpose(
                bank[:, (cb % 4) * 128:(cb % 4 + 1) * 128],
                XSTG[:, par * 1024 + cb * 128: par * 1024 + (cb + 1) * 128],
                cIdF), inc=(sPE if cb == 7 else None))
        SC.wait(sPE)
        for hf in range(2):
            dst = Xv[:, hf * 4:(hf + 1) * 4, tt * 128:(tt + 1) * 128]
            SC.op(lambda eng, hf=hf, dst=dst: cpy(eng,
                dst, PS[hf].rearrange("p (a r) -> p a r", a=4)),
                inc=(sAC if hf == 1 else None))

    SY.wait(sAC)
    SY.dma(lambda eng: eng.dma_start(out=XSTG[0:40, 0:128], in_=din["a_par"]), sIN)
    PE.wait(sIN)
    PE.op(lambda eng: eng.transpose(PS[2][:, 0:128], XSTG[:, 0:128], cIdF), inc=sPE)
    SC.wait(sPE)
    SC.op(lambda eng: cpy(eng, PRAW, PS[2][:, 0:40]), inc=sAC)
    SC.op(lambda eng: eng.activation(
        out=PAR[:, 0:8], in_=PRAW[:, 0:8], func=AF.Exp))
    SC.op(lambda eng: eng.drain(fusable=False), inc=sAC)
    DV.wait(sAC)
    DV.opE(lambda eng: eng.tensor_tensor(
        out=PAR[:, 8:16], in0=PRAW[:, 8:16], in1=PAR[:, 0:8], op=ALU.subtract), sDV)
    DV.opE(lambda eng: eng.tensor_scalar(
        out=PAR[:, 16:24], in0=PAR[:, 8:16], scalar1=43.0, scalar2=None,
        op0=ALU.subtract), sDV)
    DV.opE(lambda eng: eng.tensor_scalar(
        out=PAR[:, 24:32], in0=PAR[:, 0:8], scalar1=512.0, scalar2=None,
        op0=ALU.mult), sDV)
    DV.opE(lambda eng: eng.tensor_copy(out=PARB[:, 0:24], in_=PRAW[:, 16:40]), sDV)
    DV.opE(lambda eng: eng.tensor_scalar(
        out=PARB[:, 24:48], in0=PRAW[:, 16:40],
        scalar1=-1.0, scalar2=1.0, op0=ALU.mult, op1=ALU.add), sDV)
    DV.op(lambda eng: eng.tensor_copy(out=PARF, in_=PARB), inc=sDV)

    def prep_weight(src_name, slot):
        GP.wait(sPE)
        GP.dma(lambda eng, s=src_name: eng.dma_start(
            out=WNAT.rearrange("p (r c) -> p r c", c=1024),
            in_=din[s].rearrange("(r p) c -> p r c", p=128)), sGPD)
        PE.wait(sGPD)
        slot_v = slot.rearrange("p (ct d) -> p ct d", d=1024)
        for r in range(8):
            PE.wait(sDV)
            PE.wait(sAC)
            for dblk in range(8):
                PE.op(lambda eng, r=r, dblk=dblk: eng.transpose(
                    PST[:, dblk * 128:(dblk + 1) * 128],
                    WNAT[:, r * 1024 + dblk * 128: r * 1024 + (dblk + 1) * 128],
                    cIdB), inc=(sPE if dblk == 7 else None))
            evq, evsem = (DV, sDV) if (r % 2 == 0) else (SC, sAC)
            evq.wait(sPE)
            dst = slot_v[:, :, r * 128:(r + 1) * 128]
            evq.op(lambda eng, dst=dst: cpy(
                eng, dst, PST.rearrange("p (a r2) -> p a r2", a=8)), inc=evsem)

    def make_svec(slot_w, lhs_fn, dst_aug, dst_row, scale=-1.0):
        for hf in range(2):
            for ct in range(CT):
                mm(PS[6][0:1, 0:512], lhs_fn(ct),
                   slot_w[:, ct * 1024 + hf * 512: ct * 1024 + hf * 512 + 512],
                   start=(ct == 0), stop=(ct == CT - 1),
                   inc=(sPE if ct == CT - 1 else None))
            SC.wait(sPE)
            SC.op(lambda eng, hf=hf, dst_aug=dst_aug, dst_row=dst_row, scale=scale:
                  eng.activation(
                      out=dst_aug[dst_row:dst_row + 1, hf * 512: hf * 512 + 512],
                      in_=PS[6][0:1, 0:512], func=AF.Copy, scale=scale), inc=sAC)
            PE.wait(sAC)

    prep_weight("a_Wk", W0)
    prep_weight("a_Wv", W1)
    prep_weight("a_Wr", W2)
    prep_weight("a_Wo", W3)
    barrier()
    make_svec(W0, lambda ct: parb(0, ct), AUGA, 0)
    make_svec(W0, lambda ct: parb1m(0, ct), AUGA, 32)
    make_svec(W1, lambda ct: parb(1, ct), AUGA, 64)
    make_svec(W1, lambda ct: parb1m(1, ct), AUGB, 0)
    make_svec(W2, lambda ct: parb(2, ct), AUGB, 32)
    make_svec(W2, lambda ct: parb1m(2, ct), AUGB, 64)

    def ln_stats():
        barrier()
        DV.opE(lambda eng: eng.memset(MRB[:, 0:1], 0.0), sDV)
        for c in range(NCH):
            t0 = c * CH
            GP.wait(sAC)
            GP.dma(lambda eng, t0=t0: eng.dma_start(
                out=XBF.rearrange("p (ct j) -> p ct j", j=CH),
                in_=Xv[:, :, t0:t0 + CH]), sGPD)
            SC.wait(sGPD)
            SC.wait(sPE)
            SC.op(lambda eng: eng.activation(out=XSQ, in_=XBF, func=AF.Square),
                  inc=sAC)
            PE.wait(sGPD)
            PE.wait(sDV)
            for ct in range(CT):
                mm(PS[0], cOn[:, 0:128], XBF[:, ct * CH:(ct + 1) * CH],
                   start=(ct == 0), stop=(ct == CT - 1))
            PE.wait(sAC)
            for ct in range(CT):
                mm(PS[1], cOn[:, 0:128], XSQ[:, ct * CH:(ct + 1) * CH],
                   start=(ct == 0), stop=(ct == CT - 1),
                   inc=(sPE if ct == CT - 1 else None))
            DV.wait(sPE)
            DV.op(lambda eng: eng.tensor_scalar(
                out=MST, in0=PS[0], scalar1=1.0 / C, scalar2=None, op0=ALU.mult))
            DV.op(lambda eng: eng.tensor_tensor(
                out=VST, in0=MST, in1=MST, op=ALU.mult))
            DV.op(lambda eng: eng.scalar_tensor_tensor(
                out=VST, in0=PS[1], scalar=1.0 / C, in1=VST,
                op0=ALU.mult, op1=ALU.subtract), inc=sDV)
            SC.wait(sDV)
            SC.op(lambda eng: eng.activation(
                out=VST2, in_=VST, func=AF.Sqrt, bias=EPSB), inc=sAC)
            DV.wait(sAC)
            DV.op(lambda eng, t0=t0: eng.reciprocal(
                out=RSTD[:, t0:t0 + CH], in_=VST2))
            DV.op(lambda eng, t0=t0: eng.tensor_tensor(
                out=MRB[:, 1 + t0:1 + t0 + CH], in0=MST,
                in1=RSTD[:, t0:t0 + CH], op=ALU.mult), inc=sDV)
        barrier()

    def make_xs(c, q_=None, sem=None):
        q_ = q_ or DV
        sem = sem or sDV
        t0 = c * CH
        if c == 0:
            q_.opE(lambda eng: eng.memset(XSv[:, :, 0:1], 0.0), sem)
        else:
            q_.opE(lambda eng: eng.tensor_copy(
                out=XSv[:, :, 0:1], in_=XSv[:, :, CH:CH + 1]), sem)
        for ct in range(CT):
            q_.op(lambda eng, ct=ct, t0=t0: eng.tensor_tensor(
                out=XS[:, ct * (CH + 1) + 1: ct * (CH + 1) + 1 + CH],
                in0=X[:, ct * T + t0: ct * T + t0 + CH],
                in1=RSTD[:, t0:t0 + CH], op=ALU.mult),
                inc=(sem if ct == CT - 1 else None))

    def make_mix(m, dstbuf):
        for ct in range(CT):
            DV.op(lambda eng, ct=ct, m=m: eng.tensor_scalar(
                out=TMPG1, in0=XS[:, ct * (CH + 1) + 1: ct * (CH + 1) + 1 + CH],
                scalar1=parf(m, ct), scalar2=None, op0=ALU.mult))
            DV.op(lambda eng, ct=ct, m=m: eng.tensor_scalar(
                out=TMPG2, in0=XS[:, ct * (CH + 1): ct * (CH + 1) + CH],
                scalar1=parf1m(m, ct), scalar2=None, op0=ALU.mult))
            DV.op(lambda eng, ct=ct, dstbuf=dstbuf: eng.tensor_tensor(
                out=dstbuf[:, ct * CH:(ct + 1) * CH], in0=TMPG1, in1=TMPG2,
                op=ALU.add), inc=(sDV if ct == CT - 1 else None))

    import os
    DBG = int(os.environ.get("KDBG", "9"))
    KAUG = int(os.environ.get("KAUG", "1"))

    def attn_a():
        ln_stats()
        if DBG < 2:
            barrier()
            return
        DV.opE(lambda eng: eng.memset(PPB, -1e38), sDV)
        DV.opE(lambda eng: eng.memset(AAC, 0.0), sDV)
        DV.opE(lambda eng: eng.memset(BBC, 0.0), sDV)
        for c in range(NCH):
            t0 = c * CH
            DV.wait(sPE)
            make_xs(c)
            make_mix(2, MIXA)
            mix_r = sDV.n
            if DBG < 3:
                continue
            sig_done = []
            for i in range(CT):
                pr = PS[i % 2]
                PE.wait(sDV, mix_r)
                if i >= 2:
                    PE.wait(sAC, sig_done[i - 2])
                proj_group(pr, W2, i,
                           lambda ct: MIXA[:, ct * CH:(ct + 1) * CH],
                           (AUGB[32:33, i * 128:(i + 1) * 128],
                            MRB[32:33, 1 + t0:1 + t0 + CH]),
                           (AUGB[64:65, i * 128:(i + 1) * 128],
                            MRB[64:65, t0:t0 + CH]), inc=sPE)
                SC.wait(sPE)
                SC.op(lambda eng, i=i, pr=pr: eng.activation(
                    out=YG[:, i * CH:(i + 1) * CH], in_=pr, func=AF.Sigmoid),
                    inc=sAC)
                sig_done.append(sAC.n)
            r_all = sPE.n
            if DBG < 4:
                continue
            DV.wait(sPE, r_all)
            make_mix(0, MIXA)
            make_mix(1, MIXB)
            mix_kv = sDV.n
            wkv_end = [0, 0]
            for i in range(CT):
                pk, pv = PS[2 + (i % 2) * 2], PS[3 + (i % 2) * 2]
                PE.wait(sDV, mix_kv)
                if i >= 2:
                    PE.wait(sDV, wkv_end[i % 2])
                proj_group(pk, W0, i,
                           lambda ct: MIXA[:, ct * CH:(ct + 1) * CH],
                           (AUGA[0:1, i * 128:(i + 1) * 128],
                            MRB[0:1, 1 + t0:1 + t0 + CH]),
                           (AUGA[32:33, i * 128:(i + 1) * 128],
                            MRB[32:33, t0:t0 + CH]))
                proj_group(pv, W1, i,
                           lambda ct: MIXB[:, ct * CH:(ct + 1) * CH],
                           (AUGA[64:65, i * 128:(i + 1) * 128],
                            MRB[64:65, 1 + t0:1 + t0 + CH]),
                           (AUGB[0:1, i * 128:(i + 1) * 128],
                            MRB[0:1, t0:t0 + CH]), inc=sPE)
                kv_done = sPE.n
                if DBG < 5:
                    continue

                DV.wait(sPE, kv_done)
                DV.op(lambda eng, i=i, pk=pk: eng.scalar_tensor_tensor(
                    out=ZP[0], in0=cJ, scalar=par_w(i), in1=pk,
                    op0=ALU.mult, op1=ALU.add))
                DV.opE(lambda eng, i=i: eng.tensor_copy(
                    out=CB[:, 0:1], in_=PPB[:, i:i + 1]), sDV)
                DV.opE(lambda eng, i=i: eng.tensor_tensor_scan(
                    out=CB[:, 1:CH + 1], data0=ZP[0], data1=ZP[0],
                    initial=PPB[:, i:i + 1], op0=ALU.max, op1=ALU.max), sDV)
                DV.op(lambda eng: eng.tensor_tensor(
                    out=ZP[1], in0=CB[:, 0:CH], in1=ZP[0], op=ALU.subtract))
                DV.op(lambda eng, i=i: eng.tensor_scalar(
                    out=ZP[4], in0=ZP[1], scalar1=par_cl(i), scalar2=None,
                    op0=ALU.max))
                DV.op(lambda eng: eng.tensor_scalar(
                    out=ZP[2], in0=ZP[4], scalar1=0.0, scalar2=None,
                    op0=ALU.min))
                DV.op(lambda eng: eng.tensor_scalar(
                    out=ZP[3], in0=ZP[1], scalar1=-1.0, scalar2=0.0,
                    op0=ALU.mult, op1=ALU.min), inc=sDV)
                SC.wait(sDV)
                SC.op(lambda eng: eng.activation(out=ZP[1], in_=ZP[2], func=AF.Exp))
                SC.op(lambda eng: eng.activation(out=ZP[2], in_=ZP[3], func=AF.Exp))
                SC.op(lambda eng, i=i: eng.activation(
                    out=ZP[3], in_=ZP[4], func=AF.Exp, bias=par_upw(i),
                    scale=-1.0), inc=sAC)
                DV.wait(sAC)
                DV.op(lambda eng, pv=pv: eng.tensor_tensor(
                    out=ZP[0], in0=ZP[2], in1=pv, op=ALU.mult))
                DV.opE(lambda eng, i=i: eng.tensor_copy(
                    out=ABF[:, 0:1], in_=AAC[:, i:i + 1]), sDV)
                DV.opE(lambda eng, i=i: eng.tensor_tensor_scan(
                    out=ABF[:, 1:CH + 1], data0=ZP[1], data1=ZP[0],
                    initial=AAC[:, i:i + 1], op0=ALU.mult, op1=ALU.add), sDV)
                DV.opE(lambda eng, i=i: eng.tensor_copy(
                    out=BBF[:, 0:1], in_=BBC[:, i:i + 1]), sDV)
                DV.opE(lambda eng, i=i: eng.tensor_tensor_scan(
                    out=BBF[:, 1:CH + 1], data0=ZP[1], data1=ZP[2],
                    initial=BBC[:, i:i + 1], op0=ALU.mult, op1=ALU.add), sDV)
                DV.op(lambda eng, pv=pv: eng.tensor_tensor(
                    out=ZP[4], in0=ZP[3], in1=pv, op=ALU.mult))
                DV.op(lambda eng: eng.tensor_tensor(
                    out=ZP[0], in0=ABF[:, 0:CH], in1=ZP[4], op=ALU.add))
                DV.op(lambda eng: eng.tensor_tensor(
                    out=ZP[2], in0=BBF[:, 0:CH], in1=ZP[3], op=ALU.add),
                    inc=sDV)
                wkv_end[i % 2] = sDV.n
                DV.opE(lambda eng, i=i: eng.tensor_scalar(
                    out=PPB[:, i:i + 1], in0=CB[:, CH:CH + 1],
                    scalar1=par_w512(i), scalar2=None, op0=ALU.subtract), sDV)
                DV.opE(lambda eng, i=i: eng.tensor_copy(
                    out=AAC[:, i:i + 1], in_=ABF[:, CH:CH + 1]), sDV)
                DV.opE(lambda eng, i=i: eng.tensor_copy(
                    out=BBC[:, i:i + 1], in_=BBF[:, CH:CH + 1]), sDV)
                SC.wait(sDV)
                SC.op(lambda eng: eng.activation(out=ZP[1], in_=ZP[2], func=AF.Ln))
                SC.op(lambda eng: eng.activation(
                    out=ZP[4], in_=ZP[1], func=AF.Exp, scale=-1.0), inc=sAC)
                DV.wait(sAC)
                DV.op(lambda eng: eng.tensor_tensor(
                    out=TMPA, in0=ZP[0], in1=ZP[4], op=ALU.mult))
                DV.op(lambda eng, i=i: eng.tensor_tensor(
                    out=YG[:, i * CH:(i + 1) * CH], in0=TMPA,
                    in1=YG[:, i * CH:(i + 1) * CH], op=ALU.mult), inc=sDV)
            wkv_all = sDV.n
            if DBG < 6:
                continue
            PE.wait(sDV, wkv_all)
            for dp in range(CT):
                for ct in range(CT):
                    mm(PS[6], wsl(W3, ct, dp), YG[:, ct * CH:(ct + 1) * CH],
                       start=(ct == 0), stop=(ct == CT - 1),
                       inc=(sPE if ct == CT - 1 else None))
                DV.wait(sPE)
                DV.op(lambda eng, dp=dp, t0=t0: eng.tensor_tensor(
                    out=X[:, dp * T + t0: dp * T + t0 + CH],
                    in0=X[:, dp * T + t0: dp * T + t0 + CH], in1=PS[6],
                    op=ALU.add), inc=sDV)
                PE.wait(sDV)
        barrier()

    if n_sublayers >= 1:
        attn_a()

    def ffn_build():
        barrier()
        for dst, s in ((FP2, "cFP2"), (FP2B, "cFP2B"), (FSD, "cFSD"),
                       (FSG, "cFSG")):
            SY.dma(lambda eng, d=dst, s=s: eng.dma_start(
                out=d[0:din[s].shape[0], :], in_=din[s]), sIN)
        for dst, s in ((FG1R, "f_G1R"), (FG2R, "f_G2R"), (FG1I, "f_G1I"),
                       (FG2I, "f_G2I")):
            GP.dma(lambda eng, d=dst, s=s: eng.dma_start(
                out=d[0:din[s].shape[0], :], in_=din[s]), sGPD)
        GP.dma(lambda eng: eng.dma_start(
            out=AUGC[32:33, 0:1024], in_=din["f_bias"]), sGPD)
        PE.wait(sGPD)
        PE.wait(sIN)
        for part in range(2):
            G1 = FG1R if part == 0 else FG1I
            G2 = FG2R if part == 0 else FG2I
            for c4 in range(4):
                mm(PS[0][0:32, :], G1[0:32, c4 * 32:(c4 + 1) * 32],
                   FP2[0:32, :], start=True, stop=True, inc=sPE)
                SC.wait(sPE)
                SC.op(lambda eng, G1=G1: eng.activation(
                    out=FO1[0:32, :], in_=PS[0][0:32, :], func=AF.Copy))
                mm(PS[1][0:16, :], G2[0:16, c4 * 16:(c4 + 1) * 16],
                   FP2B[0:16, :], start=True, stop=True, inc=sPE)
                SC.wait(sPE)
                SC.op(lambda eng: eng.activation(
                    out=FO2[0:16, :], in_=PS[1][0:16, :], func=AF.Copy),
                    inc=sAC)
                PE.wait(sAC)
                for pt in range(4):
                    PE.wait(sDV)
                    mm(PS[2], FSD[0:32, pt * 128:(pt + 1) * 128], FO1[0:32, :],
                       start=True, stop=True)
                    mm(PS[3], FSG[0:16, pt * 128:(pt + 1) * 128], FO2[0:16, :],
                       start=True, stop=True, inc=sPE)
                    DV.wait(sPE)
                    DV.op(lambda eng: eng.tensor_copy(out=FEA, in_=PS[2]))
                    DV.op(lambda eng: eng.tensor_tensor(
                        out=FEA, in0=PS[3], in1=FEA, op=ALU.mult))
                    w = FWT[part * 4 + pt]
                    if c4 == 0:
                        DV.op(lambda eng, w=w: eng.tensor_copy(out=w, in_=FEA),
                              inc=sDV)
                    else:
                        DV.op(lambda eng, w=w: eng.tensor_tensor(
                            out=w, in0=w, in1=FEA, op=ALU.add), inc=sDV)
        SC.wait(sDV)
        for ct2 in range(4):
            SC.op(lambda eng, ct2=ct2: eng.activation(
                out=W0[:, ct2 * 1024: ct2 * 1024 + 512], in_=FWT[ct2],
                func=AF.Copy))
            SC.op(lambda eng, ct2=ct2: eng.activation(
                out=W0[:, ct2 * 1024 + 512: ct2 * 1024 + 1024], in_=FWT[4 + ct2],
                func=AF.Copy))
            SC.op(lambda eng, ct2=ct2: eng.activation(
                out=W0[:, (4 + ct2) * 1024: (4 + ct2) * 1024 + 512],
                in_=FWT[4 + ct2], func=AF.Copy, scale=-1.0))
            SC.op(lambda eng, ct2=ct2: eng.activation(
                out=W0[:, (4 + ct2) * 1024 + 512: (4 + ct2) * 1024 + 1024],
                in_=FWT[ct2], func=AF.Copy), inc=sAC)
        barrier()
        make_svec(W0, lambda ct: cOn[:, 0:1], AUGC, 0)
        barrier()
        SY.dma(lambda eng: eng.dma_start(out=wstash, in_=W0), sIN)

    def ffn(reload=False):
        if reload:
            barrier()
            SY.wait(sIN)
            SY.dma(lambda eng: eng.dma_start(out=W0, in_=wstash), sIN)
        ln_stats()
        if reload:
            PE.wait(sIN)
        res_done = [0, 0]
        for c in range(NCH):
            t0 = c * CH
            DV.wait(sPE)
            make_xs(c)
            xs_v = sDV.n
            PE.wait(sDV, xs_v)
            for dp in range(CT):
                psf = PS[dp % 2]
                if c * CT + dp >= 2:
                    PE.wait(sDV, res_done[dp % 2])
                proj_group(psf, W0, dp,
                           lambda ct: XS[:, ct * (CH + 1) + 1:
                                         ct * (CH + 1) + 1 + CH],
                           (AUGC[0:1, dp * 128:(dp + 1) * 128],
                            MRB[0:1, 1 + t0:1 + t0 + CH]),
                           (AUGC[32:33, dp * 128:(dp + 1) * 128],
                            cOn[32:33, 0:CH]), inc=sPE)
                DV.wait(sPE)
                DV.op(lambda eng, dp=dp, t0=t0, psf=psf: eng.tensor_tensor(
                    out=X[:, dp * T + t0: dp * T + t0 + CH],
                    in0=X[:, dp * T + t0: dp * T + t0 + CH], in1=psf,
                    op=ALU.add), inc=sDV)
                res_done[dp % 2] = sDV.n
        barrier()

    if n_sublayers >= 2:
        ffn_build()
        ffn()

    import os as _os_chk
    def load_win(src_name):
        win_v = WINB.rearrange("p (ct d) -> p ct d", d=1024)
        src_t = din[src_name].rearrange("(r p) cc -> p r cc", p=128)
        for half in range(2):
            GP.wait(sPE)
            GP.dma(lambda eng, half=half, src_t=src_t: eng.dma_start(
                out=WNAT2.rearrange("p (r cc) -> p r cc", cc=1024),
                in_=src_t[:, half * 4:(half + 1) * 4, :]), sGPD)
            PE.wait(sGPD)
            for r4 in range(4):
                r = half * 4 + r4
                PE.wait(sDV)
                PE.wait(sAC)
                for dblk in range(8):
                    PE.op(lambda eng, r4=r4, dblk=dblk: eng.transpose(
                        PST[:, dblk * 128:(dblk + 1) * 128],
                        WNAT2[:, r4 * 1024 + dblk * 128: r4 * 1024 + (dblk + 1) * 128],
                        cIdB), inc=(sPE if dblk == 7 else None))
                evq, evsem = (DV, sDV) if (r % 2 == 0) else (SC, sAC)
                evq.wait(sPE)
                dst = win_v[:, :, r * 128:(r + 1) * 128]
                evq.op(lambda eng, dst=dst: cpy(
                    eng, dst, PST.rearrange("p (a r2) -> p a r2", a=8)), inc=evsem)
        barrier()

    def kTs(dt):
        slot = W0 if dt < 4 else W1
        return slot[:, (dt % 4) * 2048:(dt % 4) * 2048 + 2048]

    def attn_b():
        ln_stats()
        GP.dma(lambda eng: eng.dma_start(
            out=AUGB[0:1, 0:1024], in_=din["b_out_b"]), sGPD)
        DV.wait(sGPD)
        for c in range(NCH):
            t0 = c * CH
            for ct in range(CT):
                DV.op(lambda eng, ct=ct, t0=t0: eng.tensor_tensor(
                    out=QTXS[:, ct * T + t0: ct * T + t0 + CH],
                    in0=X[:, ct * T + t0: ct * T + t0 + CH],
                    in1=RSTD[:, t0:t0 + CH], op=ALU.mult),
                    inc=(sDV if ct == CT - 1 else None))
        for j in range(16):
            oc = VN[j].rearrange("p (h d) -> p h d", d=65)[:, :, 64:65]
            DV.op(lambda eng, oc=oc: eng.tensor_copy(
                out=oc, in_=cOn[:, 0:16].unsqueeze(2)),
                inc=(sDV if j == 15 else None))
        barrier()
        load_win("b_wv")
        make_svec(WINB, lambda ct: cOn[:, 0:1], AUGA, 0)
        barrier()
        ev_done = [0, 0]
        for tt in range(16):
            for hf in range(2):
                idx = tt * 2 + hf
                ps_ = PS[idx % 2]
                if idx >= 2:
                    PE.wait(sAC, ev_done[idx % 2])
                for ct in range(CT):
                    mm(ps_, QTXS[:, ct * T + tt * 128: ct * T + (tt + 1) * 128],
                       WINB[:, ct * 1024 + hf * 512: ct * 1024 + hf * 512 + 512],
                       start=(ct == 0), stop=False)
                mm(ps_, MRB[0:1, 1 + tt * 128: 1 + (tt + 1) * 128],
                   AUGA[0:1, hf * 512: hf * 512 + 512],
                   start=False, stop=True, inc=sPE)
                SC.wait(sPE)
                dst = VN[tt].rearrange("p (h d) -> p h d",
                                       d=65)[:, hf * 8:(hf + 1) * 8, 0:64]
                SC.op(lambda eng, dst=dst, ps_=ps_: eng.activation(
                    out=dst, in_=ps_.rearrange("p (a b) -> p a b", a=8),
                    func=AF.Copy), inc=sAC)
                ev_done[idx % 2] = sAC.n
        barrier()
        load_win("b_wk")
        make_svec(WINB, lambda ct: cOn[:, 0:1], AUGA, 32)
        barrier()
        kv_ev = [0, 0, 0, 0]
        for c in range(NCH):
            t0 = c * CH
            for dt in range(CT):
                ps_ = PS[dt % 4]
                if kv_ev[dt % 4]:
                    PE.wait(sAC, kv_ev[dt % 4])
                for ct in range(CT):
                    mm(ps_, WINB[:, ct * 1024 + dt * 128: ct * 1024 + (dt + 1) * 128],
                       QTXS[:, ct * T + t0: ct * T + t0 + CH],
                       start=(ct == 0), stop=False)
                mm(ps_, AUGA[32:33, dt * 128:(dt + 1) * 128],
                   MRB[32:33, 1 + t0:1 + t0 + CH], start=False, stop=True, inc=sPE)
                SC.wait(sPE)
                SC.op(lambda eng, dt=dt, t0=t0, ps_=ps_: eng.activation(
                    out=kTs(dt)[:, t0:t0 + CH], in_=ps_, func=AF.Copy), inc=sAC)
                kv_ev[dt % 4] = sAC.n
            DV.wait(sAC)
            for hc in range(2):
                th = t0 + hc * 256
                GP.wait(sDV)
                GP.dma(lambda eng, th=th: eng.dma_start(
                    out=ROPE.rearrange("p (dt j) -> p dt j", j=256),
                    in_=din["cCos"].rearrange("p (dt t) -> p dt t", t=T)[:, :, th:th + 256]), sGPD)
                GP.dma(lambda eng, th=th: eng.dma_start(
                    out=RTS.rearrange("p (dt j) -> p dt j", j=256),
                    in_=din["cSin"].rearrange("p (dt t) -> p dt t", t=T)[:, :, th:th + 256]), sGPD)
                DV.wait(sGPD)
                for dt in range(4):
                    top = kTs(dt)[:, th:th + 256]
                    bot = kTs(dt + 4)[:, th:th + 256]
                    cosb = ROPE[:, dt * 256:(dt + 1) * 256]
                    sinb = RTS[:, dt * 256:(dt + 1) * 256]
                    DV.op(lambda eng, top=top, cosb=cosb: eng.tensor_tensor(
                        out=TMPC, in0=top, in1=cosb, op=ALU.mult))
                    DV.op(lambda eng, top=top, sinb=sinb: eng.tensor_tensor(
                        out=TMPD, in0=top, in1=sinb, op=ALU.mult))
                    DV.op(lambda eng, top=top, bot=bot, sinb=sinb: eng.tensor_tensor(
                        out=top, in0=bot, in1=sinb, op=ALU.mult))
                    DV.op(lambda eng, top=top: eng.tensor_tensor(
                        out=top, in0=TMPC, in1=top, op=ALU.subtract))
                    DV.op(lambda eng, bot=bot, cosb=cosb: eng.tensor_tensor(
                        out=TMPC, in0=bot, in1=cosb, op=ALU.mult))
                    DV.op(lambda eng, bot=bot: eng.tensor_tensor(
                        out=bot, in0=TMPD, in1=TMPC, op=ALU.add),
                        inc=(sDV if dt == 3 else None))
        barrier()
        load_win("b_wq")
        make_svec(WINB, lambda ct: cOn[:, 0:1], AUGA, 64)
        barrier()
        for c in range(NCH):
            t0 = c * CH
            for hc in range(2):
                th = t0 + hc * 256
                GP.wait(sDV)
                GP.dma(lambda eng, th=th: eng.dma_start(
                    out=ROPE.rearrange("p (dt j) -> p dt j", j=256),
                    in_=din["cCos"].rearrange("p (dt t) -> p dt t", t=T)[:, :, th:th + 256]), sGPD)
                GP.dma(lambda eng, th=th: eng.dma_start(
                    out=RTS.rearrange("p (dt j) -> p dt j", j=256),
                    in_=din["cSin"].rearrange("p (dt t) -> p dt t", t=T)[:, :, th:th + 256]), sGPD)
                qev = [0, 0]
                for dt in range(CT):
                    ps_ = PS[dt % 2][:, 0:256]
                    PE.wait(sDV)
                    if qev[dt % 2]:
                        PE.wait(sAC, qev[dt % 2])
                    for ct in range(CT):
                        mm(ps_, WINB[:, ct * 1024 + dt * 128: ct * 1024 + (dt + 1) * 128],
                           QTXS[:, ct * T + th: ct * T + th + 256],
                           start=(ct == 0), stop=False)
                    mm(ps_, AUGA[64:65, dt * 128:(dt + 1) * 128],
                       MRB[64:65, 1 + th:1 + th + 256], start=False, stop=True,
                       inc=sPE)
                    SC.wait(sPE)
                    SC.op(lambda eng, dt=dt, ps_=ps_: eng.activation(
                        out=QP8[:, dt * 256:(dt + 1) * 256], in_=ps_,
                        func=AF.Copy), inc=sAC)
                    qev[dt % 2] = sAC.n
                DV.wait(sAC)
                DV.wait(sGPD)
                for dt in range(4):
                    topo = QTXS[:, dt * T + th: dt * T + th + 256]
                    boto = QTXS[:, (dt + 4) * T + th:(dt + 4) * T + th + 256]
                    qpa = QP8[:, dt * 256:(dt + 1) * 256]
                    qpb = QP8[:, (dt + 4) * 256:(dt + 5) * 256]
                    cosb = ROPE[:, dt * 256:(dt + 1) * 256]
                    sinb = RTS[:, dt * 256:(dt + 1) * 256]
                    DV.op(lambda eng, qpa=qpa, cosb=cosb: eng.tensor_tensor(
                        out=TMPC, in0=qpa, in1=cosb, op=ALU.mult))
                    DV.op(lambda eng, qpb=qpb, sinb=sinb: eng.tensor_tensor(
                        out=TMPD, in0=qpb, in1=sinb, op=ALU.mult))
                    DV.op(lambda eng, topo=topo: eng.tensor_tensor(
                        out=topo, in0=TMPC, in1=TMPD, op=ALU.subtract))
                    DV.op(lambda eng, qpa=qpa, sinb=sinb: eng.tensor_tensor(
                        out=TMPC, in0=qpa, in1=sinb, op=ALU.mult))
                    DV.op(lambda eng, qpb=qpb, cosb=cosb: eng.tensor_tensor(
                        out=TMPD, in0=qpb, in1=cosb, op=ALU.mult))
                    DV.op(lambda eng, boto=boto: eng.tensor_tensor(
                        out=boto, in0=TMPC, in1=TMPD, op=ALU.add),
                        inc=(sDV if dt == 3 else None))
        barrier()
        if os.environ.get("KDUMP") == "qt":
            SY.dma(lambda eng: eng.dma_start(out=dbg, in_=QTXS), sIN)
            barrier()
        if os.environ.get("KDUMP") == "kt":
            SY.dma(lambda eng: eng.dma_start(out=dbg[:, 0:8192], in_=W0), sIN)
            SY.dma(lambda eng: eng.dma_start(out=dbg[:, 8192:16384], in_=W1), sIN)
            barrier()
        if os.environ.get("KDUMP") == "vn":
            for j in range(16):
                SY.dma(lambda eng, j=j: eng.dma_start(
                    out=dbg[:, j * 1024:(j + 1) * 1024], in_=VN[j][:, 0:1024]), sIN)
            barrier()
        load_win("b_out_w")
        barrier()
        for g in range(NCH):
            t1c = g * 512
            tr_pe = [0]
            ts_dv = [0]
            pending_norm = [None]
            for h in range(H):
                dtq = h // 2
                pbase = (h % 2) * 64
                qsl = QTXS[pbase:pbase + 64, dtq * T + t1c: dtq * T + t1c + 512]
                gran = ALIBI_GRAN[h]
                dmin = -(16 - gran // 128)
                nj = 4 * g + 4
                exp_ac = {}
                mask_dv = {}
                pv_arr = {}

                def emit_qk(j):
                    psc = PS[j % 2]
                    if j >= 2:
                        PE.wait(sAC, exp_ac[j - 2])
                    else:
                        PE.wait(sAC)
                    ksl = kTs(dtq)[pbase:pbase + 64, j * 128:(j + 1) * 128]
                    mm(psc, ksl, qsl, start=True, stop=True, inc=sPE)
                    return sPE.n

                def emit_exp(j, qk_v):
                    psc = PS[j % 2]
                    PT = PTD[j % 2]
                    SC.wait(sPE, qk_v)
                    if j >= 2:
                        SC.wait(sPE, pv_arr[j - 2])
                    if j - 1 in mask_dv:
                        SC.wait(sDV, mask_dv[j - 1])
                    nsub = 512 // gran
                    for s2 in range(nsub):
                        c0 = s2 * gran
                        c1 = c0 + gran
                        dlo = j * 128 - t1c
                        if c1 <= dlo:
                            SC.op(lambda eng, c0=c0, c1=c1, PT=PT: eng.memzero(
                                PT[:, c0:c1]))
                            continue
                        blk0 = (t1c + s2 * gran) // gran
                        didx = j - (gran // 128) * blk0 - dmin
                        bias = cBt[:, h * 16 + didx: h * 16 + didx + 1]
                        if c0 <= dlo < c1:
                            if dlo > c0:
                                SC.op(lambda eng, c0=c0, dlo=dlo, PT=PT:
                                      eng.memzero(PT[:, c0:dlo]))
                            SC.op(lambda eng, psc=psc, dlo=dlo, bias=bias:
                                  eng.activation(
                                      out=PSC, in_=psc[:, dlo:dlo + 128],
                                      func=AF.Exp, scale=0.125, bias=bias),
                                  inc=sAC)
                            DV.wait(sAC)
                            DV.op(lambda eng, dlo=dlo, PT=PT: eng.tensor_tensor(
                                out=PT[:, dlo:dlo + 128], in0=PSC, in1=cDg,
                                op=ALU.mult), inc=sDV)
                            if dlo + 128 < c1:
                                SC.op(lambda eng, psc=psc, dlo=dlo, c1=c1,
                                      bias=bias, PT=PT: eng.activation(
                                    out=PT[:, dlo + 128:c1],
                                    in_=psc[:, dlo + 128:c1],
                                    func=AF.Exp, scale=0.125, bias=bias))
                        elif c0 > dlo:
                            SC.op(lambda eng, psc=psc, c0=c0, c1=c1, bias=bias,
                                  PT=PT: eng.activation(
                                      out=PT[:, c0:c1], in_=psc[:, c0:c1],
                                      func=AF.Exp, scale=0.125, bias=bias))
                    exp_ac[j] = sAC.n
                    mask_dv[j] = sDV.n

                def emit_pv(j):
                    PT = PTD[j % 2]
                    if j == 0 and ts_dv[0]:
                        PE.wait(sDV, ts_dv[0])
                    PE.wait(sAC, exp_ac[j])
                    PE.wait(sDV, mask_dv[j])
                    for i2 in range(4):
                        mm(PS[6][:, i2 * 65:(i2 + 1) * 65],
                           PT[:, i2 * 128:(i2 + 1) * 128],
                           VN[j][:, 65 * h:65 * h + 65],
                           start=(j == 0 and i2 == 0),
                           stop=(j == nj - 1 and i2 == 3),
                           inc=(sPE if i2 == 3 else None))
                    pv_arr[j] = sPE.n

                for j in range(nj):
                    qkv = emit_qk(j)
                    emit_exp(j, qkv)
                    if j == 1 and pending_norm[0] is not None:
                        pending_norm[0]()
                        pending_norm[0] = None
                    if j >= 1:
                        emit_pv(j - 1)
                if pending_norm[0] is not None:
                    pending_norm[0]()
                    pending_norm[0] = None
                emit_pv(nj - 1)
                pv_pe = pv_arr[nj - 1]
                def make_norm(h=h, dtq=dtq, pbase=pbase,
                              pv_arr=pv_arr, nj=nj):
                    pv_pe = pv_arr[nj - 1]
                    DV.wait(sPE, pv_pe)
                    if ts_dv[0]:
                        DV.wait(sDV, ts_dv[0])
                    zcols = PS[6][:, 0:260].rearrange(
                        "p (i c) -> p i c", c=65)[:, :, 64:65]
                    DV.op(lambda eng, zcols=zcols, h=h: eng.tensor_tensor(
                        out=RCB[:, 0:4].unsqueeze(2), in0=zcols,
                        in1=cCV[:, h * 4:h * 4 + 4].unsqueeze(2), op=ALU.mult))
                    DV.op(lambda eng: eng.drain(fusable=False), inc=sDV)
                    SC.wait(sDV)
                    SC.op(lambda eng: eng.activation(
                        out=RCB[:, 4:8], in_=RCB[:, 0:4], func=AF.Ln))
                    SC.op(lambda eng: eng.activation(
                        out=RCB[:, 8:12], in_=RCB[:, 4:8], func=AF.Exp, scale=-1.0))
                    SC.op(lambda eng: eng.drain(fusable=False), inc=sAC)
                    DV.wait(sAC)
                    DV.op(lambda eng, h=h: eng.tensor_tensor(
                        out=RCB[:, 8:12], in0=RCB[:, 8:12],
                        in1=cCV[:, h * 4:h * 4 + 4], op=ALU.mult), inc=sDV)
                    DV.wait(sDV)
                    for i2 in range(4):
                        DV.op(lambda eng, i2=i2: eng.tensor_scalar(
                            out=STG64[:, i2 * 64:(i2 + 1) * 64],
                            in0=PS[6][:, i2 * 65: i2 * 65 + 64],
                            scalar1=RCB[:, 8 + i2:9 + i2], scalar2=None,
                            op0=ALU.mult), inc=(sDV if i2 == 3 else None))
                    if (os.environ.get("KDUMP") == "stg" and g == 0
                            and h == int(os.environ.get("KH", "2"))):
                        barrier()
                        SY.dma(lambda eng: eng.dma_start(
                            out=dbg[:, 0:256], in_=STG64), sIN)
                        SY.dma(lambda eng: eng.dma_start(
                            out=dbgf[:, 0:8], in_=RCB), sIN)
                        barrier()
                    PE.wait(sDV)
                    for i2 in range(4):
                        PE.op(lambda eng, i2=i2: eng.transpose(
                            PST[0:64, i2 * 128:(i2 + 1) * 128],
                            STG64[:, i2 * 64:(i2 + 1) * 64], cIdB),
                            inc=(sPE if i2 == 3 else None))
                    tr_pe[0] = sPE.n
                    SC.wait(sPE)
                    SC.op(lambda eng, pbase=pbase, dtq=dtq: eng.activation(
                        out=OTB[pbase:pbase + 64, dtq * 512:(dtq + 1) * 512],
                        in_=PST[0:64, 0:512], func=AF.Copy), inc=sAC)
                    PE.wait(sAC)
                pending_norm[0] = make_norm

            if pending_norm[0] is not None:
                pending_norm[0]()
                pending_norm[0] = None
            if g == 0 and os.environ.get("KDUMP") == "otb":
                barrier()
                SY.dma(lambda eng: eng.dma_start(out=dbg[:, 0:4096], in_=OTB), sIN)
                barrier()
            op_res = [0, 0]
            for dp in range(CT):
                pso = PS[2 + dp % 2]
                if dp >= 2:
                    PE.wait(sDV, op_res[dp % 2])
                for ct in range(CT):
                    mm(pso, WINB[:, ct * 1024 + dp * 128: ct * 1024 + (dp + 1) * 128],
                       OTB[:, ct * 512:(ct + 1) * 512],
                       start=(ct == 0), stop=False)
                mm(pso, AUGB[0:1, dp * 128:(dp + 1) * 128], cOn[0:1, 0:512],
                   start=False, stop=True, inc=sPE)
                DV.wait(sPE)
                DV.op(lambda eng, dp=dp, t1c=t1c, pso=pso: eng.tensor_tensor(
                    out=X[:, dp * T + t1c: dp * T + t1c + 512],
                    in0=X[:, dp * T + t1c: dp * T + t1c + 512], in1=pso,
                    op=ALU.add), inc=sDV)
                op_res[dp % 2] = sDV.n
        barrier()

    if n_sublayers >= 3:
        attn_b()
    if n_sublayers >= 4:
        ffn(reload=True)

    barrier()
    for tt in range(16):
        par = tt % 2
        sS = sSA if par == 0 else sSB
        if tt >= 2:
            PE.wait(sS, 16 * (tt // 2))
        PE.wait(sAC)
        for hf in range(2):
            bank = PS[hf]
            for cq in range(4):
                cb = hf * 4 + cq
                PE.op(lambda eng, cb=cb, cq=cq, tt=tt, bank=bank: eng.transpose(
                    bank[:, cq * 128:(cq + 1) * 128],
                    X[:, cb * T + tt * 128: cb * T + (tt + 1) * 128],
                    cIdF), inc=(sPE if cq == 3 else None))
            SC.wait(sPE)
            SC.op(lambda eng, hf=hf, par=par, bank=bank: cpy(eng,
                OST[:, par * 1024 + hf * 512: par * 1024 + hf * 512 + 512],
                bank), inc=sAC)
        SY.wait(sAC)
        SY.dma(lambda eng, tt=tt, par=par: eng.dma_start(
            out=out[tt * 128:(tt + 1) * 128, :],
            in_=OST[:, par * 1024:(par + 1) * 1024]), sS)
    SY.wait(sSA)
    SY.wait(sSB)

    def body(qname):
        def run(eng):
            for item in q[qname].items:
                if item[0] == "wait":
                    eng.wait_ge(item[1].h, item[2])
                else:
                    inst = item[1](eng)
                    for s, n in item[2]:
                        inst.then_inc(s.h, n)
        return run
    blk.sync(body("sync"))
    blk.gpsimd(body("gpsimd"))
    blk.vector(body("vector"))
    blk.scalar(body("scalar"))
    blk.tensor(body("tensor"))
    blk_ctx.__exit__(None, None, None)
    for ctx in sem_ctx:
        ctx.__exit__(None, None, None)
    return nc


_consts_cache = None

def kernel(L0_attn_a_x, L0_attn_a_time_decay, L0_attn_a_time_first,
           L0_attn_a_time_mix_k, L0_attn_a_time_mix_v, L0_attn_a_time_mix_r,
           L0_attn_a_Wk, L0_attn_a_Wv, L0_attn_a_Wr, L0_attn_a_Wo,
           L0_attn_b_wq, L0_attn_b_wk, L0_attn_b_wv, L0_attn_b_out_w,
           L0_attn_b_out_b, L0_ffn_G1_R, L0_ffn_G2_R, L0_ffn_G1_I, L0_ffn_G2_I,
           L0_ffn_bias_real, L0_ffn_bias_imag, _sublayers=4):
    global _consts_cache
    if _consts_cache is None:
        _consts_cache = host_constants()
    f32 = np.float32
    x = np.ascontiguousarray(np.asarray(L0_attn_a_x, f32))
    par = np.ascontiguousarray(np.stack([
        np.asarray(L0_attn_a_time_decay, f32).reshape(8, 128),
        np.asarray(L0_attn_a_time_first, f32).reshape(8, 128),
        np.asarray(L0_attn_a_time_mix_k, f32).reshape(8, 128),
        np.asarray(L0_attn_a_time_mix_v, f32).reshape(8, 128),
        np.asarray(L0_attn_a_time_mix_r, f32).reshape(8, 128)],
        axis=0).reshape(40, 128))
    g2r = np.ascontiguousarray(np.asarray(L0_ffn_G2_R, f32).reshape(4, 16, 16)
                               .transpose(1, 0, 2).reshape(16, 64))
    g2i = np.ascontiguousarray(np.asarray(L0_ffn_G2_I, f32).reshape(4, 16, 16)
                               .transpose(1, 0, 2).reshape(16, 64))
    base = {
        "a_Wk": np.ascontiguousarray(np.asarray(L0_attn_a_Wk, f32)),
        "a_Wv": np.ascontiguousarray(np.asarray(L0_attn_a_Wv, f32)),
        "a_Wr": np.ascontiguousarray(np.asarray(L0_attn_a_Wr, f32)),
        "a_Wo": np.ascontiguousarray(np.asarray(L0_attn_a_Wo, f32)),
        "b_wq": np.ascontiguousarray(np.asarray(L0_attn_b_wq, f32)),
        "b_wk": np.ascontiguousarray(np.asarray(L0_attn_b_wk, f32)),
        "b_wv": np.ascontiguousarray(np.asarray(L0_attn_b_wv, f32)),
        "b_out_w": np.ascontiguousarray(np.asarray(L0_attn_b_out_w, f32)),
        "a_par": par,
        "b_out_b": np.ascontiguousarray(np.asarray(L0_attn_b_out_b, f32))[None, :],
        "f_G1R": np.ascontiguousarray(np.asarray(L0_ffn_G1_R, f32).reshape(32, 128)),
        "f_G2R": g2r,
        "f_G1I": np.ascontiguousarray(np.asarray(L0_ffn_G1_I, f32).reshape(32, 128)),
        "f_G2I": g2i,
        "f_bias": np.concatenate([np.asarray(L0_ffn_bias_real, f32),
                                  np.asarray(L0_ffn_bias_imag, f32)])[None, :],
    }
    base.update(_consts_cache)
    nc = build(n_sublayers=_sublayers)
    in_maps = []
    for b in range(8):
        m = dict(base)
        m["xin"] = np.ascontiguousarray(x[b])
        in_maps.append(m)
    res = run_bass_kernel_spmd(nc, in_maps, core_ids=list(range(8)))
    return np.stack([np.asarray(res.results[b]["out"]) for b in range(8)], axis=0)


# version 08 (final)
# speedup vs baseline: 1.7112x; 1.0117x over previous
import numpy as np
import ml_dtypes
import concourse.bass as bass
import concourse.mybir as mybir
import bass_rust
from concourse.bass_utils import run_bass_kernel_spmd

F32 = mybir.dt.float32
BF16 = mybir.dt.bfloat16
AF = mybir.ActivationFunctionType
ALU = mybir.AluOpType
BF_NP = ml_dtypes.bfloat16

T = 2048
C = 1024
CT = 8
CH = 512
NCH = T // CH
H = 16
SHIFT = 64.0
EPS = 1e-5
ALIBI_GRAN = [128 if h < 2 else (256 if h < 4 else 512) for h in range(H)]


def host_constants():
    c = {}
    c["cJ"] = np.repeat((np.arange(CH, dtype=np.float32) + 1.0)[None, :], 128, 0)
    c["cIdentF"] = np.eye(128, dtype=np.float32)
    c["cIdentB"] = np.eye(128, dtype=BF_NP)
    c["cDiag"] = (np.arange(128)[None, :] >= np.arange(128)[:, None]).astype(BF_NP)
    c["cOnes"] = np.ones((128, 512), dtype=BF_NP)
    slopes = np.array([2.0 ** (-8.0 * (i + 1) / H) for i in range(H)], np.float64)
    bt = np.zeros((128, 256), np.float32)
    p = np.arange(128, dtype=np.float64)
    for h in range(H):
        gran = ALIBI_GRAN[h]
        dmin = -(16 - gran // 128)
        for idx in range(16):
            bt[:, h * 16 + idx] = (slopes[h] * p + slopes[h] * 128.0 * (dmin + idx)
                                   - SHIFT).astype(np.float32)
    c["cBias"] = bt
    cv = np.zeros((128, 64), np.float32)
    pq = np.arange(128, dtype=np.float64)
    for h in range(H):
        gran = ALIBI_GRAN[h]
        for i2 in range(4):
            fp = (128.0 * i2 + pq) % gran
            cv[:, h * 4 + i2] = np.exp(64.0 - slopes[h] * fp).astype(np.float32)
    c["cCVT"] = cv
    p2 = np.zeros((32, 512), BF_NP)
    for b in range(32):
        p2[b, b * 16:(b + 1) * 16] = 1
    c["cFP2"] = p2
    p2b = np.zeros((16, 512), BF_NP)
    for e in range(16):
        p2b[e, e::16] = 1
    c["cFP2B"] = p2b
    fsd = np.zeros((32, 512), BF_NP)
    for p in range(512):
        fsd[p // 16, p] = 1
    c["cFSD"] = fsd
    fsg = np.zeros((16, 512), BF_NP)
    for p in range(512):
        fsg[p % 16, p] = 1
    c["cFSG"] = fsg
    half = 512
    freqs = 1.0 / 10000.0 ** (np.arange(half, dtype=np.float32) / half)
    ang = np.arange(T, dtype=np.float32)[None, :] * freqs[:, None]
    c["cCos"] = np.cos(ang).astype(np.float32).reshape(4, 128, T).transpose(
        1, 0, 2).reshape(128, 4 * T).astype(BF_NP)
    c["cSin"] = np.sin(ang).astype(np.float32).reshape(4, 128, T).transpose(
        1, 0, 2).reshape(128, 4 * T).astype(BF_NP)
    return c


class Sem:
    def __init__(self, handle, name):
        self.h = handle
        self.name = name
        self.n = 0


class EQ:
    def __init__(self, name):
        self.name = name
        self.items = []
        self.auto_edge = False
        self.esem = None

    def wait(self, sem, val=None):
        v = sem.n if val is None else val
        if v > 0:
            self.items.append(("wait", sem, v))

    def op(self, fn, inc=None, inc_n=1):
        incs = []
        if inc is not None:
            inc.n += inc_n
            incs.append((inc, inc_n))
        if self.auto_edge and inc is not self.esem:
            self.esem.n += 1
            incs.append((self.esem, 1))
        self.items.append(("op", fn, incs))
        return incs

    def dma(self, fn, sem):
        return self.op(fn, inc=sem, inc_n=16)

    def opE(self, fn, esem):
        self.op(fn, inc=esem)
        self.wait(esem)

    def last_op_incs(self):
        for it in reversed(self.items):
            if it[0] == "op":
                return it[2]
        return None


def build(n_sublayers=4):
    nc = bass.Bass("TRN2", target_bir_lowering=False)
    import os as _os
    if not int(_os.environ.get("KRACE", "0")):
        nc.detect_race_conditions = False

    din = {}
    def dram_in(name, shape, dtype=F32):
        din[name] = nc.dram_tensor(name, shape, dtype, kind="ExternalInput").ap()
        return din[name]

    xin = dram_in("xin", [T, C])
    for nm in ["a_Wk", "a_Wv", "a_Wr", "a_Wo", "b_wq", "b_wk", "b_wv", "b_out_w"]:
        dram_in(nm, [C, C])
    dram_in("a_par", [40, 128])
    dram_in("b_out_b", [1, C])
    dram_in("f_G1R", [32, 128])
    dram_in("f_G2R", [16, 64])
    dram_in("f_G1I", [32, 128])
    dram_in("f_G2I", [16, 64])
    dram_in("f_bias", [1, C])
    dram_in("cJ", [128, CH])
    dram_in("cIdentF", [128, 128])
    dram_in("cIdentB", [128, 128], BF16)
    dram_in("cDiag", [128, 128], BF16)
    dram_in("cOnes", [128, 512], BF16)
    dram_in("cBias", [128, 256])
    dram_in("cCVT", [128, 64])
    dram_in("cCos", [128, 4 * T], BF16)
    dram_in("cFP2", [32, 512], BF16)
    dram_in("cFP2B", [16, 512], BF16)
    dram_in("cFSD", [32, 512], BF16)
    dram_in("cFSG", [16, 512], BF16)
    dram_in("cSin", [128, 4 * T], BF16)
    out = nc.dram_tensor("out", [T, C], F32, kind="ExternalOutput").ap()
    dbg = nc.dram_tensor("dbg", [128, 16384], BF16, kind="ExternalOutput").ap()
    dbgf = nc.dram_tensor("dbgf", [128, 512], F32, kind="ExternalOutput").ap()
    wstash = nc.dram_tensor("wstash", [128, 8192], BF16, kind="Internal").ap()

    tens = {}
    def sb(name, shape, dtype, offset):
        t = nc.alloc_sbuf_tensor_at(name, shape, dtype, offset=offset)
        tens[name] = t.ap()
        return tens[name]

    SB_BASE = 16512
    OFF = SB_BASE
    def place(name, shape, dtype):
        nonlocal OFF
        szb = {F32: 4, BF16: 2}[dtype]
        w = szb
        for d in shape[1:]:
            w *= d
        at = OFF
        OFF = at + ((w + 63) // 64) * 64
        return sb(name, shape, dtype, at), at

    X, _ = place("X", [128, CT * T], F32)
    cJ, _ = place("sJ", [128, CH], F32)
    cIdF, _ = place("sIdF", [128, 128], F32)
    cIdB, _ = place("sIdB", [128, 128], BF16)
    cDg, _ = place("sDiag", [128, 128], BF16)
    cOn, _ = place("sOnes", [128, 512], BF16)
    cBt, _ = place("sBias", [128, 256], F32)
    cCV, _ = place("sCVT", [128, 64], F32)
    PAR, _ = place("PAR", [128, 32], F32)
    PARB, _ = place("PARB", [128, 48], BF16)
    PARF, _ = place("PARF", [128, 48], F32)
    AUGA, _ = place("AUGA", [128, 1024], BF16)
    AUGB, _ = place("AUGB", [128, 1024], BF16)
    AUGC, _ = place("AUGC", [128, 1024], BF16)
    EPSB, _ = place("EPSB", [128, 1], F32)
    PRAW, _ = place("PRAW", [128, 40], F32)
    PPB, _ = place("PPB", [128, 8], F32)
    AAC, _ = place("AAC", [128, 8], F32)
    BBC, _ = place("BBC", [128, 8], F32)
    MRB, MRB_at = place("MRB", [128, T + 1], BF16)
    RSTD, RSTD_at = place("RSTD", [128, T], F32)
    XS, XS_at = place("XS", [128, CT * (CH + 1)], BF16)
    W0, W0_at = place("W0", [128, 8192], BF16)
    W1, W1_at = place("W1", [128, 8192], BF16)
    W2, W2_at = place("W2", [128, 8192], BF16)
    W3, W3_at = place("W3", [128, 8192], BF16)
    VAR_at = OFF
    SB_LIMIT = 229312
    assert VAR_at + 47000 <= SB_LIMIT, VAR_at

    V = VAR_at
    XSTG = sb("XSTG", [128, 3072], F32, V)
    WNAT = sb("WNAT", [128, 8192], BF16, V + 12288)
    WNATB = sb("WNATB", [128, 8192], BF16, V + 28672)
    XBF = sb("XBF", [128, CT * CH], BF16, V)
    XSQ = sb("XSQ", [128, CT * CH], BF16, V + 8192)
    MST = sb("MST", [128, CH], F32, V + 16384)
    VST = sb("VST", [128, CH], F32, V + 18432)
    VST2 = sb("VST2", [128, CH], F32, V + 20480)
    MIXA = sb("MIXA", [128, CT * CH], BF16, V)
    MIXB = sb("MIXB", [128, CT * CH], BF16, V + 8192)
    YG = sb("YG", [128, CT * CH], BF16, V + 16384)
    ZP = [sb(f"ZP{i}", [128, CH], F32, V + 24576 + i * 2048) for i in range(5)]
    CB = sb("CB", [128, CH + 1], F32, V + 34816)
    ABF = sb("ABF", [128, CH + 1], F32, V + 36992)
    BBF = sb("BBF", [128, CH + 1], F32, V + 39168)
    TMPA = sb("TMPA", [128, CH], BF16, V + 41344)
    TMPB = sb("TMPB", [128, CH], BF16, V + 42368)
    TMPG1 = sb("TMPG1", [128, CH], BF16, V + 43392)
    TMPG2 = sb("TMPG2", [128, CH], BF16, V + 44416)
    assert V + 45440 <= SB_LIMIT, V
    FO1 = sb("FO1", [128, 512], BF16, V)
    FEA = sb("FEA", [128, 512], F32, V + 1024)
    FG1R = sb("FG1R", [128, 128], BF16, V + 3072)
    FG2R = sb("FG2R", [128, 64], BF16, V + 3328)
    FG1I = sb("FG1I", [128, 128], BF16, V + 3456)
    FG2I = sb("FG2I", [128, 64], BF16, V + 3712)
    FP2 = sb("FP2", [128, 512], BF16, V + 3840)
    FP2B = sb("FP2B", [128, 512], BF16, V + 4864)
    FSD = sb("FSD", [128, 512], BF16, V + 5888)
    FSG = sb("FSG", [128, 512], BF16, V + 6912)
    FWT = [sb(f"FWT{i}", [128, 512], F32, V + 7936 + 2048 * i) for i in range(8)]
    QTXS = sb("QTXS", [128, CT * T], BF16, V)
    WINB = sb("WINB", [128, 8192], BF16, RSTD_at)
    VN = []
    for j in range(16):
        if j < 7:
            at = W2_at + j * 2112
        elif j < 14:
            at = W3_at + (j - 7) * 2112
        else:
            at = V + 32768 + (j - 14) * 2112
        VN.append(sb(f"VN{j}", [128, 1040], BF16, at))
    BW2 = V + 32768 + 4608
    ROPE = sb("ROPEB", [128, 1024], BF16, BW2)
    RTS = sb("RTS", [128, 1024], BF16, BW2 + 2048)
    QP8 = sb("QP8", [128, 2048], BF16, BW2 + 4096)
    TMPC = sb("TMPC", [128, 256], BF16, BW2 + 8192)
    TMPD = sb("TMPD", [128, 256], BF16, BW2 + 8704)
    OTB = sb("OTB", [128, CT * 512], BF16, BW2)
    WNAT2 = sb("WNAT2", [128, 4096], BF16, BW2)
    FO2 = sb("FO2", [128, 512], BF16, BW2 + 2048)
    PT0 = sb("PT0", [128, 512], BF16, MRB_at)
    PT1 = sb("PT1", [128, 512], BF16, MRB_at + 1024)
    PTD = [PT0, PT1]
    PSC = sb("PSC", [128, 128], BF16, MRB_at + 2048)
    STG64 = sb("STG64", [128, 256], BF16, MRB_at + 2304)
    RCB = sb("RCB", [128, 12], F32, MRB_at + 2816)
    LNZ = sb("LNZ", [128, 256], F32, MRB_at + 3072)
    OST = sb("OST", [128, 2 * 1024], F32, V)

    PS = [nc.alloc_psum_tensor(f"ps{i}", [128, 512], F32).ap() for i in range(7)]
    PST = nc.alloc_psum_tensor("pst", [128, 1024], BF16).ap()

    sem_names = ["sIN", "sGPD", "sPE", "sDV", "sAC", "sGE", "sSY", "sST",
                 "sCN", "sXA", "sXB", "sSA", "sSB"]
    sem_ctx = [nc.semaphore(n) for n in sem_names]
    handles = [ctx.__enter__() for ctx in sem_ctx]
    sems = {n: Sem(h, n) for n, h in zip(sem_names, handles)}
    sIN, sGPD, sPE, sDV, sAC, sGE, sSY, sST, sCN, sXA, sXB, sSA, sSB = (
        sems[n] for n in sem_names)
    all_sems = list(sems.values())
    esem = {"sync": sSY, "gpsimd": sGE, "vector": sDV, "scalar": sAC, "tensor": sPE}
    q = {n: EQ(n) for n in ("sync", "gpsimd", "vector", "scalar", "tensor")}
    for n in ("vector", "scalar"):
        q[n].auto_edge = True
        q[n].esem = esem[n]
    SY, GP, DV, SC, PE = (q[n] for n in ("sync", "gpsimd", "vector", "scalar", "tensor"))

    def barrier():
        for en, qq in q.items():
            incs = qq.last_op_incs()
            if incs is not None and len(incs) == 0:
                esem[en].n += 1
                incs.append((esem[en], 1))
        snap = [(s, s.n) for s in all_sems if s.n > 0]
        for qq in q.values():
            for s, v in snap:
                qq.wait(s, v)

    blk_ctx = nc.Block()
    blk = blk_ctx.__enter__()

    def cpy(eng, out, in_):
        if hasattr(eng, "tensor_copy"):
            return eng.tensor_copy(out=out, in_=in_)
        return eng.copy(out=out, in_=in_)

    def mm(psum, lhsT, rhs, start, stop, inc=None):
        PE.op(lambda eng, a=psum, b=lhsT, c=rhs, s1=start, s2=stop:
              eng.matmul(a, b, c, start=s1, stop=s2), inc=inc)

    def wsl(w, ct, dblk):
        return w[:, ct * 1024 + dblk * 128: ct * 1024 + dblk * 128 + 128]

    def proj_group(psum, wt, i, rhs_fn, aug1, aug2, inc=None):
        for ct in range(CT):
            mm(psum, wsl(wt, ct, i), rhs_fn(ct), start=(ct == 0), stop=False)
            if ct == 3 and aug1 is not None:
                mm(psum, aug1[0], aug1[1], start=False, stop=False)
        last = aug2 if aug2 is not None else aug1
        if last is None:
            raise ValueError
        mm(psum, last[0], last[1], start=False, stop=True, inc=inc)

    Xv = X.rearrange("p (ct t) -> p ct t", t=T)
    XSv = XS.rearrange("p (ct j) -> p ct j", j=CH + 1)
    def par_w(i): return PAR[:, i:i + 1]
    def par_upw(i): return PAR[:, 8 + i:9 + i]
    def par_cl(i): return PAR[:, 16 + i:17 + i]
    def par_w512(i): return PAR[:, 24 + i:25 + i]
    def parb(m, ct): return PARB[:, m * 8 + ct: m * 8 + ct + 1]
    def parb1m(m, ct): return PARB[:, 24 + m * 8 + ct: 25 + m * 8 + ct]
    def parf(m, ct): return PARF[:, m * 8 + ct: m * 8 + ct + 1]
    def parf1m(m, ct): return PARF[:, 24 + m * 8 + ct: 25 + m * 8 + ct]

    for dst, src in [(cJ, din["cJ"]), (cIdF, din["cIdentF"]), (cIdB, din["cIdentB"]),
                     (cDg, din["cDiag"]), (cOn, din["cOnes"]), (cBt, din["cBias"]),
                     (cCV, din["cCVT"])]:
        SY.dma(lambda eng, d=dst, s=src: eng.dma_start(out=d, in_=s), sCN)

    DV.op(lambda eng: eng.memset(EPSB, EPS), inc=sDV)
    xin_t = xin.rearrange("(tt p) c -> tt p c", p=128)
    PE.wait(sCN)
    for tt in range(16):
        par = tt % 2
        sX = sXA if par == 0 else sXB
        if tt >= 2:
            SY.wait(sAC)
        SY.dma(lambda eng, tt=tt, par=par: eng.dma_start(
            out=XSTG[:, par * 1024:(par + 1) * 1024], in_=xin_t[tt]), sX)
        PE.wait(sX)
        PE.wait(sAC)
        for cb in range(8):
            bank = PS[cb // 4]
            PE.op(lambda eng, cb=cb, par=par, bank=bank: eng.transpose(
                bank[:, (cb % 4) * 128:(cb % 4 + 1) * 128],
                XSTG[:, par * 1024 + cb * 128: par * 1024 + (cb + 1) * 128],
                cIdF), inc=(sPE if cb == 7 else None))
        SC.wait(sPE)
        for hf in range(2):
            dst = Xv[:, hf * 4:(hf + 1) * 4, tt * 128:(tt + 1) * 128]
            SC.op(lambda eng, hf=hf, dst=dst: cpy(eng,
                dst, PS[hf].rearrange("p (a r) -> p a r", a=4)),
                inc=(sAC if hf == 1 else None))

    SY.wait(sAC)
    SY.dma(lambda eng: eng.dma_start(out=XSTG[0:40, 0:128], in_=din["a_par"]), sIN)
    PE.wait(sIN)
    PE.op(lambda eng: eng.transpose(PS[2][:, 0:128], XSTG[:, 0:128], cIdF), inc=sPE)
    SC.wait(sPE)
    SC.op(lambda eng: cpy(eng, PRAW, PS[2][:, 0:40]), inc=sAC)
    SC.op(lambda eng: eng.activation(
        out=PAR[:, 0:8], in_=PRAW[:, 0:8], func=AF.Exp))
    SC.op(lambda eng: eng.drain(fusable=False), inc=sAC)
    DV.wait(sAC)
    DV.opE(lambda eng: eng.tensor_tensor(
        out=PAR[:, 8:16], in0=PRAW[:, 8:16], in1=PAR[:, 0:8], op=ALU.subtract), sDV)
    DV.opE(lambda eng: eng.tensor_scalar(
        out=PAR[:, 16:24], in0=PAR[:, 8:16], scalar1=43.0, scalar2=None,
        op0=ALU.subtract), sDV)
    DV.opE(lambda eng: eng.tensor_scalar(
        out=PAR[:, 24:32], in0=PAR[:, 0:8], scalar1=512.0, scalar2=None,
        op0=ALU.mult), sDV)
    DV.opE(lambda eng: eng.tensor_copy(out=PARB[:, 0:24], in_=PRAW[:, 16:40]), sDV)
    DV.opE(lambda eng: eng.tensor_scalar(
        out=PARB[:, 24:48], in0=PRAW[:, 16:40],
        scalar1=-1.0, scalar2=1.0, op0=ALU.mult, op1=ALU.add), sDV)
    DV.op(lambda eng: eng.tensor_copy(out=PARF, in_=PARB), inc=sDV)

    _wn = [0]
    _wn_pe = {0: 0, 1: 0}

    def prep_weight(src_name, slot):
        par = _wn[0] % 2
        nat = WNAT if par == 0 else WNATB
        _wn[0] += 1
        if _wn_pe[par]:
            GP.wait(sPE, _wn_pe[par])
        GP.dma(lambda eng, s=src_name, nat=nat: eng.dma_start(
            out=nat.rearrange("p (r c) -> p r c", c=1024),
            in_=din[s].rearrange("(r p) c -> p r c", p=128)), sGPD)
        PE.wait(sGPD)
        slot_v = slot.rearrange("p (ct d) -> p ct d", d=1024)
        for r in range(8):
            PE.wait(sDV)
            PE.wait(sAC)
            for dblk in range(8):
                PE.op(lambda eng, r=r, dblk=dblk, nat=nat: eng.transpose(
                    PST[:, dblk * 128:(dblk + 1) * 128],
                    nat[:, r * 1024 + dblk * 128: r * 1024 + (dblk + 1) * 128],
                    cIdB), inc=(sPE if dblk == 7 else None))
            evq, evsem = (DV, sDV) if (r % 2 == 0) else (SC, sAC)
            evq.wait(sPE)
            dst = slot_v[:, :, r * 128:(r + 1) * 128]
            evq.op(lambda eng, dst=dst: cpy(
                eng, dst, PST.rearrange("p (a r2) -> p a r2", a=8)), inc=evsem)
        _wn_pe[par] = sPE.n

    def make_svec(slot_w, lhs_fn, dst_aug, dst_row, scale=-1.0):
        for hf in range(2):
            for ct in range(CT):
                mm(PS[6][0:1, 0:512], lhs_fn(ct),
                   slot_w[:, ct * 1024 + hf * 512: ct * 1024 + hf * 512 + 512],
                   start=(ct == 0), stop=(ct == CT - 1),
                   inc=(sPE if ct == CT - 1 else None))
            SC.wait(sPE)
            SC.op(lambda eng, hf=hf, dst_aug=dst_aug, dst_row=dst_row, scale=scale:
                  eng.activation(
                      out=dst_aug[dst_row:dst_row + 1, hf * 512: hf * 512 + 512],
                      in_=PS[6][0:1, 0:512], func=AF.Copy, scale=scale), inc=sAC)
            PE.wait(sAC)

    prep_weight("a_Wk", W0)
    prep_weight("a_Wv", W1)
    prep_weight("a_Wr", W2)
    prep_weight("a_Wo", W3)
    barrier()
    make_svec(W0, lambda ct: parb(0, ct), AUGA, 0)
    make_svec(W0, lambda ct: parb1m(0, ct), AUGA, 32)
    make_svec(W1, lambda ct: parb(1, ct), AUGA, 64)
    make_svec(W1, lambda ct: parb1m(1, ct), AUGB, 0)
    make_svec(W2, lambda ct: parb(2, ct), AUGB, 32)
    make_svec(W2, lambda ct: parb1m(2, ct), AUGB, 64)

    def ln_stats():
        barrier()
        DV.opE(lambda eng: eng.memset(MRB[:, 0:1], 0.0), sDV)
        for c in range(NCH):
            t0 = c * CH
            GP.wait(sAC)
            GP.dma(lambda eng, t0=t0: eng.dma_start(
                out=XBF.rearrange("p (ct j) -> p ct j", j=CH),
                in_=Xv[:, :, t0:t0 + CH]), sGPD)
            SC.wait(sGPD)
            SC.wait(sPE)
            SC.op(lambda eng: eng.activation(out=XSQ, in_=XBF, func=AF.Square),
                  inc=sAC)
            PE.wait(sGPD)
            PE.wait(sDV)
            for ct in range(CT):
                mm(PS[0], cOn[:, 0:128], XBF[:, ct * CH:(ct + 1) * CH],
                   start=(ct == 0), stop=(ct == CT - 1))
            PE.wait(sAC)
            for ct in range(CT):
                mm(PS[1], cOn[:, 0:128], XSQ[:, ct * CH:(ct + 1) * CH],
                   start=(ct == 0), stop=(ct == CT - 1),
                   inc=(sPE if ct == CT - 1 else None))
            DV.wait(sPE)
            DV.op(lambda eng: eng.tensor_scalar(
                out=MST, in0=PS[0], scalar1=1.0 / C, scalar2=None, op0=ALU.mult))
            DV.op(lambda eng: eng.tensor_tensor(
                out=VST, in0=MST, in1=MST, op=ALU.mult))
            DV.op(lambda eng: eng.scalar_tensor_tensor(
                out=VST, in0=PS[1], scalar=1.0 / C, in1=VST,
                op0=ALU.mult, op1=ALU.subtract), inc=sDV)
            SC.wait(sDV)
            SC.op(lambda eng: eng.activation(
                out=VST2, in_=VST, func=AF.Sqrt, bias=EPSB), inc=sAC)
            DV.wait(sAC)
            DV.op(lambda eng, t0=t0: eng.reciprocal(
                out=RSTD[:, t0:t0 + CH], in_=VST2))
            DV.op(lambda eng, t0=t0: eng.tensor_tensor(
                out=MRB[:, 1 + t0:1 + t0 + CH], in0=MST,
                in1=RSTD[:, t0:t0 + CH], op=ALU.mult), inc=sDV)
        barrier()

    def make_xs(c, q_=None, sem=None):
        q_ = q_ or DV
        sem = sem or sDV
        t0 = c * CH
        if c == 0:
            q_.opE(lambda eng: eng.memset(XSv[:, :, 0:1], 0.0), sem)
        else:
            q_.opE(lambda eng: eng.tensor_copy(
                out=XSv[:, :, 0:1], in_=XSv[:, :, CH:CH + 1]), sem)
        for ct in range(CT):
            q_.op(lambda eng, ct=ct, t0=t0: eng.tensor_tensor(
                out=XS[:, ct * (CH + 1) + 1: ct * (CH + 1) + 1 + CH],
                in0=X[:, ct * T + t0: ct * T + t0 + CH],
                in1=RSTD[:, t0:t0 + CH], op=ALU.mult),
                inc=(sem if ct == CT - 1 else None))

    def make_mix(m, dstbuf):
        for ct in range(CT):
            DV.op(lambda eng, ct=ct, m=m: eng.tensor_scalar(
                out=TMPG1, in0=XS[:, ct * (CH + 1) + 1: ct * (CH + 1) + 1 + CH],
                scalar1=parf(m, ct), scalar2=None, op0=ALU.mult))
            DV.op(lambda eng, ct=ct, m=m: eng.tensor_scalar(
                out=TMPG2, in0=XS[:, ct * (CH + 1): ct * (CH + 1) + CH],
                scalar1=parf1m(m, ct), scalar2=None, op0=ALU.mult))
            DV.op(lambda eng, ct=ct, dstbuf=dstbuf: eng.tensor_tensor(
                out=dstbuf[:, ct * CH:(ct + 1) * CH], in0=TMPG1, in1=TMPG2,
                op=ALU.add), inc=(sDV if ct == CT - 1 else None))

    import os
    DBG = int(os.environ.get("KDBG", "9"))
    KAUG = int(os.environ.get("KAUG", "1"))

    def attn_a():
        ln_stats()
        if DBG < 2:
            barrier()
            return
        DV.opE(lambda eng: eng.memset(PPB, -1e38), sDV)
        DV.opE(lambda eng: eng.memset(AAC, 0.0), sDV)
        DV.opE(lambda eng: eng.memset(BBC, 0.0), sDV)
        for c in range(NCH):
            t0 = c * CH
            DV.wait(sPE)
            make_xs(c)
            make_mix(2, MIXA)
            mix_r = sDV.n
            if DBG < 3:
                continue
            sig_done = []
            for i in range(CT):
                pr = PS[i % 2]
                PE.wait(sDV, mix_r)
                if i >= 2:
                    PE.wait(sAC, sig_done[i - 2])
                proj_group(pr, W2, i,
                           lambda ct: MIXA[:, ct * CH:(ct + 1) * CH],
                           (AUGB[32:33, i * 128:(i + 1) * 128],
                            MRB[32:33, 1 + t0:1 + t0 + CH]),
                           (AUGB[64:65, i * 128:(i + 1) * 128],
                            MRB[64:65, t0:t0 + CH]), inc=sPE)
                SC.wait(sPE)
                SC.op(lambda eng, i=i, pr=pr: eng.activation(
                    out=YG[:, i * CH:(i + 1) * CH], in_=pr, func=AF.Sigmoid),
                    inc=sAC)
                sig_done.append(sAC.n)
            r_all = sPE.n
            if DBG < 4:
                continue
            DV.wait(sPE, r_all)
            make_mix(0, MIXA)
            make_mix(1, MIXB)
            mix_kv = sDV.n
            wkv_end = [0, 0]
            for i in range(CT):
                pk, pv = PS[2 + (i % 2) * 2], PS[3 + (i % 2) * 2]
                PE.wait(sDV, mix_kv)
                if i >= 2:
                    PE.wait(sDV, wkv_end[i % 2])
                proj_group(pk, W0, i,
                           lambda ct: MIXA[:, ct * CH:(ct + 1) * CH],
                           (AUGA[0:1, i * 128:(i + 1) * 128],
                            MRB[0:1, 1 + t0:1 + t0 + CH]),
                           (AUGA[32:33, i * 128:(i + 1) * 128],
                            MRB[32:33, t0:t0 + CH]))
                proj_group(pv, W1, i,
                           lambda ct: MIXB[:, ct * CH:(ct + 1) * CH],
                           (AUGA[64:65, i * 128:(i + 1) * 128],
                            MRB[64:65, 1 + t0:1 + t0 + CH]),
                           (AUGB[0:1, i * 128:(i + 1) * 128],
                            MRB[0:1, t0:t0 + CH]), inc=sPE)
                kv_done = sPE.n
                if DBG < 5:
                    continue

                DV.wait(sPE, kv_done)
                DV.op(lambda eng, i=i, pk=pk: eng.scalar_tensor_tensor(
                    out=ZP[0], in0=cJ, scalar=par_w(i), in1=pk,
                    op0=ALU.mult, op1=ALU.add))
                DV.opE(lambda eng, i=i: eng.tensor_copy(
                    out=CB[:, 0:1], in_=PPB[:, i:i + 1]), sDV)
                DV.opE(lambda eng, i=i: eng.tensor_tensor_scan(
                    out=CB[:, 1:CH + 1], data0=ZP[0], data1=ZP[0],
                    initial=PPB[:, i:i + 1], op0=ALU.max, op1=ALU.max), sDV)
                DV.op(lambda eng: eng.tensor_tensor(
                    out=ZP[1], in0=CB[:, 0:CH], in1=ZP[0], op=ALU.subtract))
                DV.op(lambda eng, i=i: eng.tensor_scalar(
                    out=ZP[4], in0=ZP[1], scalar1=par_cl(i), scalar2=None,
                    op0=ALU.max))
                DV.op(lambda eng: eng.tensor_scalar(
                    out=ZP[2], in0=ZP[4], scalar1=0.0, scalar2=None,
                    op0=ALU.min))
                DV.op(lambda eng: eng.tensor_scalar(
                    out=ZP[3], in0=ZP[1], scalar1=-1.0, scalar2=0.0,
                    op0=ALU.mult, op1=ALU.min), inc=sDV)
                SC.wait(sDV)
                SC.op(lambda eng: eng.activation(out=ZP[1], in_=ZP[2], func=AF.Exp))
                SC.op(lambda eng: eng.activation(out=ZP[2], in_=ZP[3], func=AF.Exp))
                SC.op(lambda eng, i=i: eng.activation(
                    out=ZP[3], in_=ZP[4], func=AF.Exp, bias=par_upw(i),
                    scale=-1.0), inc=sAC)
                DV.wait(sAC)
                DV.op(lambda eng, pv=pv: eng.tensor_tensor(
                    out=ZP[0], in0=ZP[2], in1=pv, op=ALU.mult))
                DV.opE(lambda eng, i=i: eng.tensor_copy(
                    out=ABF[:, 0:1], in_=AAC[:, i:i + 1]), sDV)
                DV.opE(lambda eng, i=i: eng.tensor_tensor_scan(
                    out=ABF[:, 1:CH + 1], data0=ZP[1], data1=ZP[0],
                    initial=AAC[:, i:i + 1], op0=ALU.mult, op1=ALU.add), sDV)
                DV.opE(lambda eng, i=i: eng.tensor_copy(
                    out=BBF[:, 0:1], in_=BBC[:, i:i + 1]), sDV)
                DV.opE(lambda eng, i=i: eng.tensor_tensor_scan(
                    out=BBF[:, 1:CH + 1], data0=ZP[1], data1=ZP[2],
                    initial=BBC[:, i:i + 1], op0=ALU.mult, op1=ALU.add), sDV)
                DV.op(lambda eng, pv=pv: eng.tensor_tensor(
                    out=ZP[4], in0=ZP[3], in1=pv, op=ALU.mult))
                DV.op(lambda eng: eng.tensor_tensor(
                    out=ZP[0], in0=ABF[:, 0:CH], in1=ZP[4], op=ALU.add))
                DV.op(lambda eng: eng.tensor_tensor(
                    out=ZP[2], in0=BBF[:, 0:CH], in1=ZP[3], op=ALU.add),
                    inc=sDV)
                wkv_end[i % 2] = sDV.n
                DV.opE(lambda eng, i=i: eng.tensor_scalar(
                    out=PPB[:, i:i + 1], in0=CB[:, CH:CH + 1],
                    scalar1=par_w512(i), scalar2=None, op0=ALU.subtract), sDV)
                DV.opE(lambda eng, i=i: eng.tensor_copy(
                    out=AAC[:, i:i + 1], in_=ABF[:, CH:CH + 1]), sDV)
                DV.opE(lambda eng, i=i: eng.tensor_copy(
                    out=BBC[:, i:i + 1], in_=BBF[:, CH:CH + 1]), sDV)
                SC.wait(sDV)
                SC.op(lambda eng: eng.activation(out=ZP[1], in_=ZP[2], func=AF.Ln))
                SC.op(lambda eng: eng.activation(
                    out=ZP[4], in_=ZP[1], func=AF.Exp, scale=-1.0), inc=sAC)
                DV.wait(sAC)
                DV.op(lambda eng: eng.tensor_tensor(
                    out=TMPA, in0=ZP[0], in1=ZP[4], op=ALU.mult))
                DV.op(lambda eng, i=i: eng.tensor_tensor(
                    out=YG[:, i * CH:(i + 1) * CH], in0=TMPA,
                    in1=YG[:, i * CH:(i + 1) * CH], op=ALU.mult), inc=sDV)
            wkv_all = sDV.n
            if DBG < 6:
                continue
            PE.wait(sDV, wkv_all)
            for dp in range(CT):
                for ct in range(CT):
                    mm(PS[6], wsl(W3, ct, dp), YG[:, ct * CH:(ct + 1) * CH],
                       start=(ct == 0), stop=(ct == CT - 1),
                       inc=(sPE if ct == CT - 1 else None))
                DV.wait(sPE)
                DV.op(lambda eng, dp=dp, t0=t0: eng.tensor_tensor(
                    out=X[:, dp * T + t0: dp * T + t0 + CH],
                    in0=X[:, dp * T + t0: dp * T + t0 + CH], in1=PS[6],
                    op=ALU.add), inc=sDV)
                PE.wait(sDV)
        barrier()

    if n_sublayers >= 1:
        attn_a()

    def ffn_build():
        barrier()
        for dst, s in ((FP2, "cFP2"), (FP2B, "cFP2B"), (FSD, "cFSD"),
                       (FSG, "cFSG")):
            SY.dma(lambda eng, d=dst, s=s: eng.dma_start(
                out=d[0:din[s].shape[0], :], in_=din[s]), sIN)
        for dst, s in ((FG1R, "f_G1R"), (FG2R, "f_G2R"), (FG1I, "f_G1I"),
                       (FG2I, "f_G2I")):
            GP.dma(lambda eng, d=dst, s=s: eng.dma_start(
                out=d[0:din[s].shape[0], :], in_=din[s]), sGPD)
        GP.dma(lambda eng: eng.dma_start(
            out=AUGC[32:33, 0:1024], in_=din["f_bias"]), sGPD)
        PE.wait(sGPD)
        PE.wait(sIN)
        for part in range(2):
            G1 = FG1R if part == 0 else FG1I
            G2 = FG2R if part == 0 else FG2I
            for c4 in range(4):
                mm(PS[0][0:32, :], G1[0:32, c4 * 32:(c4 + 1) * 32],
                   FP2[0:32, :], start=True, stop=True, inc=sPE)
                SC.wait(sPE)
                SC.op(lambda eng, G1=G1: eng.activation(
                    out=FO1[0:32, :], in_=PS[0][0:32, :], func=AF.Copy))
                mm(PS[1][0:16, :], G2[0:16, c4 * 16:(c4 + 1) * 16],
                   FP2B[0:16, :], start=True, stop=True, inc=sPE)
                SC.wait(sPE)
                SC.op(lambda eng: eng.activation(
                    out=FO2[0:16, :], in_=PS[1][0:16, :], func=AF.Copy),
                    inc=sAC)
                PE.wait(sAC)
                for pt in range(4):
                    PE.wait(sDV)
                    mm(PS[2], FSD[0:32, pt * 128:(pt + 1) * 128], FO1[0:32, :],
                       start=True, stop=True)
                    mm(PS[3], FSG[0:16, pt * 128:(pt + 1) * 128], FO2[0:16, :],
                       start=True, stop=True, inc=sPE)
                    DV.wait(sPE)
                    DV.op(lambda eng: eng.tensor_copy(out=FEA, in_=PS[2]))
                    DV.op(lambda eng: eng.tensor_tensor(
                        out=FEA, in0=PS[3], in1=FEA, op=ALU.mult))
                    w = FWT[part * 4 + pt]
                    if c4 == 0:
                        DV.op(lambda eng, w=w: eng.tensor_copy(out=w, in_=FEA),
                              inc=sDV)
                    else:
                        DV.op(lambda eng, w=w: eng.tensor_tensor(
                            out=w, in0=w, in1=FEA, op=ALU.add), inc=sDV)
        SC.wait(sDV)
        for ct2 in range(4):
            SC.op(lambda eng, ct2=ct2: eng.activation(
                out=W0[:, ct2 * 1024: ct2 * 1024 + 512], in_=FWT[ct2],
                func=AF.Copy))
            SC.op(lambda eng, ct2=ct2: eng.activation(
                out=W0[:, ct2 * 1024 + 512: ct2 * 1024 + 1024], in_=FWT[4 + ct2],
                func=AF.Copy))
            SC.op(lambda eng, ct2=ct2: eng.activation(
                out=W0[:, (4 + ct2) * 1024: (4 + ct2) * 1024 + 512],
                in_=FWT[4 + ct2], func=AF.Copy, scale=-1.0))
            SC.op(lambda eng, ct2=ct2: eng.activation(
                out=W0[:, (4 + ct2) * 1024 + 512: (4 + ct2) * 1024 + 1024],
                in_=FWT[ct2], func=AF.Copy), inc=sAC)
        barrier()
        make_svec(W0, lambda ct: cOn[:, 0:1], AUGC, 0)
        barrier()
        SY.dma(lambda eng: eng.dma_start(out=wstash, in_=W0), sIN)

    def ffn(reload=False):
        if reload:
            barrier()
            SY.wait(sIN)
            SY.dma(lambda eng: eng.dma_start(out=W0, in_=wstash), sIN)
        ln_stats()
        if reload:
            PE.wait(sIN)
        res_done = [0, 0]
        for c in range(NCH):
            t0 = c * CH
            DV.wait(sPE)
            make_xs(c)
            xs_v = sDV.n
            PE.wait(sDV, xs_v)
            for dp in range(CT):
                psf = PS[dp % 2]
                if c * CT + dp >= 2:
                    PE.wait(sDV, res_done[dp % 2])
                proj_group(psf, W0, dp,
                           lambda ct: XS[:, ct * (CH + 1) + 1:
                                         ct * (CH + 1) + 1 + CH],
                           (AUGC[0:1, dp * 128:(dp + 1) * 128],
                            MRB[0:1, 1 + t0:1 + t0 + CH]),
                           (AUGC[32:33, dp * 128:(dp + 1) * 128],
                            cOn[32:33, 0:CH]), inc=sPE)
                DV.wait(sPE)
                DV.op(lambda eng, dp=dp, t0=t0, psf=psf: eng.tensor_tensor(
                    out=X[:, dp * T + t0: dp * T + t0 + CH],
                    in0=X[:, dp * T + t0: dp * T + t0 + CH], in1=psf,
                    op=ALU.add), inc=sDV)
                res_done[dp % 2] = sDV.n
        barrier()

    if n_sublayers >= 2:
        ffn_build()
        ffn()

    import os as _os_chk
    def load_win(src_name):
        win_v = WINB.rearrange("p (ct d) -> p ct d", d=1024)
        src_t = din[src_name].rearrange("(r p) cc -> p r cc", p=128)
        for half in range(2):
            GP.wait(sPE)
            GP.dma(lambda eng, half=half, src_t=src_t: eng.dma_start(
                out=WNAT2.rearrange("p (r cc) -> p r cc", cc=1024),
                in_=src_t[:, half * 4:(half + 1) * 4, :]), sGPD)
            PE.wait(sGPD)
            for r4 in range(4):
                r = half * 4 + r4
                PE.wait(sDV)
                PE.wait(sAC)
                for dblk in range(8):
                    PE.op(lambda eng, r4=r4, dblk=dblk: eng.transpose(
                        PST[:, dblk * 128:(dblk + 1) * 128],
                        WNAT2[:, r4 * 1024 + dblk * 128: r4 * 1024 + (dblk + 1) * 128],
                        cIdB), inc=(sPE if dblk == 7 else None))
                evq, evsem = (DV, sDV) if (r % 2 == 0) else (SC, sAC)
                evq.wait(sPE)
                dst = win_v[:, :, r * 128:(r + 1) * 128]
                evq.op(lambda eng, dst=dst: cpy(
                    eng, dst, PST.rearrange("p (a r2) -> p a r2", a=8)), inc=evsem)
        barrier()

    def kTs(dt):
        slot = W0 if dt < 4 else W1
        return slot[:, (dt % 4) * 2048:(dt % 4) * 2048 + 2048]

    def attn_b():
        ln_stats()
        GP.dma(lambda eng: eng.dma_start(
            out=AUGB[0:1, 0:1024], in_=din["b_out_b"]), sGPD)
        DV.wait(sGPD)
        for c in range(NCH):
            t0 = c * CH
            for ct in range(CT):
                DV.op(lambda eng, ct=ct, t0=t0: eng.tensor_tensor(
                    out=QTXS[:, ct * T + t0: ct * T + t0 + CH],
                    in0=X[:, ct * T + t0: ct * T + t0 + CH],
                    in1=RSTD[:, t0:t0 + CH], op=ALU.mult),
                    inc=(sDV if ct == CT - 1 else None))
        for j in range(16):
            oc = VN[j].rearrange("p (h d) -> p h d", d=65)[:, :, 64:65]
            DV.op(lambda eng, oc=oc: eng.tensor_copy(
                out=oc, in_=cOn[:, 0:16].unsqueeze(2)),
                inc=(sDV if j == 15 else None))
        barrier()
        load_win("b_wv")
        make_svec(WINB, lambda ct: cOn[:, 0:1], AUGA, 0)
        barrier()
        ev_done = [0, 0]
        for tt in range(16):
            for hf in range(2):
                idx = tt * 2 + hf
                ps_ = PS[idx % 2]
                if idx >= 2:
                    PE.wait(sAC, ev_done[idx % 2])
                for ct in range(CT):
                    mm(ps_, QTXS[:, ct * T + tt * 128: ct * T + (tt + 1) * 128],
                       WINB[:, ct * 1024 + hf * 512: ct * 1024 + hf * 512 + 512],
                       start=(ct == 0), stop=False)
                mm(ps_, MRB[0:1, 1 + tt * 128: 1 + (tt + 1) * 128],
                   AUGA[0:1, hf * 512: hf * 512 + 512],
                   start=False, stop=True, inc=sPE)
                SC.wait(sPE)
                dst = VN[tt].rearrange("p (h d) -> p h d",
                                       d=65)[:, hf * 8:(hf + 1) * 8, 0:64]
                SC.op(lambda eng, dst=dst, ps_=ps_: eng.activation(
                    out=dst, in_=ps_.rearrange("p (a b) -> p a b", a=8),
                    func=AF.Copy), inc=sAC)
                ev_done[idx % 2] = sAC.n
        barrier()
        load_win("b_wk")
        make_svec(WINB, lambda ct: cOn[:, 0:1], AUGA, 32)
        barrier()
        kv_ev = [0, 0, 0, 0]
        for c in range(NCH):
            t0 = c * CH
            for dt in range(CT):
                ps_ = PS[dt % 4]
                if kv_ev[dt % 4]:
                    PE.wait(sAC, kv_ev[dt % 4])
                for ct in range(CT):
                    mm(ps_, WINB[:, ct * 1024 + dt * 128: ct * 1024 + (dt + 1) * 128],
                       QTXS[:, ct * T + t0: ct * T + t0 + CH],
                       start=(ct == 0), stop=False)
                mm(ps_, AUGA[32:33, dt * 128:(dt + 1) * 128],
                   MRB[32:33, 1 + t0:1 + t0 + CH], start=False, stop=True, inc=sPE)
                SC.wait(sPE)
                SC.op(lambda eng, dt=dt, t0=t0, ps_=ps_: eng.activation(
                    out=kTs(dt)[:, t0:t0 + CH], in_=ps_, func=AF.Copy), inc=sAC)
                kv_ev[dt % 4] = sAC.n
            DV.wait(sAC)
            for hc in range(2):
                th = t0 + hc * 256
                GP.wait(sDV)
                GP.dma(lambda eng, th=th: eng.dma_start(
                    out=ROPE.rearrange("p (dt j) -> p dt j", j=256),
                    in_=din["cCos"].rearrange("p (dt t) -> p dt t", t=T)[:, :, th:th + 256]), sGPD)
                GP.dma(lambda eng, th=th: eng.dma_start(
                    out=RTS.rearrange("p (dt j) -> p dt j", j=256),
                    in_=din["cSin"].rearrange("p (dt t) -> p dt t", t=T)[:, :, th:th + 256]), sGPD)
                DV.wait(sGPD)
                for dt in range(4):
                    top = kTs(dt)[:, th:th + 256]
                    bot = kTs(dt + 4)[:, th:th + 256]
                    cosb = ROPE[:, dt * 256:(dt + 1) * 256]
                    sinb = RTS[:, dt * 256:(dt + 1) * 256]
                    DV.op(lambda eng, top=top, cosb=cosb: eng.tensor_tensor(
                        out=TMPC, in0=top, in1=cosb, op=ALU.mult))
                    DV.op(lambda eng, top=top, sinb=sinb: eng.tensor_tensor(
                        out=TMPD, in0=top, in1=sinb, op=ALU.mult))
                    DV.op(lambda eng, top=top, bot=bot, sinb=sinb: eng.tensor_tensor(
                        out=top, in0=bot, in1=sinb, op=ALU.mult))
                    DV.op(lambda eng, top=top: eng.tensor_tensor(
                        out=top, in0=TMPC, in1=top, op=ALU.subtract))
                    DV.op(lambda eng, bot=bot, cosb=cosb: eng.tensor_tensor(
                        out=TMPC, in0=bot, in1=cosb, op=ALU.mult))
                    DV.op(lambda eng, bot=bot: eng.tensor_tensor(
                        out=bot, in0=TMPD, in1=TMPC, op=ALU.add),
                        inc=(sDV if dt == 3 else None))
        barrier()
        load_win("b_wq")
        make_svec(WINB, lambda ct: cOn[:, 0:1], AUGA, 64)
        barrier()
        for c in range(NCH):
            t0 = c * CH
            for hc in range(2):
                th = t0 + hc * 256
                GP.wait(sDV)
                GP.dma(lambda eng, th=th: eng.dma_start(
                    out=ROPE.rearrange("p (dt j) -> p dt j", j=256),
                    in_=din["cCos"].rearrange("p (dt t) -> p dt t", t=T)[:, :, th:th + 256]), sGPD)
                GP.dma(lambda eng, th=th: eng.dma_start(
                    out=RTS.rearrange("p (dt j) -> p dt j", j=256),
                    in_=din["cSin"].rearrange("p (dt t) -> p dt t", t=T)[:, :, th:th + 256]), sGPD)
                qev = [0, 0]
                for dt in range(CT):
                    ps_ = PS[dt % 2][:, 0:256]
                    PE.wait(sDV)
                    if qev[dt % 2]:
                        PE.wait(sAC, qev[dt % 2])
                    for ct in range(CT):
                        mm(ps_, WINB[:, ct * 1024 + dt * 128: ct * 1024 + (dt + 1) * 128],
                           QTXS[:, ct * T + th: ct * T + th + 256],
                           start=(ct == 0), stop=False)
                    mm(ps_, AUGA[64:65, dt * 128:(dt + 1) * 128],
                       MRB[64:65, 1 + th:1 + th + 256], start=False, stop=True,
                       inc=sPE)
                    SC.wait(sPE)
                    SC.op(lambda eng, dt=dt, ps_=ps_: eng.activation(
                        out=QP8[:, dt * 256:(dt + 1) * 256], in_=ps_,
                        func=AF.Copy), inc=sAC)
                    qev[dt % 2] = sAC.n
                DV.wait(sAC)
                DV.wait(sGPD)
                for dt in range(4):
                    topo = QTXS[:, dt * T + th: dt * T + th + 256]
                    boto = QTXS[:, (dt + 4) * T + th:(dt + 4) * T + th + 256]
                    qpa = QP8[:, dt * 256:(dt + 1) * 256]
                    qpb = QP8[:, (dt + 4) * 256:(dt + 5) * 256]
                    cosb = ROPE[:, dt * 256:(dt + 1) * 256]
                    sinb = RTS[:, dt * 256:(dt + 1) * 256]
                    DV.op(lambda eng, qpa=qpa, cosb=cosb: eng.tensor_tensor(
                        out=TMPC, in0=qpa, in1=cosb, op=ALU.mult))
                    DV.op(lambda eng, qpb=qpb, sinb=sinb: eng.tensor_tensor(
                        out=TMPD, in0=qpb, in1=sinb, op=ALU.mult))
                    DV.op(lambda eng, topo=topo: eng.tensor_tensor(
                        out=topo, in0=TMPC, in1=TMPD, op=ALU.subtract))
                    DV.op(lambda eng, qpa=qpa, sinb=sinb: eng.tensor_tensor(
                        out=TMPC, in0=qpa, in1=sinb, op=ALU.mult))
                    DV.op(lambda eng, qpb=qpb, cosb=cosb: eng.tensor_tensor(
                        out=TMPD, in0=qpb, in1=cosb, op=ALU.mult))
                    DV.op(lambda eng, boto=boto: eng.tensor_tensor(
                        out=boto, in0=TMPC, in1=TMPD, op=ALU.add),
                        inc=(sDV if dt == 3 else None))
        barrier()
        if os.environ.get("KDUMP") == "qt":
            SY.dma(lambda eng: eng.dma_start(out=dbg, in_=QTXS), sIN)
            barrier()
        if os.environ.get("KDUMP") == "kt":
            SY.dma(lambda eng: eng.dma_start(out=dbg[:, 0:8192], in_=W0), sIN)
            SY.dma(lambda eng: eng.dma_start(out=dbg[:, 8192:16384], in_=W1), sIN)
            barrier()
        if os.environ.get("KDUMP") == "vn":
            for j in range(16):
                SY.dma(lambda eng, j=j: eng.dma_start(
                    out=dbg[:, j * 1024:(j + 1) * 1024], in_=VN[j][:, 0:1024]), sIN)
            barrier()
        load_win("b_out_w")
        barrier()
        for g in range(NCH):
            t1c = g * 512
            tr_pe = [0]
            ts_dv = [0]
            pending_norm = [None]
            for h in range(H):
                dtq = h // 2
                pbase = (h % 2) * 64
                qsl = QTXS[pbase:pbase + 64, dtq * T + t1c: dtq * T + t1c + 512]
                gran = ALIBI_GRAN[h]
                dmin = -(16 - gran // 128)
                nj = 4 * g + 4
                exp_ac = {}
                mask_dv = {}
                pv_arr = {}

                def emit_qk(j):
                    psc = PS[j % 2]
                    if j >= 2:
                        PE.wait(sAC, exp_ac[j - 2])
                    else:
                        PE.wait(sAC)
                    ksl = kTs(dtq)[pbase:pbase + 64, j * 128:(j + 1) * 128]
                    mm(psc, ksl, qsl, start=True, stop=True, inc=sPE)
                    return sPE.n

                def emit_exp(j, qk_v):
                    psc = PS[j % 2]
                    PT = PTD[j % 2]
                    SC.wait(sPE, qk_v)
                    if j >= 2:
                        SC.wait(sPE, pv_arr[j - 2])
                    if j - 1 in mask_dv:
                        SC.wait(sDV, mask_dv[j - 1])
                    nsub = 512 // gran
                    for s2 in range(nsub):
                        c0 = s2 * gran
                        c1 = c0 + gran
                        dlo = j * 128 - t1c
                        if c1 <= dlo:
                            SC.op(lambda eng, c0=c0, c1=c1, PT=PT: eng.memzero(
                                PT[:, c0:c1]))
                            continue
                        blk0 = (t1c + s2 * gran) // gran
                        didx = j - (gran // 128) * blk0 - dmin
                        bias = cBt[:, h * 16 + didx: h * 16 + didx + 1]
                        if c0 <= dlo < c1:
                            if dlo > c0:
                                SC.op(lambda eng, c0=c0, dlo=dlo, PT=PT:
                                      eng.memzero(PT[:, c0:dlo]))
                            SC.op(lambda eng, psc=psc, dlo=dlo, bias=bias:
                                  eng.activation(
                                      out=PSC, in_=psc[:, dlo:dlo + 128],
                                      func=AF.Exp, scale=0.125, bias=bias),
                                  inc=sAC)
                            DV.wait(sAC)
                            DV.op(lambda eng, dlo=dlo, PT=PT: eng.tensor_tensor(
                                out=PT[:, dlo:dlo + 128], in0=PSC, in1=cDg,
                                op=ALU.mult), inc=sDV)
                            if dlo + 128 < c1:
                                SC.op(lambda eng, psc=psc, dlo=dlo, c1=c1,
                                      bias=bias, PT=PT: eng.activation(
                                    out=PT[:, dlo + 128:c1],
                                    in_=psc[:, dlo + 128:c1],
                                    func=AF.Exp, scale=0.125, bias=bias))
                        elif c0 > dlo:
                            SC.op(lambda eng, psc=psc, c0=c0, c1=c1, bias=bias,
                                  PT=PT: eng.activation(
                                      out=PT[:, c0:c1], in_=psc[:, c0:c1],
                                      func=AF.Exp, scale=0.125, bias=bias))
                    exp_ac[j] = sAC.n
                    mask_dv[j] = sDV.n

                def emit_pv(j):
                    PT = PTD[j % 2]
                    if j == 0 and ts_dv[0]:
                        PE.wait(sDV, ts_dv[0])
                    PE.wait(sAC, exp_ac[j])
                    PE.wait(sDV, mask_dv[j])
                    for i2 in range(4):
                        mm(PS[6][:, i2 * 65:(i2 + 1) * 65],
                           PT[:, i2 * 128:(i2 + 1) * 128],
                           VN[j][:, 65 * h:65 * h + 65],
                           start=(j == 0 and i2 == 0),
                           stop=(j == nj - 1 and i2 == 3),
                           inc=(sPE if i2 == 3 else None))
                    pv_arr[j] = sPE.n

                for j in range(nj):
                    qkv = emit_qk(j)
                    emit_exp(j, qkv)
                    if j == 1 and pending_norm[0] is not None:
                        pending_norm[0]()
                        pending_norm[0] = None
                    if j >= 1:
                        emit_pv(j - 1)
                if pending_norm[0] is not None:
                    pending_norm[0]()
                    pending_norm[0] = None
                emit_pv(nj - 1)
                pv_pe = pv_arr[nj - 1]
                def make_norm(h=h, dtq=dtq, pbase=pbase,
                              pv_arr=pv_arr, nj=nj):
                    pv_pe = pv_arr[nj - 1]
                    DV.wait(sPE, pv_pe)
                    if ts_dv[0]:
                        DV.wait(sDV, ts_dv[0])
                    zcols = PS[6][:, 0:260].rearrange(
                        "p (i c) -> p i c", c=65)[:, :, 64:65]
                    DV.op(lambda eng, zcols=zcols, h=h: eng.tensor_tensor(
                        out=RCB[:, 0:4].unsqueeze(2), in0=zcols,
                        in1=cCV[:, h * 4:h * 4 + 4].unsqueeze(2), op=ALU.mult))
                    DV.op(lambda eng: eng.drain(fusable=False), inc=sDV)
                    SC.wait(sDV)
                    SC.op(lambda eng: eng.activation(
                        out=RCB[:, 4:8], in_=RCB[:, 0:4], func=AF.Ln))
                    SC.op(lambda eng: eng.activation(
                        out=RCB[:, 8:12], in_=RCB[:, 4:8], func=AF.Exp, scale=-1.0))
                    SC.op(lambda eng: eng.drain(fusable=False), inc=sAC)
                    DV.wait(sAC)
                    DV.op(lambda eng, h=h: eng.tensor_tensor(
                        out=RCB[:, 8:12], in0=RCB[:, 8:12],
                        in1=cCV[:, h * 4:h * 4 + 4], op=ALU.mult), inc=sDV)
                    DV.wait(sDV)
                    for i2 in range(4):
                        DV.op(lambda eng, i2=i2: eng.tensor_scalar(
                            out=STG64[:, i2 * 64:(i2 + 1) * 64],
                            in0=PS[6][:, i2 * 65: i2 * 65 + 64],
                            scalar1=RCB[:, 8 + i2:9 + i2], scalar2=None,
                            op0=ALU.mult), inc=(sDV if i2 == 3 else None))
                    if (os.environ.get("KDUMP") == "stg" and g == 0
                            and h == int(os.environ.get("KH", "2"))):
                        barrier()
                        SY.dma(lambda eng: eng.dma_start(
                            out=dbg[:, 0:256], in_=STG64), sIN)
                        SY.dma(lambda eng: eng.dma_start(
                            out=dbgf[:, 0:8], in_=RCB), sIN)
                        barrier()
                    PE.wait(sDV)
                    for i2 in range(4):
                        PE.op(lambda eng, i2=i2: eng.transpose(
                            PST[0:64, i2 * 128:(i2 + 1) * 128],
                            STG64[:, i2 * 64:(i2 + 1) * 64], cIdB),
                            inc=(sPE if i2 == 3 else None))
                    tr_pe[0] = sPE.n
                    SC.wait(sPE)
                    SC.op(lambda eng, pbase=pbase, dtq=dtq: eng.activation(
                        out=OTB[pbase:pbase + 64, dtq * 512:(dtq + 1) * 512],
                        in_=PST[0:64, 0:512], func=AF.Copy), inc=sAC)
                    PE.wait(sAC)
                pending_norm[0] = make_norm

            if pending_norm[0] is not None:
                pending_norm[0]()
                pending_norm[0] = None
            if g == 0 and os.environ.get("KDUMP") == "otb":
                barrier()
                SY.dma(lambda eng: eng.dma_start(out=dbg[:, 0:4096], in_=OTB), sIN)
                barrier()
            op_res = [0, 0]
            for dp in range(CT):
                pso = PS[2 + dp % 2]
                if dp >= 2:
                    PE.wait(sDV, op_res[dp % 2])
                for ct in range(CT):
                    mm(pso, WINB[:, ct * 1024 + dp * 128: ct * 1024 + (dp + 1) * 128],
                       OTB[:, ct * 512:(ct + 1) * 512],
                       start=(ct == 0), stop=False)
                mm(pso, AUGB[0:1, dp * 128:(dp + 1) * 128], cOn[0:1, 0:512],
                   start=False, stop=True, inc=sPE)
                DV.wait(sPE)
                DV.op(lambda eng, dp=dp, t1c=t1c, pso=pso: eng.tensor_tensor(
                    out=X[:, dp * T + t1c: dp * T + t1c + 512],
                    in0=X[:, dp * T + t1c: dp * T + t1c + 512], in1=pso,
                    op=ALU.add), inc=sDV)
                op_res[dp % 2] = sDV.n
        barrier()

    if n_sublayers >= 3:
        attn_b()
    if n_sublayers >= 4:
        ffn(reload=True)

    barrier()
    for tt in range(16):
        par = tt % 2
        sS = sSA if par == 0 else sSB
        if tt >= 2:
            PE.wait(sS, 16 * (tt // 2))
        PE.wait(sAC)
        for hf in range(2):
            bank = PS[hf]
            for cq in range(4):
                cb = hf * 4 + cq
                PE.op(lambda eng, cb=cb, cq=cq, tt=tt, bank=bank: eng.transpose(
                    bank[:, cq * 128:(cq + 1) * 128],
                    X[:, cb * T + tt * 128: cb * T + (tt + 1) * 128],
                    cIdF), inc=(sPE if cq == 3 else None))
            SC.wait(sPE)
            SC.op(lambda eng, hf=hf, par=par, bank=bank: cpy(eng,
                OST[:, par * 1024 + hf * 512: par * 1024 + hf * 512 + 512],
                bank), inc=sAC)
        SY.wait(sAC)
        SY.dma(lambda eng, tt=tt, par=par: eng.dma_start(
            out=out[tt * 128:(tt + 1) * 128, :],
            in_=OST[:, par * 1024:(par + 1) * 1024]), sS)
    SY.wait(sSA)
    SY.wait(sSB)

    def body(qname):
        def run(eng):
            for item in q[qname].items:
                if item[0] == "wait":
                    eng.wait_ge(item[1].h, item[2])
                else:
                    inst = item[1](eng)
                    for s, n in item[2]:
                        inst.then_inc(s.h, n)
        return run
    blk.sync(body("sync"))
    blk.gpsimd(body("gpsimd"))
    blk.vector(body("vector"))
    blk.scalar(body("scalar"))
    blk.tensor(body("tensor"))
    blk_ctx.__exit__(None, None, None)
    for ctx in sem_ctx:
        ctx.__exit__(None, None, None)
    return nc


_consts_cache = None

def kernel(L0_attn_a_x, L0_attn_a_time_decay, L0_attn_a_time_first,
           L0_attn_a_time_mix_k, L0_attn_a_time_mix_v, L0_attn_a_time_mix_r,
           L0_attn_a_Wk, L0_attn_a_Wv, L0_attn_a_Wr, L0_attn_a_Wo,
           L0_attn_b_wq, L0_attn_b_wk, L0_attn_b_wv, L0_attn_b_out_w,
           L0_attn_b_out_b, L0_ffn_G1_R, L0_ffn_G2_R, L0_ffn_G1_I, L0_ffn_G2_I,
           L0_ffn_bias_real, L0_ffn_bias_imag, _sublayers=4):
    global _consts_cache
    if _consts_cache is None:
        _consts_cache = host_constants()
    f32 = np.float32
    x = np.ascontiguousarray(np.asarray(L0_attn_a_x, f32))
    par = np.ascontiguousarray(np.stack([
        np.asarray(L0_attn_a_time_decay, f32).reshape(8, 128),
        np.asarray(L0_attn_a_time_first, f32).reshape(8, 128),
        np.asarray(L0_attn_a_time_mix_k, f32).reshape(8, 128),
        np.asarray(L0_attn_a_time_mix_v, f32).reshape(8, 128),
        np.asarray(L0_attn_a_time_mix_r, f32).reshape(8, 128)],
        axis=0).reshape(40, 128))
    g2r = np.ascontiguousarray(np.asarray(L0_ffn_G2_R, f32).reshape(4, 16, 16)
                               .transpose(1, 0, 2).reshape(16, 64))
    g2i = np.ascontiguousarray(np.asarray(L0_ffn_G2_I, f32).reshape(4, 16, 16)
                               .transpose(1, 0, 2).reshape(16, 64))
    base = {
        "a_Wk": np.ascontiguousarray(np.asarray(L0_attn_a_Wk, f32)),
        "a_Wv": np.ascontiguousarray(np.asarray(L0_attn_a_Wv, f32)),
        "a_Wr": np.ascontiguousarray(np.asarray(L0_attn_a_Wr, f32)),
        "a_Wo": np.ascontiguousarray(np.asarray(L0_attn_a_Wo, f32)),
        "b_wq": np.ascontiguousarray(np.asarray(L0_attn_b_wq, f32)),
        "b_wk": np.ascontiguousarray(np.asarray(L0_attn_b_wk, f32)),
        "b_wv": np.ascontiguousarray(np.asarray(L0_attn_b_wv, f32)),
        "b_out_w": np.ascontiguousarray(np.asarray(L0_attn_b_out_w, f32)),
        "a_par": par,
        "b_out_b": np.ascontiguousarray(np.asarray(L0_attn_b_out_b, f32))[None, :],
        "f_G1R": np.ascontiguousarray(np.asarray(L0_ffn_G1_R, f32).reshape(32, 128)),
        "f_G2R": g2r,
        "f_G1I": np.ascontiguousarray(np.asarray(L0_ffn_G1_I, f32).reshape(32, 128)),
        "f_G2I": g2i,
        "f_bias": np.concatenate([np.asarray(L0_ffn_bias_real, f32),
                                  np.asarray(L0_ffn_bias_imag, f32)])[None, :],
    }
    base.update(_consts_cache)
    nc = build(n_sublayers=_sublayers)
    in_maps = []
    for b in range(8):
        m = dict(base)
        m["xin"] = np.ascontiguousarray(x[b])
        in_maps.append(m)
    res = run_bass_kernel_spmd(nc, in_maps, core_ids=list(range(8)))
    return np.stack([np.asarray(res.results[b]["out"]) for b in range(8)], axis=0)


# version 09
# speedup vs baseline: 1.7134x; 1.0013x over previous
import numpy as np
import ml_dtypes
import concourse.bass as bass
import concourse.mybir as mybir
import bass_rust
from concourse.bass_utils import run_bass_kernel_spmd

F32 = mybir.dt.float32
BF16 = mybir.dt.bfloat16
AF = mybir.ActivationFunctionType
ALU = mybir.AluOpType
BF_NP = ml_dtypes.bfloat16

T = 2048
C = 1024
CT = 8
CH = 512
NCH = T // CH
H = 16
SHIFT = 64.0
EPS = 1e-5
ALIBI_GRAN = [128 if h < 2 else (256 if h < 4 else 512) for h in range(H)]


def host_constants():
    c = {}
    c["cJ"] = np.repeat((np.arange(CH, dtype=np.float32) + 1.0)[None, :], 128, 0)
    c["cIdentF"] = np.eye(128, dtype=np.float32)
    c["cIdentB"] = np.eye(128, dtype=BF_NP)
    c["cDiag"] = (np.arange(128)[None, :] >= np.arange(128)[:, None]).astype(BF_NP)
    c["cOnes"] = np.ones((128, 512), dtype=BF_NP)
    c["cZeros"] = np.zeros((128, 512), dtype=BF_NP)
    slopes = np.array([2.0 ** (-8.0 * (i + 1) / H) for i in range(H)], np.float64)
    bt = np.zeros((128, 256), np.float32)
    p = np.arange(128, dtype=np.float64)
    for h in range(H):
        gran = ALIBI_GRAN[h]
        dmin = -(16 - gran // 128)
        for idx in range(16):
            bt[:, h * 16 + idx] = (slopes[h] * p + slopes[h] * 128.0 * (dmin + idx)
                                   - SHIFT).astype(np.float32)
    c["cBias"] = bt
    cv = np.zeros((128, 64), np.float32)
    pq = np.arange(128, dtype=np.float64)
    for h in range(H):
        gran = ALIBI_GRAN[h]
        for i2 in range(4):
            fp = (128.0 * i2 + pq) % gran
            cv[:, h * 4 + i2] = np.exp(64.0 - slopes[h] * fp).astype(np.float32)
    c["cCVT"] = cv
    p2 = np.zeros((32, 512), BF_NP)
    for b in range(32):
        p2[b, b * 16:(b + 1) * 16] = 1
    c["cFP2"] = p2
    p2b = np.zeros((16, 512), BF_NP)
    for e in range(16):
        p2b[e, e::16] = 1
    c["cFP2B"] = p2b
    fsd = np.zeros((32, 512), BF_NP)
    for p in range(512):
        fsd[p // 16, p] = 1
    c["cFSD"] = fsd
    fsg = np.zeros((16, 512), BF_NP)
    for p in range(512):
        fsg[p % 16, p] = 1
    c["cFSG"] = fsg
    half = 512
    freqs = 1.0 / 10000.0 ** (np.arange(half, dtype=np.float32) / half)
    ang = np.arange(T, dtype=np.float32)[None, :] * freqs[:, None]
    c["cCos"] = np.cos(ang).astype(np.float32).reshape(4, 128, T).transpose(
        1, 0, 2).reshape(128, 4 * T).astype(BF_NP)
    c["cSin"] = np.sin(ang).astype(np.float32).reshape(4, 128, T).transpose(
        1, 0, 2).reshape(128, 4 * T).astype(BF_NP)
    return c


class Sem:
    def __init__(self, handle, name):
        self.h = handle
        self.name = name
        self.n = 0


class EQ:
    def __init__(self, name):
        self.name = name
        self.items = []
        self.auto_edge = False
        self.esem = None

    def wait(self, sem, val=None):
        v = sem.n if val is None else val
        if v > 0:
            self.items.append(("wait", sem, v))

    def op(self, fn, inc=None, inc_n=1):
        incs = []
        if inc is not None:
            inc.n += inc_n
            incs.append((inc, inc_n))
        if self.auto_edge and inc is not self.esem:
            self.esem.n += 1
            incs.append((self.esem, 1))
        self.items.append(("op", fn, incs))
        return incs

    def dma(self, fn, sem):
        return self.op(fn, inc=sem, inc_n=16)

    def opE(self, fn, esem):
        self.op(fn, inc=esem)
        self.wait(esem)

    def last_op_incs(self):
        for it in reversed(self.items):
            if it[0] == "op":
                return it[2]
        return None


def build(n_sublayers=4):
    nc = bass.Bass("TRN2", target_bir_lowering=False)
    import os as _os
    if not int(_os.environ.get("KRACE", "0")):
        nc.detect_race_conditions = False

    din = {}
    def dram_in(name, shape, dtype=F32):
        din[name] = nc.dram_tensor(name, shape, dtype, kind="ExternalInput").ap()
        return din[name]

    xin = dram_in("xin", [T, C])
    for nm in ["a_Wk", "a_Wv", "a_Wr", "a_Wo", "b_wq", "b_wk", "b_wv", "b_out_w"]:
        dram_in(nm, [C, C])
    dram_in("a_par", [40, 128])
    dram_in("b_out_b", [1, C])
    dram_in("f_G1R", [32, 128])
    dram_in("f_G2R", [16, 64])
    dram_in("f_G1I", [32, 128])
    dram_in("f_G2I", [16, 64])
    dram_in("f_bias", [1, C])
    dram_in("cJ", [128, CH])
    dram_in("cIdentF", [128, 128])
    dram_in("cIdentB", [128, 128], BF16)
    dram_in("cDiag", [128, 128], BF16)
    dram_in("cOnes", [128, 512], BF16)
    dram_in("cZeros", [128, 512], BF16)
    dram_in("cBias", [128, 256])
    dram_in("cCVT", [128, 64])
    dram_in("cCos", [128, 4 * T], BF16)
    dram_in("cFP2", [32, 512], BF16)
    dram_in("cFP2B", [16, 512], BF16)
    dram_in("cFSD", [32, 512], BF16)
    dram_in("cFSG", [16, 512], BF16)
    dram_in("cSin", [128, 4 * T], BF16)
    out = nc.dram_tensor("out", [T, C], F32, kind="ExternalOutput").ap()
    dbg = nc.dram_tensor("dbg", [128, 16384], BF16, kind="ExternalOutput").ap()
    dbgf = nc.dram_tensor("dbgf", [128, 512], F32, kind="ExternalOutput").ap()
    wstash = nc.dram_tensor("wstash", [128, 8192], BF16, kind="Internal").ap()

    tens = {}
    def sb(name, shape, dtype, offset):
        t = nc.alloc_sbuf_tensor_at(name, shape, dtype, offset=offset)
        tens[name] = t.ap()
        return tens[name]

    SB_BASE = 16512
    OFF = SB_BASE
    def place(name, shape, dtype):
        nonlocal OFF
        szb = {F32: 4, BF16: 2}[dtype]
        w = szb
        for d in shape[1:]:
            w *= d
        at = OFF
        OFF = at + ((w + 63) // 64) * 64
        return sb(name, shape, dtype, at), at

    X, _ = place("X", [128, CT * T], F32)
    cJ, _ = place("sJ", [128, CH], F32)
    cIdF, _ = place("sIdF", [128, 128], F32)
    cIdB, _ = place("sIdB", [128, 128], BF16)
    cDg, _ = place("sDiag", [128, 128], BF16)
    cOn, _ = place("sOnes", [128, 512], BF16)
    cZe, _ = place("sZeros", [128, 512], BF16)
    cBt, _ = place("sBias", [128, 256], F32)
    cCV, _ = place("sCVT", [128, 64], F32)
    PAR, _ = place("PAR", [128, 32], F32)
    PARB, _ = place("PARB", [128, 48], BF16)
    PARF, _ = place("PARF", [128, 48], F32)
    AUGA, _ = place("AUGA", [128, 1024], BF16)
    AUGB, _ = place("AUGB", [128, 1024], BF16)
    AUGC, _ = place("AUGC", [128, 1024], BF16)
    EPSB, _ = place("EPSB", [128, 1], F32)
    PRAW, _ = place("PRAW", [128, 40], F32)
    PPB, _ = place("PPB", [128, 8], F32)
    AAC, _ = place("AAC", [128, 8], F32)
    BBC, _ = place("BBC", [128, 8], F32)
    MRB, MRB_at = place("MRB", [128, T + 1], BF16)
    RSTD, RSTD_at = place("RSTD", [128, T], F32)
    XS, XS_at = place("XS", [128, CT * (CH + 1)], BF16)
    W0, W0_at = place("W0", [128, 8192], BF16)
    W1, W1_at = place("W1", [128, 8192], BF16)
    W2, W2_at = place("W2", [128, 8192], BF16)
    W3, W3_at = place("W3", [128, 8192], BF16)
    VAR_at = OFF
    SB_LIMIT = 229312
    assert VAR_at + 47000 <= SB_LIMIT, VAR_at

    V = VAR_at
    XSTG = sb("XSTG", [128, 3072], F32, V)
    WNAT = sb("WNAT", [128, 8192], BF16, V + 12288)
    WNATB = sb("WNATB", [128, 8192], BF16, V + 28672)
    XBF = sb("XBF", [128, CT * CH], BF16, V)
    XSQ = sb("XSQ", [128, CT * CH], BF16, V + 8192)
    MST = sb("MST", [128, CH], F32, V + 16384)
    VST = sb("VST", [128, CH], F32, V + 18432)
    VST2 = sb("VST2", [128, CH], F32, V + 20480)
    MIXA = sb("MIXA", [128, CT * CH], BF16, V)
    MIXB = sb("MIXB", [128, CT * CH], BF16, V + 8192)
    YG = sb("YG", [128, CT * CH], BF16, V + 16384)
    ZP = [sb(f"ZP{i}", [128, CH], F32, V + 24576 + i * 2048) for i in range(5)]
    CB = sb("CB", [128, CH + 1], F32, V + 34816)
    ABF = sb("ABF", [128, CH + 1], F32, V + 36992)
    BBF = sb("BBF", [128, CH + 1], F32, V + 39168)
    TMPA = sb("TMPA", [128, CH], BF16, V + 41344)
    TMPB = sb("TMPB", [128, CH], BF16, V + 42368)
    TMPG1 = sb("TMPG1", [128, CH], BF16, V + 43392)
    TMPG2 = sb("TMPG2", [128, CH], BF16, V + 44416)
    assert V + 45440 <= SB_LIMIT, V
    FO1 = sb("FO1", [128, 512], BF16, V)
    FEA = sb("FEA", [128, 512], F32, V + 1024)
    FG1R = sb("FG1R", [128, 128], BF16, V + 3072)
    FG2R = sb("FG2R", [128, 64], BF16, V + 3328)
    FG1I = sb("FG1I", [128, 128], BF16, V + 3456)
    FG2I = sb("FG2I", [128, 64], BF16, V + 3712)
    FP2 = sb("FP2", [128, 512], BF16, V + 3840)
    FP2B = sb("FP2B", [128, 512], BF16, V + 4864)
    FSD = sb("FSD", [128, 512], BF16, V + 5888)
    FSG = sb("FSG", [128, 512], BF16, V + 6912)
    FWT = [sb(f"FWT{i}", [128, 512], F32, V + 7936 + 2048 * i) for i in range(8)]
    QTXS = sb("QTXS", [128, CT * T], BF16, V)
    WINB = sb("WINB", [128, 8192], BF16, RSTD_at)
    VN = []
    for j in range(16):
        if j < 7:
            at = W2_at + j * 2112
        elif j < 14:
            at = W3_at + (j - 7) * 2112
        else:
            at = V + 32768 + (j - 14) * 2112
        VN.append(sb(f"VN{j}", [128, 1040], BF16, at))
    BW2 = V + 32768 + 4608
    ROPE = sb("ROPEB", [128, 1024], BF16, BW2)
    RTS = sb("RTS", [128, 1024], BF16, BW2 + 2048)
    QP8 = sb("QP8", [128, 2048], BF16, BW2 + 4096)
    TMPC = sb("TMPC", [128, 256], BF16, BW2 + 8192)
    TMPD = sb("TMPD", [128, 256], BF16, BW2 + 8704)
    OTB = sb("OTB", [128, CT * 512], BF16, BW2)
    WNAT2 = sb("WNAT2", [128, 4096], BF16, BW2)
    FO2 = sb("FO2", [128, 512], BF16, BW2 + 2048)
    PT0 = sb("PT0", [128, 512], BF16, MRB_at)
    PT1 = sb("PT1", [128, 512], BF16, MRB_at + 1024)
    PTD = [PT0, PT1]
    PSC = sb("PSC", [128, 128], BF16, MRB_at + 2048)
    STG64 = sb("STG64", [128, 256], BF16, MRB_at + 2304)
    RCB = sb("RCB", [128, 12], F32, MRB_at + 2816)
    LNZ = sb("LNZ", [128, 256], F32, MRB_at + 3072)
    OST = sb("OST", [128, 2 * 1024], F32, V)

    PS = [nc.alloc_psum_tensor(f"ps{i}", [128, 512], F32).ap() for i in range(7)]
    PST = nc.alloc_psum_tensor("pst", [128, 1024], BF16).ap()

    sem_names = ["sIN", "sGPD", "sPE", "sDV", "sAC", "sGE", "sSY", "sST",
                 "sCN", "sXA", "sXB", "sSA", "sSB"]
    sem_ctx = [nc.semaphore(n) for n in sem_names]
    handles = [ctx.__enter__() for ctx in sem_ctx]
    sems = {n: Sem(h, n) for n, h in zip(sem_names, handles)}
    sIN, sGPD, sPE, sDV, sAC, sGE, sSY, sST, sCN, sXA, sXB, sSA, sSB = (
        sems[n] for n in sem_names)
    all_sems = list(sems.values())
    esem = {"sync": sSY, "gpsimd": sGE, "vector": sDV, "scalar": sAC, "tensor": sPE}
    q = {n: EQ(n) for n in ("sync", "gpsimd", "vector", "scalar", "tensor")}
    for n in ("vector", "scalar"):
        q[n].auto_edge = True
        q[n].esem = esem[n]
    SY, GP, DV, SC, PE = (q[n] for n in ("sync", "gpsimd", "vector", "scalar", "tensor"))

    def barrier():
        for en, qq in q.items():
            incs = qq.last_op_incs()
            if incs is not None and len(incs) == 0:
                esem[en].n += 1
                incs.append((esem[en], 1))
        snap = [(s, s.n) for s in all_sems if s.n > 0]
        for qq in q.values():
            for s, v in snap:
                qq.wait(s, v)

    blk_ctx = nc.Block()
    blk = blk_ctx.__enter__()

    def cpy(eng, out, in_):
        if hasattr(eng, "tensor_copy"):
            return eng.tensor_copy(out=out, in_=in_)
        return eng.copy(out=out, in_=in_)

    def mm(psum, lhsT, rhs, start, stop, inc=None):
        PE.op(lambda eng, a=psum, b=lhsT, c=rhs, s1=start, s2=stop:
              eng.matmul(a, b, c, start=s1, stop=s2), inc=inc)

    def wsl(w, ct, dblk):
        return w[:, ct * 1024 + dblk * 128: ct * 1024 + dblk * 128 + 128]

    def proj_group(psum, wt, i, rhs_fn, aug1, aug2, inc=None):
        for ct in range(CT):
            mm(psum, wsl(wt, ct, i), rhs_fn(ct), start=(ct == 0), stop=False)
            if ct == 3 and aug1 is not None:
                mm(psum, aug1[0], aug1[1], start=False, stop=False)
        last = aug2 if aug2 is not None else aug1
        if last is None:
            raise ValueError
        mm(psum, last[0], last[1], start=False, stop=True, inc=inc)

    Xv = X.rearrange("p (ct t) -> p ct t", t=T)
    XSv = XS.rearrange("p (ct j) -> p ct j", j=CH + 1)
    def par_w(i): return PAR[:, i:i + 1]
    def par_upw(i): return PAR[:, 8 + i:9 + i]
    def par_cl(i): return PAR[:, 16 + i:17 + i]
    def par_w512(i): return PAR[:, 24 + i:25 + i]
    def parb(m, ct): return PARB[:, m * 8 + ct: m * 8 + ct + 1]
    def parb1m(m, ct): return PARB[:, 24 + m * 8 + ct: 25 + m * 8 + ct]
    def parf(m, ct): return PARF[:, m * 8 + ct: m * 8 + ct + 1]
    def parf1m(m, ct): return PARF[:, 24 + m * 8 + ct: 25 + m * 8 + ct]

    for dst, src in [(cJ, din["cJ"]), (cIdF, din["cIdentF"]), (cIdB, din["cIdentB"]),
                     (cDg, din["cDiag"]), (cOn, din["cOnes"]), (cBt, din["cBias"]),
                     (cCV, din["cCVT"]), (cZe, din["cZeros"])]:
        SY.dma(lambda eng, d=dst, s=src: eng.dma_start(out=d, in_=s), sCN)

    DV.op(lambda eng: eng.memset(EPSB, EPS), inc=sDV)
    xin_t = xin.rearrange("(tt p) c -> tt p c", p=128)
    PE.wait(sCN)
    for tt in range(16):
        par = tt % 2
        sX = sXA if par == 0 else sXB
        if tt >= 2:
            SY.wait(sAC)
        SY.dma(lambda eng, tt=tt, par=par: eng.dma_start(
            out=XSTG[:, par * 1024:(par + 1) * 1024], in_=xin_t[tt]), sX)
        PE.wait(sX)
        PE.wait(sAC)
        for cb in range(8):
            bank = PS[cb // 4]
            PE.op(lambda eng, cb=cb, par=par, bank=bank: eng.transpose(
                bank[:, (cb % 4) * 128:(cb % 4 + 1) * 128],
                XSTG[:, par * 1024 + cb * 128: par * 1024 + (cb + 1) * 128],
                cIdF), inc=(sPE if cb == 7 else None))
        SC.wait(sPE)
        for hf in range(2):
            dst = Xv[:, hf * 4:(hf + 1) * 4, tt * 128:(tt + 1) * 128]
            SC.op(lambda eng, hf=hf, dst=dst: cpy(eng,
                dst, PS[hf].rearrange("p (a r) -> p a r", a=4)),
                inc=(sAC if hf == 1 else None))

    SY.wait(sAC)
    SY.dma(lambda eng: eng.dma_start(out=XSTG[0:40, 0:128], in_=din["a_par"]), sIN)
    PE.wait(sIN)
    PE.op(lambda eng: eng.transpose(PS[2][:, 0:128], XSTG[:, 0:128], cIdF), inc=sPE)
    SC.wait(sPE)
    SC.op(lambda eng: cpy(eng, PRAW, PS[2][:, 0:40]), inc=sAC)
    SC.op(lambda eng: eng.activation(
        out=PAR[:, 0:8], in_=PRAW[:, 0:8], func=AF.Exp))
    SC.op(lambda eng: eng.drain(fusable=False), inc=sAC)
    DV.wait(sAC)
    DV.opE(lambda eng: eng.tensor_tensor(
        out=PAR[:, 8:16], in0=PRAW[:, 8:16], in1=PAR[:, 0:8], op=ALU.subtract), sDV)
    DV.opE(lambda eng: eng.tensor_scalar(
        out=PAR[:, 16:24], in0=PAR[:, 8:16], scalar1=43.0, scalar2=None,
        op0=ALU.subtract), sDV)
    DV.opE(lambda eng: eng.tensor_scalar(
        out=PAR[:, 24:32], in0=PAR[:, 0:8], scalar1=512.0, scalar2=None,
        op0=ALU.mult), sDV)
    DV.opE(lambda eng: eng.tensor_copy(out=PARB[:, 0:24], in_=PRAW[:, 16:40]), sDV)
    DV.opE(lambda eng: eng.tensor_scalar(
        out=PARB[:, 24:48], in0=PRAW[:, 16:40],
        scalar1=-1.0, scalar2=1.0, op0=ALU.mult, op1=ALU.add), sDV)
    DV.op(lambda eng: eng.tensor_copy(out=PARF, in_=PARB), inc=sDV)

    _wn = [0]
    _wn_pe = {0: 0, 1: 0}

    def prep_weight(src_name, slot):
        par = _wn[0] % 2
        nat = WNAT if par == 0 else WNATB
        _wn[0] += 1
        if _wn_pe[par]:
            GP.wait(sPE, _wn_pe[par])
        GP.dma(lambda eng, s=src_name, nat=nat: eng.dma_start(
            out=nat.rearrange("p (r c) -> p r c", c=1024),
            in_=din[s].rearrange("(r p) c -> p r c", p=128)), sGPD)
        PE.wait(sGPD)
        slot_v = slot.rearrange("p (ct d) -> p ct d", d=1024)
        for r in range(8):
            PE.wait(sDV)
            PE.wait(sAC)
            for dblk in range(8):
                PE.op(lambda eng, r=r, dblk=dblk, nat=nat: eng.transpose(
                    PST[:, dblk * 128:(dblk + 1) * 128],
                    nat[:, r * 1024 + dblk * 128: r * 1024 + (dblk + 1) * 128],
                    cIdB), inc=(sPE if dblk == 7 else None))
            evq, evsem = (DV, sDV) if (r % 2 == 0) else (SC, sAC)
            evq.wait(sPE)
            dst = slot_v[:, :, r * 128:(r + 1) * 128]
            evq.op(lambda eng, dst=dst: cpy(
                eng, dst, PST.rearrange("p (a r2) -> p a r2", a=8)), inc=evsem)
        _wn_pe[par] = sPE.n

    def make_svec(slot_w, lhs_fn, dst_aug, dst_row, scale=-1.0):
        for hf in range(2):
            for ct in range(CT):
                mm(PS[6][0:1, 0:512], lhs_fn(ct),
                   slot_w[:, ct * 1024 + hf * 512: ct * 1024 + hf * 512 + 512],
                   start=(ct == 0), stop=(ct == CT - 1),
                   inc=(sPE if ct == CT - 1 else None))
            SC.wait(sPE)
            SC.op(lambda eng, hf=hf, dst_aug=dst_aug, dst_row=dst_row, scale=scale:
                  eng.activation(
                      out=dst_aug[dst_row:dst_row + 1, hf * 512: hf * 512 + 512],
                      in_=PS[6][0:1, 0:512], func=AF.Copy, scale=scale), inc=sAC)
            PE.wait(sAC)

    prep_weight("a_Wk", W0)
    prep_weight("a_Wv", W1)
    prep_weight("a_Wr", W2)
    prep_weight("a_Wo", W3)
    barrier()
    make_svec(W0, lambda ct: parb(0, ct), AUGA, 0)
    make_svec(W0, lambda ct: parb1m(0, ct), AUGA, 32)
    make_svec(W1, lambda ct: parb(1, ct), AUGA, 64)
    make_svec(W1, lambda ct: parb1m(1, ct), AUGB, 0)
    make_svec(W2, lambda ct: parb(2, ct), AUGB, 32)
    make_svec(W2, lambda ct: parb1m(2, ct), AUGB, 64)

    def ln_stats():
        barrier()
        DV.opE(lambda eng: eng.memset(MRB[:, 0:1], 0.0), sDV)
        for c in range(NCH):
            t0 = c * CH
            GP.wait(sAC)
            GP.dma(lambda eng, t0=t0: eng.dma_start(
                out=XBF.rearrange("p (ct j) -> p ct j", j=CH),
                in_=Xv[:, :, t0:t0 + CH]), sGPD)
            SC.wait(sGPD)
            SC.wait(sPE)
            SC.op(lambda eng: eng.activation(out=XSQ, in_=XBF, func=AF.Square),
                  inc=sAC)
            PE.wait(sGPD)
            PE.wait(sDV)
            for ct in range(CT):
                mm(PS[0], cOn[:, 0:128], XBF[:, ct * CH:(ct + 1) * CH],
                   start=(ct == 0), stop=(ct == CT - 1))
            PE.wait(sAC)
            for ct in range(CT):
                mm(PS[1], cOn[:, 0:128], XSQ[:, ct * CH:(ct + 1) * CH],
                   start=(ct == 0), stop=(ct == CT - 1),
                   inc=(sPE if ct == CT - 1 else None))
            DV.wait(sPE)
            DV.op(lambda eng: eng.tensor_scalar(
                out=MST, in0=PS[0], scalar1=1.0 / C, scalar2=None, op0=ALU.mult))
            DV.op(lambda eng: eng.tensor_tensor(
                out=VST, in0=MST, in1=MST, op=ALU.mult))
            DV.op(lambda eng: eng.scalar_tensor_tensor(
                out=VST, in0=PS[1], scalar=1.0 / C, in1=VST,
                op0=ALU.mult, op1=ALU.subtract), inc=sDV)
            SC.wait(sDV)
            SC.op(lambda eng: eng.activation(
                out=VST2, in_=VST, func=AF.Sqrt, bias=EPSB), inc=sAC)
            DV.wait(sAC)
            DV.op(lambda eng, t0=t0: eng.reciprocal(
                out=RSTD[:, t0:t0 + CH], in_=VST2))
            DV.op(lambda eng, t0=t0: eng.tensor_tensor(
                out=MRB[:, 1 + t0:1 + t0 + CH], in0=MST,
                in1=RSTD[:, t0:t0 + CH], op=ALU.mult), inc=sDV)
        barrier()

    def make_xs(c, q_=None, sem=None):
        q_ = q_ or DV
        sem = sem or sDV
        t0 = c * CH
        if c == 0:
            q_.opE(lambda eng: eng.memset(XSv[:, :, 0:1], 0.0), sem)
        else:
            q_.opE(lambda eng: eng.tensor_copy(
                out=XSv[:, :, 0:1], in_=XSv[:, :, CH:CH + 1]), sem)
        for ct in range(CT):
            q_.op(lambda eng, ct=ct, t0=t0: eng.tensor_tensor(
                out=XS[:, ct * (CH + 1) + 1: ct * (CH + 1) + 1 + CH],
                in0=X[:, ct * T + t0: ct * T + t0 + CH],
                in1=RSTD[:, t0:t0 + CH], op=ALU.mult),
                inc=(sem if ct == CT - 1 else None))

    def make_mix(m, dstbuf):
        for ct in range(CT):
            DV.op(lambda eng, ct=ct, m=m: eng.tensor_scalar(
                out=TMPG1, in0=XS[:, ct * (CH + 1) + 1: ct * (CH + 1) + 1 + CH],
                scalar1=parf(m, ct), scalar2=None, op0=ALU.mult))
            DV.op(lambda eng, ct=ct, m=m: eng.tensor_scalar(
                out=TMPG2, in0=XS[:, ct * (CH + 1): ct * (CH + 1) + CH],
                scalar1=parf1m(m, ct), scalar2=None, op0=ALU.mult))
            DV.op(lambda eng, ct=ct, dstbuf=dstbuf: eng.tensor_tensor(
                out=dstbuf[:, ct * CH:(ct + 1) * CH], in0=TMPG1, in1=TMPG2,
                op=ALU.add), inc=(sDV if ct == CT - 1 else None))

    import os
    DBG = int(os.environ.get("KDBG", "9"))
    KAUG = int(os.environ.get("KAUG", "1"))

    def attn_a():
        ln_stats()
        if DBG < 2:
            barrier()
            return
        DV.opE(lambda eng: eng.memset(PPB, -1e38), sDV)
        DV.opE(lambda eng: eng.memset(AAC, 0.0), sDV)
        DV.opE(lambda eng: eng.memset(BBC, 0.0), sDV)
        for c in range(NCH):
            t0 = c * CH
            DV.wait(sPE)
            make_xs(c)
            make_mix(2, MIXA)
            mix_r = sDV.n
            if DBG < 3:
                continue
            sig_done = []
            for i in range(CT):
                pr = PS[i % 2]
                PE.wait(sDV, mix_r)
                if i >= 2:
                    PE.wait(sAC, sig_done[i - 2])
                proj_group(pr, W2, i,
                           lambda ct: MIXA[:, ct * CH:(ct + 1) * CH],
                           (AUGB[32:33, i * 128:(i + 1) * 128],
                            MRB[32:33, 1 + t0:1 + t0 + CH]),
                           (AUGB[64:65, i * 128:(i + 1) * 128],
                            MRB[64:65, t0:t0 + CH]), inc=sPE)
                SC.wait(sPE)
                SC.op(lambda eng, i=i, pr=pr: eng.activation(
                    out=YG[:, i * CH:(i + 1) * CH], in_=pr, func=AF.Sigmoid),
                    inc=sAC)
                sig_done.append(sAC.n)
            r_all = sPE.n
            if DBG < 4:
                continue
            DV.wait(sPE, r_all)
            make_mix(0, MIXA)
            make_mix(1, MIXB)
            mix_kv = sDV.n
            wkv_end = [0, 0]
            for i in range(CT):
                pk, pv = PS[2 + (i % 2) * 2], PS[3 + (i % 2) * 2]
                PE.wait(sDV, mix_kv)
                if i >= 2:
                    PE.wait(sDV, wkv_end[i % 2])
                proj_group(pk, W0, i,
                           lambda ct: MIXA[:, ct * CH:(ct + 1) * CH],
                           (AUGA[0:1, i * 128:(i + 1) * 128],
                            MRB[0:1, 1 + t0:1 + t0 + CH]),
                           (AUGA[32:33, i * 128:(i + 1) * 128],
                            MRB[32:33, t0:t0 + CH]))
                proj_group(pv, W1, i,
                           lambda ct: MIXB[:, ct * CH:(ct + 1) * CH],
                           (AUGA[64:65, i * 128:(i + 1) * 128],
                            MRB[64:65, 1 + t0:1 + t0 + CH]),
                           (AUGB[0:1, i * 128:(i + 1) * 128],
                            MRB[0:1, t0:t0 + CH]), inc=sPE)
                kv_done = sPE.n
                if DBG < 5:
                    continue

                DV.wait(sPE, kv_done)
                DV.op(lambda eng, i=i, pk=pk: eng.scalar_tensor_tensor(
                    out=ZP[0], in0=cJ, scalar=par_w(i), in1=pk,
                    op0=ALU.mult, op1=ALU.add))
                DV.opE(lambda eng, i=i: eng.tensor_copy(
                    out=CB[:, 0:1], in_=PPB[:, i:i + 1]), sDV)
                DV.opE(lambda eng, i=i: eng.tensor_tensor_scan(
                    out=CB[:, 1:CH + 1], data0=ZP[0], data1=ZP[0],
                    initial=PPB[:, i:i + 1], op0=ALU.max, op1=ALU.max), sDV)
                DV.op(lambda eng: eng.tensor_tensor(
                    out=ZP[1], in0=CB[:, 0:CH], in1=ZP[0], op=ALU.subtract))
                DV.op(lambda eng, i=i: eng.tensor_scalar(
                    out=ZP[4], in0=ZP[1], scalar1=par_cl(i), scalar2=None,
                    op0=ALU.max))
                DV.op(lambda eng: eng.tensor_scalar(
                    out=ZP[2], in0=ZP[4], scalar1=0.0, scalar2=None,
                    op0=ALU.min))
                DV.op(lambda eng: eng.tensor_scalar(
                    out=ZP[3], in0=ZP[1], scalar1=-1.0, scalar2=0.0,
                    op0=ALU.mult, op1=ALU.min), inc=sDV)
                SC.wait(sDV)
                SC.op(lambda eng: eng.activation(out=ZP[1], in_=ZP[2], func=AF.Exp))
                SC.op(lambda eng: eng.activation(out=ZP[2], in_=ZP[3], func=AF.Exp))
                SC.op(lambda eng, i=i: eng.activation(
                    out=ZP[3], in_=ZP[4], func=AF.Exp, bias=par_upw(i),
                    scale=-1.0), inc=sAC)
                DV.wait(sAC)
                DV.op(lambda eng, pv=pv: eng.tensor_tensor(
                    out=ZP[0], in0=ZP[2], in1=pv, op=ALU.mult))
                DV.opE(lambda eng, i=i: eng.tensor_copy(
                    out=ABF[:, 0:1], in_=AAC[:, i:i + 1]), sDV)
                DV.opE(lambda eng, i=i: eng.tensor_tensor_scan(
                    out=ABF[:, 1:CH + 1], data0=ZP[1], data1=ZP[0],
                    initial=AAC[:, i:i + 1], op0=ALU.mult, op1=ALU.add), sDV)
                DV.opE(lambda eng, i=i: eng.tensor_copy(
                    out=BBF[:, 0:1], in_=BBC[:, i:i + 1]), sDV)
                DV.opE(lambda eng, i=i: eng.tensor_tensor_scan(
                    out=BBF[:, 1:CH + 1], data0=ZP[1], data1=ZP[2],
                    initial=BBC[:, i:i + 1], op0=ALU.mult, op1=ALU.add), sDV)
                DV.op(lambda eng, pv=pv: eng.tensor_tensor(
                    out=ZP[4], in0=ZP[3], in1=pv, op=ALU.mult))
                DV.op(lambda eng: eng.tensor_tensor(
                    out=ZP[0], in0=ABF[:, 0:CH], in1=ZP[4], op=ALU.add))
                DV.op(lambda eng: eng.tensor_tensor(
                    out=ZP[2], in0=BBF[:, 0:CH], in1=ZP[3], op=ALU.add),
                    inc=sDV)
                wkv_end[i % 2] = sDV.n
                DV.opE(lambda eng, i=i: eng.tensor_scalar(
                    out=PPB[:, i:i + 1], in0=CB[:, CH:CH + 1],
                    scalar1=par_w512(i), scalar2=None, op0=ALU.subtract), sDV)
                DV.opE(lambda eng, i=i: eng.tensor_copy(
                    out=AAC[:, i:i + 1], in_=ABF[:, CH:CH + 1]), sDV)
                DV.opE(lambda eng, i=i: eng.tensor_copy(
                    out=BBC[:, i:i + 1], in_=BBF[:, CH:CH + 1]), sDV)
                SC.wait(sDV)
                SC.op(lambda eng: eng.activation(out=ZP[1], in_=ZP[2], func=AF.Ln))
                SC.op(lambda eng: eng.activation(
                    out=ZP[4], in_=ZP[1], func=AF.Exp, scale=-1.0), inc=sAC)
                DV.wait(sAC)
                DV.op(lambda eng: eng.tensor_tensor(
                    out=TMPA, in0=ZP[0], in1=ZP[4], op=ALU.mult))
                DV.op(lambda eng, i=i: eng.tensor_tensor(
                    out=YG[:, i * CH:(i + 1) * CH], in0=TMPA,
                    in1=YG[:, i * CH:(i + 1) * CH], op=ALU.mult), inc=sDV)
            wkv_all = sDV.n
            if DBG < 6:
                continue
            PE.wait(sDV, wkv_all)
            for dp in range(CT):
                for ct in range(CT):
                    mm(PS[6], wsl(W3, ct, dp), YG[:, ct * CH:(ct + 1) * CH],
                       start=(ct == 0), stop=(ct == CT - 1),
                       inc=(sPE if ct == CT - 1 else None))
                DV.wait(sPE)
                DV.op(lambda eng, dp=dp, t0=t0: eng.tensor_tensor(
                    out=X[:, dp * T + t0: dp * T + t0 + CH],
                    in0=X[:, dp * T + t0: dp * T + t0 + CH], in1=PS[6],
                    op=ALU.add), inc=sDV)
                PE.wait(sDV)
        barrier()

    if n_sublayers >= 1:
        attn_a()

    def ffn_build():
        barrier()
        for dst, s in ((FP2, "cFP2"), (FP2B, "cFP2B"), (FSD, "cFSD"),
                       (FSG, "cFSG")):
            SY.dma(lambda eng, d=dst, s=s: eng.dma_start(
                out=d[0:din[s].shape[0], :], in_=din[s]), sIN)
        for dst, s in ((FG1R, "f_G1R"), (FG2R, "f_G2R"), (FG1I, "f_G1I"),
                       (FG2I, "f_G2I")):
            GP.dma(lambda eng, d=dst, s=s: eng.dma_start(
                out=d[0:din[s].shape[0], :], in_=din[s]), sGPD)
        GP.dma(lambda eng: eng.dma_start(
            out=AUGC[32:33, 0:1024], in_=din["f_bias"]), sGPD)
        PE.wait(sGPD)
        PE.wait(sIN)
        for part in range(2):
            G1 = FG1R if part == 0 else FG1I
            G2 = FG2R if part == 0 else FG2I
            for c4 in range(4):
                mm(PS[0][0:32, :], G1[0:32, c4 * 32:(c4 + 1) * 32],
                   FP2[0:32, :], start=True, stop=True, inc=sPE)
                SC.wait(sPE)
                SC.op(lambda eng, G1=G1: eng.activation(
                    out=FO1[0:32, :], in_=PS[0][0:32, :], func=AF.Copy))
                mm(PS[1][0:16, :], G2[0:16, c4 * 16:(c4 + 1) * 16],
                   FP2B[0:16, :], start=True, stop=True, inc=sPE)
                SC.wait(sPE)
                SC.op(lambda eng: eng.activation(
                    out=FO2[0:16, :], in_=PS[1][0:16, :], func=AF.Copy),
                    inc=sAC)
                PE.wait(sAC)
                for pt in range(4):
                    PE.wait(sDV)
                    mm(PS[2], FSD[0:32, pt * 128:(pt + 1) * 128], FO1[0:32, :],
                       start=True, stop=True)
                    mm(PS[3], FSG[0:16, pt * 128:(pt + 1) * 128], FO2[0:16, :],
                       start=True, stop=True, inc=sPE)
                    DV.wait(sPE)
                    DV.op(lambda eng: eng.tensor_copy(out=FEA, in_=PS[2]))
                    DV.op(lambda eng: eng.tensor_tensor(
                        out=FEA, in0=PS[3], in1=FEA, op=ALU.mult))
                    w = FWT[part * 4 + pt]
                    if c4 == 0:
                        DV.op(lambda eng, w=w: eng.tensor_copy(out=w, in_=FEA),
                              inc=sDV)
                    else:
                        DV.op(lambda eng, w=w: eng.tensor_tensor(
                            out=w, in0=w, in1=FEA, op=ALU.add), inc=sDV)
        SC.wait(sDV)
        for ct2 in range(4):
            SC.op(lambda eng, ct2=ct2: eng.activation(
                out=W0[:, ct2 * 1024: ct2 * 1024 + 512], in_=FWT[ct2],
                func=AF.Copy))
            SC.op(lambda eng, ct2=ct2: eng.activation(
                out=W0[:, ct2 * 1024 + 512: ct2 * 1024 + 1024], in_=FWT[4 + ct2],
                func=AF.Copy))
            SC.op(lambda eng, ct2=ct2: eng.activation(
                out=W0[:, (4 + ct2) * 1024: (4 + ct2) * 1024 + 512],
                in_=FWT[4 + ct2], func=AF.Copy, scale=-1.0))
            SC.op(lambda eng, ct2=ct2: eng.activation(
                out=W0[:, (4 + ct2) * 1024 + 512: (4 + ct2) * 1024 + 1024],
                in_=FWT[ct2], func=AF.Copy), inc=sAC)
        barrier()
        make_svec(W0, lambda ct: cOn[:, 0:1], AUGC, 0)
        barrier()
        SY.dma(lambda eng: eng.dma_start(out=wstash, in_=W0), sIN)

    def ffn(reload=False):
        if reload:
            barrier()
            SY.wait(sIN)
            SY.dma(lambda eng: eng.dma_start(out=W0, in_=wstash), sIN)
        ln_stats()
        if reload:
            PE.wait(sIN)
        res_done = [0, 0]
        for c in range(NCH):
            t0 = c * CH
            DV.wait(sPE)
            make_xs(c)
            xs_v = sDV.n
            PE.wait(sDV, xs_v)
            for dp in range(CT):
                psf = PS[dp % 2]
                if c * CT + dp >= 2:
                    PE.wait(sDV, res_done[dp % 2])
                proj_group(psf, W0, dp,
                           lambda ct: XS[:, ct * (CH + 1) + 1:
                                         ct * (CH + 1) + 1 + CH],
                           (AUGC[0:1, dp * 128:(dp + 1) * 128],
                            MRB[0:1, 1 + t0:1 + t0 + CH]),
                           (AUGC[32:33, dp * 128:(dp + 1) * 128],
                            cOn[32:33, 0:CH]), inc=sPE)
                DV.wait(sPE)
                DV.op(lambda eng, dp=dp, t0=t0, psf=psf: eng.tensor_tensor(
                    out=X[:, dp * T + t0: dp * T + t0 + CH],
                    in0=X[:, dp * T + t0: dp * T + t0 + CH], in1=psf,
                    op=ALU.add), inc=sDV)
                res_done[dp % 2] = sDV.n
        barrier()

    if n_sublayers >= 2:
        ffn_build()
        ffn()

    import os as _os_chk
    def load_win(src_name):
        win_v = WINB.rearrange("p (ct d) -> p ct d", d=1024)
        src_t = din[src_name].rearrange("(r p) cc -> p r cc", p=128)
        for half in range(2):
            GP.wait(sPE)
            GP.dma(lambda eng, half=half, src_t=src_t: eng.dma_start(
                out=WNAT2.rearrange("p (r cc) -> p r cc", cc=1024),
                in_=src_t[:, half * 4:(half + 1) * 4, :]), sGPD)
            PE.wait(sGPD)
            for r4 in range(4):
                r = half * 4 + r4
                PE.wait(sDV)
                PE.wait(sAC)
                for dblk in range(8):
                    PE.op(lambda eng, r4=r4, dblk=dblk: eng.transpose(
                        PST[:, dblk * 128:(dblk + 1) * 128],
                        WNAT2[:, r4 * 1024 + dblk * 128: r4 * 1024 + (dblk + 1) * 128],
                        cIdB), inc=(sPE if dblk == 7 else None))
                evq, evsem = (DV, sDV) if (r % 2 == 0) else (SC, sAC)
                evq.wait(sPE)
                dst = win_v[:, :, r * 128:(r + 1) * 128]
                evq.op(lambda eng, dst=dst: cpy(
                    eng, dst, PST.rearrange("p (a r2) -> p a r2", a=8)), inc=evsem)
        barrier()

    def kTs(dt):
        slot = W0 if dt < 4 else W1
        return slot[:, (dt % 4) * 2048:(dt % 4) * 2048 + 2048]

    def attn_b():
        ln_stats()
        GP.dma(lambda eng: eng.dma_start(
            out=AUGB[0:1, 0:1024], in_=din["b_out_b"]), sGPD)
        DV.wait(sGPD)
        for c in range(NCH):
            t0 = c * CH
            for ct in range(CT):
                DV.op(lambda eng, ct=ct, t0=t0: eng.tensor_tensor(
                    out=QTXS[:, ct * T + t0: ct * T + t0 + CH],
                    in0=X[:, ct * T + t0: ct * T + t0 + CH],
                    in1=RSTD[:, t0:t0 + CH], op=ALU.mult),
                    inc=(sDV if ct == CT - 1 else None))
        for j in range(16):
            oc = VN[j].rearrange("p (h d) -> p h d", d=65)[:, :, 64:65]
            DV.op(lambda eng, oc=oc: eng.tensor_copy(
                out=oc, in_=cOn[:, 0:16].unsqueeze(2)),
                inc=(sDV if j == 15 else None))
        barrier()
        load_win("b_wv")
        make_svec(WINB, lambda ct: cOn[:, 0:1], AUGA, 0)
        barrier()
        ev_done = [0, 0]
        for tt in range(16):
            for hf in range(2):
                idx = tt * 2 + hf
                ps_ = PS[idx % 2]
                if idx >= 2:
                    PE.wait(sAC, ev_done[idx % 2])
                for ct in range(CT):
                    mm(ps_, QTXS[:, ct * T + tt * 128: ct * T + (tt + 1) * 128],
                       WINB[:, ct * 1024 + hf * 512: ct * 1024 + hf * 512 + 512],
                       start=(ct == 0), stop=False)
                mm(ps_, MRB[0:1, 1 + tt * 128: 1 + (tt + 1) * 128],
                   AUGA[0:1, hf * 512: hf * 512 + 512],
                   start=False, stop=True, inc=sPE)
                SC.wait(sPE)
                dst = VN[tt].rearrange("p (h d) -> p h d",
                                       d=65)[:, hf * 8:(hf + 1) * 8, 0:64]
                SC.op(lambda eng, dst=dst, ps_=ps_: eng.activation(
                    out=dst, in_=ps_.rearrange("p (a b) -> p a b", a=8),
                    func=AF.Copy), inc=sAC)
                ev_done[idx % 2] = sAC.n
        barrier()
        load_win("b_wk")
        make_svec(WINB, lambda ct: cOn[:, 0:1], AUGA, 32)
        barrier()
        kv_ev = [0, 0, 0, 0]
        for c in range(NCH):
            t0 = c * CH
            for dt in range(CT):
                ps_ = PS[dt % 4]
                if kv_ev[dt % 4]:
                    PE.wait(sAC, kv_ev[dt % 4])
                for ct in range(CT):
                    mm(ps_, WINB[:, ct * 1024 + dt * 128: ct * 1024 + (dt + 1) * 128],
                       QTXS[:, ct * T + t0: ct * T + t0 + CH],
                       start=(ct == 0), stop=False)
                mm(ps_, AUGA[32:33, dt * 128:(dt + 1) * 128],
                   MRB[32:33, 1 + t0:1 + t0 + CH], start=False, stop=True, inc=sPE)
                SC.wait(sPE)
                SC.op(lambda eng, dt=dt, t0=t0, ps_=ps_: eng.activation(
                    out=kTs(dt)[:, t0:t0 + CH], in_=ps_, func=AF.Copy), inc=sAC)
                kv_ev[dt % 4] = sAC.n
            DV.wait(sAC)
            for hc in range(2):
                th = t0 + hc * 256
                GP.wait(sDV)
                GP.dma(lambda eng, th=th: eng.dma_start(
                    out=ROPE.rearrange("p (dt j) -> p dt j", j=256),
                    in_=din["cCos"].rearrange("p (dt t) -> p dt t", t=T)[:, :, th:th + 256]), sGPD)
                GP.dma(lambda eng, th=th: eng.dma_start(
                    out=RTS.rearrange("p (dt j) -> p dt j", j=256),
                    in_=din["cSin"].rearrange("p (dt t) -> p dt t", t=T)[:, :, th:th + 256]), sGPD)
                DV.wait(sGPD)
                for dt in range(4):
                    top = kTs(dt)[:, th:th + 256]
                    bot = kTs(dt + 4)[:, th:th + 256]
                    cosb = ROPE[:, dt * 256:(dt + 1) * 256]
                    sinb = RTS[:, dt * 256:(dt + 1) * 256]
                    DV.op(lambda eng, top=top, cosb=cosb: eng.tensor_tensor(
                        out=TMPC, in0=top, in1=cosb, op=ALU.mult))
                    DV.op(lambda eng, top=top, sinb=sinb: eng.tensor_tensor(
                        out=TMPD, in0=top, in1=sinb, op=ALU.mult))
                    DV.op(lambda eng, top=top, bot=bot, sinb=sinb: eng.tensor_tensor(
                        out=top, in0=bot, in1=sinb, op=ALU.mult))
                    DV.op(lambda eng, top=top: eng.tensor_tensor(
                        out=top, in0=TMPC, in1=top, op=ALU.subtract))
                    DV.op(lambda eng, bot=bot, cosb=cosb: eng.tensor_tensor(
                        out=TMPC, in0=bot, in1=cosb, op=ALU.mult))
                    DV.op(lambda eng, bot=bot: eng.tensor_tensor(
                        out=bot, in0=TMPD, in1=TMPC, op=ALU.add),
                        inc=(sDV if dt == 3 else None))
        barrier()
        load_win("b_wq")
        make_svec(WINB, lambda ct: cOn[:, 0:1], AUGA, 64)
        barrier()
        for c in range(NCH):
            t0 = c * CH
            for hc in range(2):
                th = t0 + hc * 256
                GP.wait(sDV)
                GP.dma(lambda eng, th=th: eng.dma_start(
                    out=ROPE.rearrange("p (dt j) -> p dt j", j=256),
                    in_=din["cCos"].rearrange("p (dt t) -> p dt t", t=T)[:, :, th:th + 256]), sGPD)
                GP.dma(lambda eng, th=th: eng.dma_start(
                    out=RTS.rearrange("p (dt j) -> p dt j", j=256),
                    in_=din["cSin"].rearrange("p (dt t) -> p dt t", t=T)[:, :, th:th + 256]), sGPD)
                qev = [0, 0]
                for dt in range(CT):
                    ps_ = PS[dt % 2][:, 0:256]
                    PE.wait(sDV)
                    if qev[dt % 2]:
                        PE.wait(sAC, qev[dt % 2])
                    for ct in range(CT):
                        mm(ps_, WINB[:, ct * 1024 + dt * 128: ct * 1024 + (dt + 1) * 128],
                           QTXS[:, ct * T + th: ct * T + th + 256],
                           start=(ct == 0), stop=False)
                    mm(ps_, AUGA[64:65, dt * 128:(dt + 1) * 128],
                       MRB[64:65, 1 + th:1 + th + 256], start=False, stop=True,
                       inc=sPE)
                    SC.wait(sPE)
                    SC.op(lambda eng, dt=dt, ps_=ps_: eng.activation(
                        out=QP8[:, dt * 256:(dt + 1) * 256], in_=ps_,
                        func=AF.Copy), inc=sAC)
                    qev[dt % 2] = sAC.n
                DV.wait(sAC)
                DV.wait(sGPD)
                for dt in range(4):
                    topo = QTXS[:, dt * T + th: dt * T + th + 256]
                    boto = QTXS[:, (dt + 4) * T + th:(dt + 4) * T + th + 256]
                    qpa = QP8[:, dt * 256:(dt + 1) * 256]
                    qpb = QP8[:, (dt + 4) * 256:(dt + 5) * 256]
                    cosb = ROPE[:, dt * 256:(dt + 1) * 256]
                    sinb = RTS[:, dt * 256:(dt + 1) * 256]
                    DV.op(lambda eng, qpa=qpa, cosb=cosb: eng.tensor_tensor(
                        out=TMPC, in0=qpa, in1=cosb, op=ALU.mult))
                    DV.op(lambda eng, qpb=qpb, sinb=sinb: eng.tensor_tensor(
                        out=TMPD, in0=qpb, in1=sinb, op=ALU.mult))
                    DV.op(lambda eng, topo=topo: eng.tensor_tensor(
                        out=topo, in0=TMPC, in1=TMPD, op=ALU.subtract))
                    DV.op(lambda eng, qpa=qpa, sinb=sinb: eng.tensor_tensor(
                        out=TMPC, in0=qpa, in1=sinb, op=ALU.mult))
                    DV.op(lambda eng, qpb=qpb, cosb=cosb: eng.tensor_tensor(
                        out=TMPD, in0=qpb, in1=cosb, op=ALU.mult))
                    DV.op(lambda eng, boto=boto: eng.tensor_tensor(
                        out=boto, in0=TMPC, in1=TMPD, op=ALU.add),
                        inc=(sDV if dt == 3 else None))
        barrier()
        if os.environ.get("KDUMP") == "qt":
            SY.dma(lambda eng: eng.dma_start(out=dbg, in_=QTXS), sIN)
            barrier()
        if os.environ.get("KDUMP") == "kt":
            SY.dma(lambda eng: eng.dma_start(out=dbg[:, 0:8192], in_=W0), sIN)
            SY.dma(lambda eng: eng.dma_start(out=dbg[:, 8192:16384], in_=W1), sIN)
            barrier()
        if os.environ.get("KDUMP") == "vn":
            for j in range(16):
                SY.dma(lambda eng, j=j: eng.dma_start(
                    out=dbg[:, j * 1024:(j + 1) * 1024], in_=VN[j][:, 0:1024]), sIN)
            barrier()
        load_win("b_out_w")
        barrier()
        for g in range(NCH):
            t1c = g * 512
            tr_pe = [0]
            ts_dv = [0]
            pending_norm = [None]
            for h in range(H):
                dtq = h // 2
                pbase = (h % 2) * 64
                qsl = QTXS[pbase:pbase + 64, dtq * T + t1c: dtq * T + t1c + 512]
                gran = ALIBI_GRAN[h]
                dmin = -(16 - gran // 128)
                nj = 4 * g + 4
                exp_ac = {}
                mask_dv = {}
                pv_arr = {}

                def emit_qk(j):
                    psc = PS[j % 2]
                    if j >= 2:
                        PE.wait(sAC, exp_ac[j - 2])
                    else:
                        PE.wait(sAC)
                    ksl = kTs(dtq)[pbase:pbase + 64, j * 128:(j + 1) * 128]
                    mm(psc, ksl, qsl, start=True, stop=True, inc=sPE)
                    return sPE.n

                def emit_exp(j, qk_v):
                    psc = PS[j % 2]
                    PT = PTD[j % 2]
                    SC.wait(sPE, qk_v)
                    if j >= 2:
                        SC.wait(sPE, pv_arr[j - 2])
                    if j - 1 in mask_dv:
                        SC.wait(sDV, mask_dv[j - 1])
                    nsub = 512 // gran
                    for s2 in range(nsub):
                        c0 = s2 * gran
                        c1 = c0 + gran
                        dlo = j * 128 - t1c
                        if c1 <= dlo:
                            SC.op(lambda eng, c0=c0, c1=c1, PT=PT: eng.activation(
                                out=PT[:, c0:c1], in_=cZe[:, c0:c1],
                                func=AF.Copy))
                            continue
                        blk0 = (t1c + s2 * gran) // gran
                        didx = j - (gran // 128) * blk0 - dmin
                        bias = cBt[:, h * 16 + didx: h * 16 + didx + 1]
                        if c0 <= dlo < c1:
                            if dlo > c0:
                                SC.op(lambda eng, c0=c0, dlo=dlo, PT=PT:
                                      eng.activation(
                                          out=PT[:, c0:dlo],
                                          in_=cZe[:, c0:dlo], func=AF.Copy))
                            SC.op(lambda eng, psc=psc, dlo=dlo, bias=bias:
                                  eng.activation(
                                      out=PSC, in_=psc[:, dlo:dlo + 128],
                                      func=AF.Exp, scale=0.125, bias=bias),
                                  inc=sAC)
                            DV.wait(sAC)
                            DV.op(lambda eng, dlo=dlo, PT=PT: eng.tensor_tensor(
                                out=PT[:, dlo:dlo + 128], in0=PSC, in1=cDg,
                                op=ALU.mult), inc=sDV)
                            if dlo + 128 < c1:
                                SC.op(lambda eng, psc=psc, dlo=dlo, c1=c1,
                                      bias=bias, PT=PT: eng.activation(
                                    out=PT[:, dlo + 128:c1],
                                    in_=psc[:, dlo + 128:c1],
                                    func=AF.Exp, scale=0.125, bias=bias))
                        elif c0 > dlo:
                            SC.op(lambda eng, psc=psc, c0=c0, c1=c1, bias=bias,
                                  PT=PT: eng.activation(
                                      out=PT[:, c0:c1], in_=psc[:, c0:c1],
                                      func=AF.Exp, scale=0.125, bias=bias))
                    exp_ac[j] = sAC.n
                    mask_dv[j] = sDV.n

                def emit_pv(j):
                    PT = PTD[j % 2]
                    if j == 0 and ts_dv[0]:
                        PE.wait(sDV, ts_dv[0])
                    PE.wait(sAC, exp_ac[j])
                    PE.wait(sDV, mask_dv[j])
                    for i2 in range(4):
                        mm(PS[6][:, i2 * 65:(i2 + 1) * 65],
                           PT[:, i2 * 128:(i2 + 1) * 128],
                           VN[j][:, 65 * h:65 * h + 65],
                           start=(j == 0 and i2 == 0),
                           stop=(j == nj - 1 and i2 == 3),
                           inc=(sPE if i2 == 3 else None))
                    pv_arr[j] = sPE.n

                for j in range(nj):
                    qkv = emit_qk(j)
                    emit_exp(j, qkv)
                    if j == 1 and pending_norm[0] is not None:
                        pending_norm[0]()
                        pending_norm[0] = None
                    if j >= 1:
                        emit_pv(j - 1)
                if pending_norm[0] is not None:
                    pending_norm[0]()
                    pending_norm[0] = None
                emit_pv(nj - 1)
                pv_pe = pv_arr[nj - 1]
                def make_norm(h=h, dtq=dtq, pbase=pbase,
                              pv_arr=pv_arr, nj=nj):
                    pv_pe = pv_arr[nj - 1]
                    DV.wait(sPE, pv_pe)
                    if ts_dv[0]:
                        DV.wait(sDV, ts_dv[0])
                    zcols = PS[6][:, 0:260].rearrange(
                        "p (i c) -> p i c", c=65)[:, :, 64:65]
                    DV.op(lambda eng, zcols=zcols, h=h: eng.tensor_tensor(
                        out=RCB[:, 0:4].unsqueeze(2), in0=zcols,
                        in1=cCV[:, h * 4:h * 4 + 4].unsqueeze(2), op=ALU.mult))
                    DV.op(lambda eng: eng.drain(fusable=False), inc=sDV)
                    SC.wait(sDV)
                    SC.op(lambda eng: eng.activation(
                        out=RCB[:, 4:8], in_=RCB[:, 0:4], func=AF.Ln))
                    SC.op(lambda eng: eng.activation(
                        out=RCB[:, 8:12], in_=RCB[:, 4:8], func=AF.Exp, scale=-1.0))
                    SC.op(lambda eng: eng.drain(fusable=False), inc=sAC)
                    DV.wait(sAC)
                    DV.op(lambda eng, h=h: eng.tensor_tensor(
                        out=RCB[:, 8:12], in0=RCB[:, 8:12],
                        in1=cCV[:, h * 4:h * 4 + 4], op=ALU.mult), inc=sDV)
                    DV.wait(sDV)
                    for i2 in range(4):
                        DV.op(lambda eng, i2=i2: eng.tensor_scalar(
                            out=STG64[:, i2 * 64:(i2 + 1) * 64],
                            in0=PS[6][:, i2 * 65: i2 * 65 + 64],
                            scalar1=RCB[:, 8 + i2:9 + i2], scalar2=None,
                            op0=ALU.mult), inc=(sDV if i2 == 3 else None))
                    if (os.environ.get("KDUMP") == "stg" and g == 0
                            and h == int(os.environ.get("KH", "2"))):
                        barrier()
                        SY.dma(lambda eng: eng.dma_start(
                            out=dbg[:, 0:256], in_=STG64), sIN)
                        SY.dma(lambda eng: eng.dma_start(
                            out=dbgf[:, 0:8], in_=RCB), sIN)
                        barrier()
                    PE.wait(sDV)
                    for i2 in range(4):
                        PE.op(lambda eng, i2=i2: eng.transpose(
                            PST[0:64, i2 * 128:(i2 + 1) * 128],
                            STG64[:, i2 * 64:(i2 + 1) * 64], cIdB),
                            inc=(sPE if i2 == 3 else None))
                    tr_pe[0] = sPE.n
                    SC.wait(sPE)
                    SC.op(lambda eng, pbase=pbase, dtq=dtq: eng.activation(
                        out=OTB[pbase:pbase + 64, dtq * 512:(dtq + 1) * 512],
                        in_=PST[0:64, 0:512], func=AF.Copy), inc=sAC)
                    PE.wait(sAC)
                pending_norm[0] = make_norm

            if pending_norm[0] is not None:
                pending_norm[0]()
                pending_norm[0] = None
            if g == 0 and os.environ.get("KDUMP") == "otb":
                barrier()
                SY.dma(lambda eng: eng.dma_start(out=dbg[:, 0:4096], in_=OTB), sIN)
                barrier()
            op_res = [0, 0]
            for dp in range(CT):
                pso = PS[2 + dp % 2]
                if dp >= 2:
                    PE.wait(sDV, op_res[dp % 2])
                for ct in range(CT):
                    mm(pso, WINB[:, ct * 1024 + dp * 128: ct * 1024 + (dp + 1) * 128],
                       OTB[:, ct * 512:(ct + 1) * 512],
                       start=(ct == 0), stop=False)
                mm(pso, AUGB[0:1, dp * 128:(dp + 1) * 128], cOn[0:1, 0:512],
                   start=False, stop=True, inc=sPE)
                DV.wait(sPE)
                DV.op(lambda eng, dp=dp, t1c=t1c, pso=pso: eng.tensor_tensor(
                    out=X[:, dp * T + t1c: dp * T + t1c + 512],
                    in0=X[:, dp * T + t1c: dp * T + t1c + 512], in1=pso,
                    op=ALU.add), inc=sDV)
                op_res[dp % 2] = sDV.n
        barrier()

    if n_sublayers >= 3:
        attn_b()
    if n_sublayers >= 4:
        ffn(reload=True)

    barrier()
    for tt in range(16):
        par = tt % 2
        sS = sSA if par == 0 else sSB
        if tt >= 2:
            PE.wait(sS, 16 * (tt // 2))
        PE.wait(sAC)
        for hf in range(2):
            bank = PS[hf]
            for cq in range(4):
                cb = hf * 4 + cq
                PE.op(lambda eng, cb=cb, cq=cq, tt=tt, bank=bank: eng.transpose(
                    bank[:, cq * 128:(cq + 1) * 128],
                    X[:, cb * T + tt * 128: cb * T + (tt + 1) * 128],
                    cIdF), inc=(sPE if cq == 3 else None))
            SC.wait(sPE)
            SC.op(lambda eng, hf=hf, par=par, bank=bank: cpy(eng,
                OST[:, par * 1024 + hf * 512: par * 1024 + hf * 512 + 512],
                bank), inc=sAC)
        SY.wait(sAC)
        SY.dma(lambda eng, tt=tt, par=par: eng.dma_start(
            out=out[tt * 128:(tt + 1) * 128, :],
            in_=OST[:, par * 1024:(par + 1) * 1024]), sS)
    SY.wait(sSA)
    SY.wait(sSB)

    def body(qname):
        def run(eng):
            for item in q[qname].items:
                if item[0] == "wait":
                    eng.wait_ge(item[1].h, item[2])
                else:
                    inst = item[1](eng)
                    for s, n in item[2]:
                        inst.then_inc(s.h, n)
        return run
    blk.sync(body("sync"))
    blk.gpsimd(body("gpsimd"))
    blk.vector(body("vector"))
    blk.scalar(body("scalar"))
    blk.tensor(body("tensor"))
    blk_ctx.__exit__(None, None, None)
    for ctx in sem_ctx:
        ctx.__exit__(None, None, None)
    return nc


_consts_cache = None

def kernel(L0_attn_a_x, L0_attn_a_time_decay, L0_attn_a_time_first,
           L0_attn_a_time_mix_k, L0_attn_a_time_mix_v, L0_attn_a_time_mix_r,
           L0_attn_a_Wk, L0_attn_a_Wv, L0_attn_a_Wr, L0_attn_a_Wo,
           L0_attn_b_wq, L0_attn_b_wk, L0_attn_b_wv, L0_attn_b_out_w,
           L0_attn_b_out_b, L0_ffn_G1_R, L0_ffn_G2_R, L0_ffn_G1_I, L0_ffn_G2_I,
           L0_ffn_bias_real, L0_ffn_bias_imag, _sublayers=4):
    global _consts_cache
    if _consts_cache is None:
        _consts_cache = host_constants()
    f32 = np.float32
    x = np.ascontiguousarray(np.asarray(L0_attn_a_x, f32))
    par = np.ascontiguousarray(np.stack([
        np.asarray(L0_attn_a_time_decay, f32).reshape(8, 128),
        np.asarray(L0_attn_a_time_first, f32).reshape(8, 128),
        np.asarray(L0_attn_a_time_mix_k, f32).reshape(8, 128),
        np.asarray(L0_attn_a_time_mix_v, f32).reshape(8, 128),
        np.asarray(L0_attn_a_time_mix_r, f32).reshape(8, 128)],
        axis=0).reshape(40, 128))
    g2r = np.ascontiguousarray(np.asarray(L0_ffn_G2_R, f32).reshape(4, 16, 16)
                               .transpose(1, 0, 2).reshape(16, 64))
    g2i = np.ascontiguousarray(np.asarray(L0_ffn_G2_I, f32).reshape(4, 16, 16)
                               .transpose(1, 0, 2).reshape(16, 64))
    base = {
        "a_Wk": np.ascontiguousarray(np.asarray(L0_attn_a_Wk, f32)),
        "a_Wv": np.ascontiguousarray(np.asarray(L0_attn_a_Wv, f32)),
        "a_Wr": np.ascontiguousarray(np.asarray(L0_attn_a_Wr, f32)),
        "a_Wo": np.ascontiguousarray(np.asarray(L0_attn_a_Wo, f32)),
        "b_wq": np.ascontiguousarray(np.asarray(L0_attn_b_wq, f32)),
        "b_wk": np.ascontiguousarray(np.asarray(L0_attn_b_wk, f32)),
        "b_wv": np.ascontiguousarray(np.asarray(L0_attn_b_wv, f32)),
        "b_out_w": np.ascontiguousarray(np.asarray(L0_attn_b_out_w, f32)),
        "a_par": par,
        "b_out_b": np.ascontiguousarray(np.asarray(L0_attn_b_out_b, f32))[None, :],
        "f_G1R": np.ascontiguousarray(np.asarray(L0_ffn_G1_R, f32).reshape(32, 128)),
        "f_G2R": g2r,
        "f_G1I": np.ascontiguousarray(np.asarray(L0_ffn_G1_I, f32).reshape(32, 128)),
        "f_G2I": g2i,
        "f_bias": np.concatenate([np.asarray(L0_ffn_bias_real, f32),
                                  np.asarray(L0_ffn_bias_imag, f32)])[None, :],
    }
    base.update(_consts_cache)
    nc = build(n_sublayers=_sublayers)
    in_maps = []
    for b in range(8):
        m = dict(base)
        m["xin"] = np.ascontiguousarray(x[b])
        in_maps.append(m)
    res = run_bass_kernel_spmd(nc, in_maps, core_ids=list(range(8)))
    return np.stack([np.asarray(res.results[b]["out"]) for b in range(8)], axis=0)


# version 10
# speedup vs baseline: 1.7209x; 1.0044x over previous
import numpy as np
import ml_dtypes
import concourse.bass as bass
import concourse.mybir as mybir
import bass_rust
from concourse.bass_utils import run_bass_kernel_spmd

F32 = mybir.dt.float32
BF16 = mybir.dt.bfloat16
AF = mybir.ActivationFunctionType
ALU = mybir.AluOpType
BF_NP = ml_dtypes.bfloat16

T = 2048
C = 1024
CT = 8
CH = 512
NCH = T // CH
H = 16
SHIFT = 64.0
EPS = 1e-5
ALIBI_GRAN = [128 if h < 2 else (256 if h < 4 else 512) for h in range(H)]


def host_constants():
    c = {}
    c["cJ"] = np.repeat((np.arange(CH, dtype=np.float32) + 1.0)[None, :], 128, 0)
    c["cIdentF"] = np.eye(128, dtype=np.float32)
    c["cIdentB"] = np.eye(128, dtype=BF_NP)
    c["cDiag"] = (np.arange(128)[None, :] >= np.arange(128)[:, None]).astype(BF_NP)
    c["cOnes"] = np.ones((128, 512), dtype=BF_NP)
    slopes = np.array([2.0 ** (-8.0 * (i + 1) / H) for i in range(H)], np.float64)
    bt = np.zeros((128, 256), np.float32)
    p = np.arange(128, dtype=np.float64)
    for h in range(H):
        gran = ALIBI_GRAN[h]
        dmin = -(16 - gran // 128)
        for idx in range(16):
            bt[:, h * 16 + idx] = (slopes[h] * p + slopes[h] * 128.0 * (dmin + idx)
                                   - SHIFT).astype(np.float32)
    c["cBias"] = bt
    cv = np.zeros((128, 64), np.float32)
    pq = np.arange(128, dtype=np.float64)
    for h in range(H):
        gran = ALIBI_GRAN[h]
        for i2 in range(4):
            fp = (128.0 * i2 + pq) % gran
            cv[:, h * 4 + i2] = np.exp(64.0 - slopes[h] * fp).astype(np.float32)
    c["cCVT"] = cv
    p2 = np.zeros((32, 512), BF_NP)
    for b in range(32):
        p2[b, b * 16:(b + 1) * 16] = 1
    c["cFP2"] = p2
    p2b = np.zeros((16, 512), BF_NP)
    for e in range(16):
        p2b[e, e::16] = 1
    c["cFP2B"] = p2b
    fsd = np.zeros((32, 512), BF_NP)
    for p in range(512):
        fsd[p // 16, p] = 1
    c["cFSD"] = fsd
    fsg = np.zeros((16, 512), BF_NP)
    for p in range(512):
        fsg[p % 16, p] = 1
    c["cFSG"] = fsg
    half = 512
    freqs = 1.0 / 10000.0 ** (np.arange(half, dtype=np.float32) / half)
    ang = np.arange(T, dtype=np.float32)[None, :] * freqs[:, None]
    c["cCos"] = np.cos(ang).astype(np.float32).reshape(4, 128, T).transpose(
        1, 0, 2).reshape(128, 4 * T).astype(BF_NP)
    c["cSin"] = np.sin(ang).astype(np.float32).reshape(4, 128, T).transpose(
        1, 0, 2).reshape(128, 4 * T).astype(BF_NP)
    return c


class Sem:
    def __init__(self, handle, name):
        self.h = handle
        self.name = name
        self.n = 0


class EQ:
    def __init__(self, name):
        self.name = name
        self.items = []
        self.auto_edge = False
        self.esem = None

    def wait(self, sem, val=None):
        v = sem.n if val is None else val
        if v > 0:
            self.items.append(("wait", sem, v))

    def op(self, fn, inc=None, inc_n=1):
        incs = []
        if inc is not None:
            inc.n += inc_n
            incs.append((inc, inc_n))
        if self.auto_edge and inc is not self.esem:
            self.esem.n += 1
            incs.append((self.esem, 1))
        self.items.append(("op", fn, incs))
        return incs

    def dma(self, fn, sem):
        return self.op(fn, inc=sem, inc_n=16)

    def opE(self, fn, esem):
        self.op(fn, inc=esem)
        self.wait(esem)

    def last_op_incs(self):
        for it in reversed(self.items):
            if it[0] == "op":
                return it[2]
        return None


def build(n_sublayers=4):
    nc = bass.Bass("TRN2", target_bir_lowering=False)
    import os as _os
    if not int(_os.environ.get("KRACE", "0")):
        nc.detect_race_conditions = False

    din = {}
    def dram_in(name, shape, dtype=F32):
        din[name] = nc.dram_tensor(name, shape, dtype, kind="ExternalInput").ap()
        return din[name]

    xin = dram_in("xin", [T, C])
    for nm in ["a_Wk", "a_Wv", "a_Wr", "a_Wo", "b_wq", "b_wk", "b_wv", "b_out_w"]:
        dram_in(nm, [C, C])
    dram_in("a_par", [40, 128])
    dram_in("b_out_b", [1, C])
    dram_in("f_G1R", [32, 128])
    dram_in("f_G2R", [16, 64])
    dram_in("f_G1I", [32, 128])
    dram_in("f_G2I", [16, 64])
    dram_in("f_bias", [1, C])
    dram_in("cJ", [128, CH])
    dram_in("cIdentF", [128, 128])
    dram_in("cIdentB", [128, 128], BF16)
    dram_in("cDiag", [128, 128], BF16)
    dram_in("cOnes", [128, 512], BF16)
    dram_in("cBias", [128, 256])
    dram_in("cCVT", [128, 64])
    dram_in("cCos", [128, 4 * T], BF16)
    dram_in("cFP2", [32, 512], BF16)
    dram_in("cFP2B", [16, 512], BF16)
    dram_in("cFSD", [32, 512], BF16)
    dram_in("cFSG", [16, 512], BF16)
    dram_in("cSin", [128, 4 * T], BF16)
    out = nc.dram_tensor("out", [T, C], F32, kind="ExternalOutput").ap()
    dbg = nc.dram_tensor("dbg", [128, 16384], BF16, kind="ExternalOutput").ap()
    dbgf = nc.dram_tensor("dbgf", [128, 512], F32, kind="ExternalOutput").ap()
    wstash = nc.dram_tensor("wstash", [128, 8192], BF16, kind="Internal").ap()

    tens = {}
    def sb(name, shape, dtype, offset):
        t = nc.alloc_sbuf_tensor_at(name, shape, dtype, offset=offset)
        tens[name] = t.ap()
        return tens[name]

    SB_BASE = 16512
    OFF = SB_BASE
    def place(name, shape, dtype):
        nonlocal OFF
        szb = {F32: 4, BF16: 2}[dtype]
        w = szb
        for d in shape[1:]:
            w *= d
        at = OFF
        OFF = at + ((w + 63) // 64) * 64
        return sb(name, shape, dtype, at), at

    X, _ = place("X", [128, CT * T], F32)
    cJ, _ = place("sJ", [128, CH], F32)
    cIdF, _ = place("sIdF", [128, 128], F32)
    cIdB, _ = place("sIdB", [128, 128], BF16)
    cDg, _ = place("sDiag", [128, 128], BF16)
    cOn, _ = place("sOnes", [128, 512], BF16)
    cBt, _ = place("sBias", [128, 256], F32)
    cCV, _ = place("sCVT", [128, 64], F32)
    PAR, _ = place("PAR", [128, 32], F32)
    PARB, _ = place("PARB", [128, 48], BF16)
    PARF, _ = place("PARF", [128, 48], F32)
    AUGA, _ = place("AUGA", [128, 1024], BF16)
    AUGB, _ = place("AUGB", [128, 1024], BF16)
    AUGC, _ = place("AUGC", [128, 1024], BF16)
    EPSB, _ = place("EPSB", [128, 1], F32)
    PRAW, _ = place("PRAW", [128, 40], F32)
    PPB, _ = place("PPB", [128, 8], F32)
    AAC, _ = place("AAC", [128, 8], F32)
    BBC, _ = place("BBC", [128, 8], F32)
    MRB, MRB_at = place("MRB", [128, T + 1], BF16)
    RSTD, RSTD_at = place("RSTD", [128, T], F32)
    XS, XS_at = place("XS", [128, CT * (CH + 1)], BF16)
    W0, W0_at = place("W0", [128, 8192], BF16)
    W1, W1_at = place("W1", [128, 8192], BF16)
    W2, W2_at = place("W2", [128, 8192], BF16)
    W3, W3_at = place("W3", [128, 8192], BF16)
    VAR_at = OFF
    SB_LIMIT = 229312
    assert VAR_at + 47000 <= SB_LIMIT, VAR_at

    V = VAR_at
    XSTG = sb("XSTG", [128, 3072], F32, V)
    WNAT = sb("WNAT", [128, 8192], BF16, V + 12288)
    WNATB = sb("WNATB", [128, 8192], BF16, V + 28672)
    XBF = sb("XBF", [128, CT * CH], BF16, V)
    XSQ = sb("XSQ", [128, CT * CH], BF16, V + 8192)
    MST = sb("MST", [128, CH], F32, V + 16384)
    VST = sb("VST", [128, CH], F32, V + 18432)
    VST2 = sb("VST2", [128, CH], F32, V + 20480)
    MIXA = sb("MIXA", [128, CT * CH], BF16, V)
    MIXB = sb("MIXB", [128, CT * CH], BF16, V + 8192)
    YG = sb("YG", [128, CT * CH], BF16, V + 16384)
    ZP = [sb(f"ZP{i}", [128, CH], F32, V + 24576 + i * 2048) for i in range(5)]
    CB = sb("CB", [128, CH + 1], F32, V + 34816)
    ABF = sb("ABF", [128, CH + 1], F32, V + 36992)
    BBF = sb("BBF", [128, CH + 1], F32, V + 39168)
    TMPA = sb("TMPA", [128, CH], BF16, V + 41344)
    TMPB = sb("TMPB", [128, CH], BF16, V + 42368)
    TMPG1 = sb("TMPG1", [128, CH], BF16, V + 43392)
    TMPG2 = sb("TMPG2", [128, CH], BF16, V + 44416)
    assert V + 45440 <= SB_LIMIT, V
    FO1 = sb("FO1", [128, 512], BF16, V)
    FEA = sb("FEA", [128, 512], F32, V + 1024)
    FG1R = sb("FG1R", [128, 128], BF16, V + 3072)
    FG2R = sb("FG2R", [128, 64], BF16, V + 3328)
    FG1I = sb("FG1I", [128, 128], BF16, V + 3456)
    FG2I = sb("FG2I", [128, 64], BF16, V + 3712)
    FP2 = sb("FP2", [128, 512], BF16, V + 3840)
    FP2B = sb("FP2B", [128, 512], BF16, V + 4864)
    FSD = sb("FSD", [128, 512], BF16, V + 5888)
    FSG = sb("FSG", [128, 512], BF16, V + 6912)
    FWT = [sb(f"FWT{i}", [128, 512], F32, V + 7936 + 2048 * i) for i in range(8)]
    QTXS = sb("QTXS", [128, CT * T], BF16, V)
    WINB = sb("WINB", [128, 8192], BF16, RSTD_at)
    VN = []
    for j in range(16):
        if j < 7:
            at = W2_at + j * 2112
        elif j < 14:
            at = W3_at + (j - 7) * 2112
        else:
            at = V + 32768 + (j - 14) * 2112
        VN.append(sb(f"VN{j}", [128, 1040], BF16, at))
    BW2 = V + 32768 + 4608
    ROPE = sb("ROPEB", [128, 1024], BF16, BW2)
    RTS = sb("RTS", [128, 1024], BF16, BW2 + 2048)
    QP8 = sb("QP8", [128, 2048], BF16, BW2 + 4096)
    TMPC = sb("TMPC", [128, 256], BF16, BW2 + 8192)
    TMPD = sb("TMPD", [128, 256], BF16, BW2 + 8704)
    OTB = sb("OTB", [128, CT * 512], BF16, BW2)
    WNAT2 = sb("WNAT2", [128, 4096], BF16, BW2)
    FO2 = sb("FO2", [128, 512], BF16, BW2 + 2048)
    PT0 = sb("PT0", [128, 512], BF16, MRB_at)
    PT1 = sb("PT1", [128, 512], BF16, MRB_at + 1024)
    PTD = [PT0, PT1]
    PSC = sb("PSC", [128, 128], BF16, MRB_at + 2048)
    STG64 = sb("STG64", [128, 256], BF16, MRB_at + 2304)
    RCB = sb("RCB", [128, 12], F32, MRB_at + 2816)
    LNZ = sb("LNZ", [128, 256], F32, MRB_at + 3072)
    OST = sb("OST", [128, 2 * 1024], F32, V)

    PS = [nc.alloc_psum_tensor(f"ps{i}", [128, 512], F32).ap() for i in range(7)]
    PST = nc.alloc_psum_tensor("pst", [128, 1024], BF16).ap()

    sem_names = ["sIN", "sGPD", "sPE", "sDV", "sAC", "sGE", "sSY", "sST",
                 "sCN", "sXA", "sXB", "sSA", "sSB"]
    sem_ctx = [nc.semaphore(n) for n in sem_names]
    handles = [ctx.__enter__() for ctx in sem_ctx]
    sems = {n: Sem(h, n) for n, h in zip(sem_names, handles)}
    sIN, sGPD, sPE, sDV, sAC, sGE, sSY, sST, sCN, sXA, sXB, sSA, sSB = (
        sems[n] for n in sem_names)
    all_sems = list(sems.values())
    esem = {"sync": sSY, "gpsimd": sGE, "vector": sDV, "scalar": sAC, "tensor": sPE}
    q = {n: EQ(n) for n in ("sync", "gpsimd", "vector", "scalar", "tensor")}
    for n in ("vector", "scalar"):
        q[n].auto_edge = True
        q[n].esem = esem[n]
    SY, GP, DV, SC, PE = (q[n] for n in ("sync", "gpsimd", "vector", "scalar", "tensor"))

    def barrier():
        for en, qq in q.items():
            incs = qq.last_op_incs()
            if incs is not None and len(incs) == 0:
                esem[en].n += 1
                incs.append((esem[en], 1))
        snap = [(s, s.n) for s in all_sems if s.n > 0]
        for qq in q.values():
            for s, v in snap:
                qq.wait(s, v)

    blk_ctx = nc.Block()
    blk = blk_ctx.__enter__()

    def cpy(eng, out, in_):
        if hasattr(eng, "tensor_copy"):
            return eng.tensor_copy(out=out, in_=in_)
        return eng.copy(out=out, in_=in_)

    def mm(psum, lhsT, rhs, start, stop, inc=None):
        PE.op(lambda eng, a=psum, b=lhsT, c=rhs, s1=start, s2=stop:
              eng.matmul(a, b, c, start=s1, stop=s2), inc=inc)

    def wsl(w, ct, dblk):
        return w[:, ct * 1024 + dblk * 128: ct * 1024 + dblk * 128 + 128]

    def proj_group(psum, wt, i, rhs_fn, aug1, aug2, inc=None):
        for ct in range(CT):
            mm(psum, wsl(wt, ct, i), rhs_fn(ct), start=(ct == 0), stop=False)
            if ct == 3 and aug1 is not None:
                mm(psum, aug1[0], aug1[1], start=False, stop=False)
        last = aug2 if aug2 is not None else aug1
        if last is None:
            raise ValueError
        mm(psum, last[0], last[1], start=False, stop=True, inc=inc)

    Xv = X.rearrange("p (ct t) -> p ct t", t=T)
    XSv = XS.rearrange("p (ct j) -> p ct j", j=CH + 1)
    def par_w(i): return PAR[:, i:i + 1]
    def par_upw(i): return PAR[:, 8 + i:9 + i]
    def par_cl(i): return PAR[:, 16 + i:17 + i]
    def par_w512(i): return PAR[:, 24 + i:25 + i]
    def parb(m, ct): return PARB[:, m * 8 + ct: m * 8 + ct + 1]
    def parb1m(m, ct): return PARB[:, 24 + m * 8 + ct: 25 + m * 8 + ct]
    def parf(m, ct): return PARF[:, m * 8 + ct: m * 8 + ct + 1]
    def parf1m(m, ct): return PARF[:, 24 + m * 8 + ct: 25 + m * 8 + ct]

    for dst, src in [(cJ, din["cJ"]), (cIdF, din["cIdentF"]), (cIdB, din["cIdentB"]),
                     (cDg, din["cDiag"]), (cOn, din["cOnes"]), (cBt, din["cBias"]),
                     (cCV, din["cCVT"])]:
        SY.dma(lambda eng, d=dst, s=src: eng.dma_start(out=d, in_=s), sCN)

    DV.op(lambda eng: eng.memset(EPSB, EPS), inc=sDV)
    xin_t = xin.rearrange("(tt p) c -> tt p c", p=128)
    PE.wait(sCN)
    for tt in range(16):
        par = tt % 2
        sX = sXA if par == 0 else sXB
        if tt >= 2:
            SY.wait(sAC)
        SY.dma(lambda eng, tt=tt, par=par: eng.dma_start(
            out=XSTG[:, par * 1024:(par + 1) * 1024], in_=xin_t[tt]), sX)
        PE.wait(sX)
        PE.wait(sAC)
        for cb in range(8):
            bank = PS[cb // 4]
            PE.op(lambda eng, cb=cb, par=par, bank=bank: eng.transpose(
                bank[:, (cb % 4) * 128:(cb % 4 + 1) * 128],
                XSTG[:, par * 1024 + cb * 128: par * 1024 + (cb + 1) * 128],
                cIdF), inc=(sPE if cb == 7 else None))
        SC.wait(sPE)
        for hf in range(2):
            dst = Xv[:, hf * 4:(hf + 1) * 4, tt * 128:(tt + 1) * 128]
            SC.op(lambda eng, hf=hf, dst=dst: cpy(eng,
                dst, PS[hf].rearrange("p (a r) -> p a r", a=4)),
                inc=(sAC if hf == 1 else None))

    SY.wait(sAC)
    SY.dma(lambda eng: eng.dma_start(out=XSTG[0:40, 0:128], in_=din["a_par"]), sIN)
    PE.wait(sIN)
    PE.op(lambda eng: eng.transpose(PS[2][:, 0:128], XSTG[:, 0:128], cIdF), inc=sPE)
    SC.wait(sPE)
    SC.op(lambda eng: cpy(eng, PRAW, PS[2][:, 0:40]), inc=sAC)
    SC.op(lambda eng: eng.activation(
        out=PAR[:, 0:8], in_=PRAW[:, 0:8], func=AF.Exp))
    SC.op(lambda eng: eng.drain(fusable=False), inc=sAC)
    DV.wait(sAC)
    DV.opE(lambda eng: eng.tensor_tensor(
        out=PAR[:, 8:16], in0=PRAW[:, 8:16], in1=PAR[:, 0:8], op=ALU.subtract), sDV)
    DV.opE(lambda eng: eng.tensor_scalar(
        out=PAR[:, 16:24], in0=PAR[:, 8:16], scalar1=43.0, scalar2=None,
        op0=ALU.subtract), sDV)
    DV.opE(lambda eng: eng.tensor_scalar(
        out=PAR[:, 24:32], in0=PAR[:, 0:8], scalar1=512.0, scalar2=None,
        op0=ALU.mult), sDV)
    DV.opE(lambda eng: eng.tensor_copy(out=PARB[:, 0:24], in_=PRAW[:, 16:40]), sDV)
    DV.opE(lambda eng: eng.tensor_scalar(
        out=PARB[:, 24:48], in0=PRAW[:, 16:40],
        scalar1=-1.0, scalar2=1.0, op0=ALU.mult, op1=ALU.add), sDV)
    DV.op(lambda eng: eng.tensor_copy(out=PARF, in_=PARB), inc=sDV)

    _wn = [0]
    _wn_pe = {0: 0, 1: 0}

    def prep_weight(src_name, slot):
        par = _wn[0] % 2
        nat = WNAT if par == 0 else WNATB
        _wn[0] += 1
        if _wn_pe[par]:
            GP.wait(sPE, _wn_pe[par])
        GP.dma(lambda eng, s=src_name, nat=nat: eng.dma_start(
            out=nat.rearrange("p (r c) -> p r c", c=1024),
            in_=din[s].rearrange("(r p) c -> p r c", p=128)), sGPD)
        PE.wait(sGPD)
        slot_v = slot.rearrange("p (ct d) -> p ct d", d=1024)
        for r in range(8):
            PE.wait(sDV)
            PE.wait(sAC)
            for dblk in range(8):
                PE.op(lambda eng, r=r, dblk=dblk, nat=nat: eng.transpose(
                    PST[:, dblk * 128:(dblk + 1) * 128],
                    nat[:, r * 1024 + dblk * 128: r * 1024 + (dblk + 1) * 128],
                    cIdB), inc=(sPE if dblk == 7 else None))
            evq, evsem = (DV, sDV) if (r % 2 == 0) else (SC, sAC)
            evq.wait(sPE)
            dst = slot_v[:, :, r * 128:(r + 1) * 128]
            evq.op(lambda eng, dst=dst: cpy(
                eng, dst, PST.rearrange("p (a r2) -> p a r2", a=8)), inc=evsem)
        _wn_pe[par] = sPE.n

    def make_svec(slot_w, lhs_fn, dst_aug, dst_row, scale=-1.0):
        for hf in range(2):
            for ct in range(CT):
                mm(PS[6][0:1, 0:512], lhs_fn(ct),
                   slot_w[:, ct * 1024 + hf * 512: ct * 1024 + hf * 512 + 512],
                   start=(ct == 0), stop=(ct == CT - 1),
                   inc=(sPE if ct == CT - 1 else None))
            SC.wait(sPE)
            SC.op(lambda eng, hf=hf, dst_aug=dst_aug, dst_row=dst_row, scale=scale:
                  eng.activation(
                      out=dst_aug[dst_row:dst_row + 1, hf * 512: hf * 512 + 512],
                      in_=PS[6][0:1, 0:512], func=AF.Copy, scale=scale), inc=sAC)
            PE.wait(sAC)

    prep_weight("a_Wk", W0)
    prep_weight("a_Wv", W1)
    prep_weight("a_Wr", W2)
    prep_weight("a_Wo", W3)
    barrier()
    make_svec(W0, lambda ct: parb(0, ct), AUGA, 0)
    make_svec(W0, lambda ct: parb1m(0, ct), AUGA, 32)
    make_svec(W1, lambda ct: parb(1, ct), AUGA, 64)
    make_svec(W1, lambda ct: parb1m(1, ct), AUGB, 0)
    make_svec(W2, lambda ct: parb(2, ct), AUGB, 32)
    make_svec(W2, lambda ct: parb1m(2, ct), AUGB, 64)

    def ln_stats():
        barrier()
        DV.opE(lambda eng: eng.memset(MRB[:, 0:1], 0.0), sDV)
        for c in range(NCH):
            t0 = c * CH
            GP.wait(sAC)
            GP.dma(lambda eng, t0=t0: eng.dma_start(
                out=XBF.rearrange("p (ct j) -> p ct j", j=CH),
                in_=Xv[:, :, t0:t0 + CH]), sGPD)
            SC.wait(sGPD)
            SC.wait(sPE)
            SC.op(lambda eng: eng.activation(out=XSQ, in_=XBF, func=AF.Square),
                  inc=sAC)
            PE.wait(sGPD)
            PE.wait(sDV)
            for ct in range(CT):
                mm(PS[0], cOn[:, 0:128], XBF[:, ct * CH:(ct + 1) * CH],
                   start=(ct == 0), stop=(ct == CT - 1))
            PE.wait(sAC)
            for ct in range(CT):
                mm(PS[1], cOn[:, 0:128], XSQ[:, ct * CH:(ct + 1) * CH],
                   start=(ct == 0), stop=(ct == CT - 1),
                   inc=(sPE if ct == CT - 1 else None))
            DV.wait(sPE)
            DV.op(lambda eng: eng.tensor_scalar(
                out=MST, in0=PS[0], scalar1=1.0 / C, scalar2=None, op0=ALU.mult))
            DV.op(lambda eng: eng.tensor_tensor(
                out=VST, in0=MST, in1=MST, op=ALU.mult))
            DV.op(lambda eng: eng.scalar_tensor_tensor(
                out=VST, in0=PS[1], scalar=1.0 / C, in1=VST,
                op0=ALU.mult, op1=ALU.subtract), inc=sDV)
            SC.wait(sDV)
            SC.op(lambda eng: eng.activation(
                out=VST2, in_=VST, func=AF.Sqrt, bias=EPSB), inc=sAC)
            DV.wait(sAC)
            DV.op(lambda eng, t0=t0: eng.reciprocal(
                out=RSTD[:, t0:t0 + CH], in_=VST2))
            DV.op(lambda eng, t0=t0: eng.tensor_tensor(
                out=MRB[:, 1 + t0:1 + t0 + CH], in0=MST,
                in1=RSTD[:, t0:t0 + CH], op=ALU.mult), inc=sDV)
        barrier()

    def make_xs(c, q_=None, sem=None):
        q_ = q_ or DV
        sem = sem or sDV
        t0 = c * CH
        if c == 0:
            q_.opE(lambda eng: eng.memset(XSv[:, :, 0:1], 0.0), sem)
        else:
            q_.opE(lambda eng: eng.tensor_copy(
                out=XSv[:, :, 0:1], in_=XSv[:, :, CH:CH + 1]), sem)
        for ct in range(CT):
            q_.op(lambda eng, ct=ct, t0=t0: eng.tensor_tensor(
                out=XS[:, ct * (CH + 1) + 1: ct * (CH + 1) + 1 + CH],
                in0=X[:, ct * T + t0: ct * T + t0 + CH],
                in1=RSTD[:, t0:t0 + CH], op=ALU.mult),
                inc=(sem if ct == CT - 1 else None))

    def make_mix(m, dstbuf):
        for ct in range(CT):
            DV.op(lambda eng, ct=ct, m=m: eng.tensor_scalar(
                out=TMPG1, in0=XS[:, ct * (CH + 1) + 1: ct * (CH + 1) + 1 + CH],
                scalar1=parf(m, ct), scalar2=None, op0=ALU.mult))
            DV.op(lambda eng, ct=ct, m=m: eng.tensor_scalar(
                out=TMPG2, in0=XS[:, ct * (CH + 1): ct * (CH + 1) + CH],
                scalar1=parf1m(m, ct), scalar2=None, op0=ALU.mult))
            DV.op(lambda eng, ct=ct, dstbuf=dstbuf: eng.tensor_tensor(
                out=dstbuf[:, ct * CH:(ct + 1) * CH], in0=TMPG1, in1=TMPG2,
                op=ALU.add), inc=(sDV if ct == CT - 1 else None))

    import os
    DBG = int(os.environ.get("KDBG", "9"))
    KAUG = int(os.environ.get("KAUG", "1"))

    def attn_a():
        ln_stats()
        if DBG < 2:
            barrier()
            return
        DV.opE(lambda eng: eng.memset(PPB, -1e38), sDV)
        DV.opE(lambda eng: eng.memset(AAC, 0.0), sDV)
        DV.opE(lambda eng: eng.memset(BBC, 0.0), sDV)
        for c in range(NCH):
            t0 = c * CH
            DV.wait(sPE)
            make_xs(c)
            make_mix(2, MIXA)
            mix_r = sDV.n
            if DBG < 3:
                continue
            sig_done = []
            for i in range(CT):
                pr = PS[i % 2]
                PE.wait(sDV, mix_r)
                if i >= 2:
                    PE.wait(sAC, sig_done[i - 2])
                proj_group(pr, W2, i,
                           lambda ct: MIXA[:, ct * CH:(ct + 1) * CH],
                           (AUGB[32:33, i * 128:(i + 1) * 128],
                            MRB[32:33, 1 + t0:1 + t0 + CH]),
                           (AUGB[64:65, i * 128:(i + 1) * 128],
                            MRB[64:65, t0:t0 + CH]), inc=sPE)
                SC.wait(sPE)
                SC.op(lambda eng, i=i, pr=pr: eng.activation(
                    out=YG[:, i * CH:(i + 1) * CH], in_=pr, func=AF.Sigmoid),
                    inc=sAC)
                sig_done.append(sAC.n)
            r_all = sPE.n
            if DBG < 4:
                continue
            DV.wait(sPE, r_all)
            make_mix(0, MIXA)
            make_mix(1, MIXB)
            mix_kv = sDV.n
            wkv_end = [0, 0]
            for i in range(CT):
                pk, pv = PS[2 + (i % 2) * 2], PS[3 + (i % 2) * 2]
                PE.wait(sDV, mix_kv)
                if i >= 2:
                    PE.wait(sDV, wkv_end[i % 2])
                proj_group(pk, W0, i,
                           lambda ct: MIXA[:, ct * CH:(ct + 1) * CH],
                           (AUGA[0:1, i * 128:(i + 1) * 128],
                            MRB[0:1, 1 + t0:1 + t0 + CH]),
                           (AUGA[32:33, i * 128:(i + 1) * 128],
                            MRB[32:33, t0:t0 + CH]))
                proj_group(pv, W1, i,
                           lambda ct: MIXB[:, ct * CH:(ct + 1) * CH],
                           (AUGA[64:65, i * 128:(i + 1) * 128],
                            MRB[64:65, 1 + t0:1 + t0 + CH]),
                           (AUGB[0:1, i * 128:(i + 1) * 128],
                            MRB[0:1, t0:t0 + CH]), inc=sPE)
                kv_done = sPE.n
                if DBG < 5:
                    continue

                DV.wait(sPE, kv_done)
                DV.op(lambda eng, i=i, pk=pk: eng.scalar_tensor_tensor(
                    out=ZP[0], in0=cJ, scalar=par_w(i), in1=pk,
                    op0=ALU.mult, op1=ALU.add))
                DV.opE(lambda eng, i=i: eng.tensor_copy(
                    out=CB[:, 0:1], in_=PPB[:, i:i + 1]), sDV)
                DV.opE(lambda eng, i=i: eng.tensor_tensor_scan(
                    out=CB[:, 1:CH + 1], data0=ZP[0], data1=ZP[0],
                    initial=PPB[:, i:i + 1], op0=ALU.max, op1=ALU.max), sDV)
                DV.op(lambda eng: eng.tensor_tensor(
                    out=ZP[1], in0=CB[:, 0:CH], in1=ZP[0], op=ALU.subtract))
                DV.op(lambda eng, i=i: eng.tensor_scalar(
                    out=ZP[4], in0=ZP[1], scalar1=par_cl(i), scalar2=None,
                    op0=ALU.max))
                DV.op(lambda eng: eng.tensor_scalar(
                    out=ZP[2], in0=ZP[4], scalar1=0.0, scalar2=None,
                    op0=ALU.min))
                DV.op(lambda eng: eng.tensor_scalar(
                    out=ZP[3], in0=ZP[1], scalar1=-1.0, scalar2=0.0,
                    op0=ALU.mult, op1=ALU.min), inc=sDV)
                SC.wait(sDV)
                SC.op(lambda eng: eng.activation(out=ZP[1], in_=ZP[2], func=AF.Exp))
                SC.op(lambda eng: eng.activation(out=ZP[2], in_=ZP[3], func=AF.Exp))
                SC.op(lambda eng, i=i: eng.activation(
                    out=ZP[3], in_=ZP[4], func=AF.Exp, bias=par_upw(i),
                    scale=-1.0), inc=sAC)
                DV.wait(sAC)
                DV.op(lambda eng, pv=pv: eng.tensor_tensor(
                    out=ZP[0], in0=ZP[2], in1=pv, op=ALU.mult))
                DV.opE(lambda eng, i=i: eng.tensor_copy(
                    out=ABF[:, 0:1], in_=AAC[:, i:i + 1]), sDV)
                DV.opE(lambda eng, i=i: eng.tensor_tensor_scan(
                    out=ABF[:, 1:CH + 1], data0=ZP[1], data1=ZP[0],
                    initial=AAC[:, i:i + 1], op0=ALU.mult, op1=ALU.add), sDV)
                DV.opE(lambda eng, i=i: eng.tensor_copy(
                    out=BBF[:, 0:1], in_=BBC[:, i:i + 1]), sDV)
                DV.opE(lambda eng, i=i: eng.tensor_tensor_scan(
                    out=BBF[:, 1:CH + 1], data0=ZP[1], data1=ZP[2],
                    initial=BBC[:, i:i + 1], op0=ALU.mult, op1=ALU.add), sDV)
                DV.op(lambda eng, pv=pv: eng.tensor_tensor(
                    out=ZP[4], in0=ZP[3], in1=pv, op=ALU.mult))
                DV.op(lambda eng: eng.tensor_tensor(
                    out=ZP[0], in0=ABF[:, 0:CH], in1=ZP[4], op=ALU.add))
                DV.op(lambda eng: eng.tensor_tensor(
                    out=ZP[2], in0=BBF[:, 0:CH], in1=ZP[3], op=ALU.add),
                    inc=sDV)
                wkv_end[i % 2] = sDV.n
                DV.opE(lambda eng, i=i: eng.tensor_scalar(
                    out=PPB[:, i:i + 1], in0=CB[:, CH:CH + 1],
                    scalar1=par_w512(i), scalar2=None, op0=ALU.subtract), sDV)
                DV.opE(lambda eng, i=i: eng.tensor_copy(
                    out=AAC[:, i:i + 1], in_=ABF[:, CH:CH + 1]), sDV)
                DV.opE(lambda eng, i=i: eng.tensor_copy(
                    out=BBC[:, i:i + 1], in_=BBF[:, CH:CH + 1]), sDV)
                SC.wait(sDV)
                SC.op(lambda eng: eng.activation(out=ZP[1], in_=ZP[2], func=AF.Ln))
                SC.op(lambda eng: eng.activation(
                    out=ZP[4], in_=ZP[1], func=AF.Exp, scale=-1.0), inc=sAC)
                DV.wait(sAC)
                DV.op(lambda eng: eng.tensor_tensor(
                    out=TMPA, in0=ZP[0], in1=ZP[4], op=ALU.mult))
                DV.op(lambda eng, i=i: eng.tensor_tensor(
                    out=YG[:, i * CH:(i + 1) * CH], in0=TMPA,
                    in1=YG[:, i * CH:(i + 1) * CH], op=ALU.mult), inc=sDV)
            wkv_all = sDV.n
            if DBG < 6:
                continue
            PE.wait(sDV, wkv_all)
            wo_res = [0, 0]
            for dp in range(CT):
                pw = PS[6] if dp % 2 == 0 else PS[5]
                if dp >= 2:
                    PE.wait(sDV, wo_res[dp % 2])
                for ct in range(CT):
                    mm(pw, wsl(W3, ct, dp), YG[:, ct * CH:(ct + 1) * CH],
                       start=(ct == 0), stop=(ct == CT - 1),
                       inc=(sPE if ct == CT - 1 else None))
                DV.wait(sPE)
                DV.op(lambda eng, dp=dp, t0=t0, pw=pw: eng.tensor_tensor(
                    out=X[:, dp * T + t0: dp * T + t0 + CH],
                    in0=X[:, dp * T + t0: dp * T + t0 + CH], in1=pw,
                    op=ALU.add), inc=sDV)
                wo_res[dp % 2] = sDV.n
        barrier()

    if n_sublayers >= 1:
        attn_a()

    def ffn_build():
        barrier()
        for dst, s in ((FP2, "cFP2"), (FP2B, "cFP2B"), (FSD, "cFSD"),
                       (FSG, "cFSG")):
            SY.dma(lambda eng, d=dst, s=s: eng.dma_start(
                out=d[0:din[s].shape[0], :], in_=din[s]), sIN)
        for dst, s in ((FG1R, "f_G1R"), (FG2R, "f_G2R"), (FG1I, "f_G1I"),
                       (FG2I, "f_G2I")):
            GP.dma(lambda eng, d=dst, s=s: eng.dma_start(
                out=d[0:din[s].shape[0], :], in_=din[s]), sGPD)
        GP.dma(lambda eng: eng.dma_start(
            out=AUGC[32:33, 0:1024], in_=din["f_bias"]), sGPD)
        PE.wait(sGPD)
        PE.wait(sIN)
        for part in range(2):
            G1 = FG1R if part == 0 else FG1I
            G2 = FG2R if part == 0 else FG2I
            for c4 in range(4):
                mm(PS[0][0:32, :], G1[0:32, c4 * 32:(c4 + 1) * 32],
                   FP2[0:32, :], start=True, stop=True, inc=sPE)
                SC.wait(sPE)
                SC.op(lambda eng, G1=G1: eng.activation(
                    out=FO1[0:32, :], in_=PS[0][0:32, :], func=AF.Copy))
                mm(PS[1][0:16, :], G2[0:16, c4 * 16:(c4 + 1) * 16],
                   FP2B[0:16, :], start=True, stop=True, inc=sPE)
                SC.wait(sPE)
                SC.op(lambda eng: eng.activation(
                    out=FO2[0:16, :], in_=PS[1][0:16, :], func=AF.Copy),
                    inc=sAC)
                PE.wait(sAC)
                for pt in range(4):
                    PE.wait(sDV)
                    mm(PS[2], FSD[0:32, pt * 128:(pt + 1) * 128], FO1[0:32, :],
                       start=True, stop=True)
                    mm(PS[3], FSG[0:16, pt * 128:(pt + 1) * 128], FO2[0:16, :],
                       start=True, stop=True, inc=sPE)
                    DV.wait(sPE)
                    DV.op(lambda eng: eng.tensor_copy(out=FEA, in_=PS[2]))
                    DV.op(lambda eng: eng.tensor_tensor(
                        out=FEA, in0=PS[3], in1=FEA, op=ALU.mult))
                    w = FWT[part * 4 + pt]
                    if c4 == 0:
                        DV.op(lambda eng, w=w: eng.tensor_copy(out=w, in_=FEA),
                              inc=sDV)
                    else:
                        DV.op(lambda eng, w=w: eng.tensor_tensor(
                            out=w, in0=w, in1=FEA, op=ALU.add), inc=sDV)
        SC.wait(sDV)
        for ct2 in range(4):
            SC.op(lambda eng, ct2=ct2: eng.activation(
                out=W0[:, ct2 * 1024: ct2 * 1024 + 512], in_=FWT[ct2],
                func=AF.Copy))
            SC.op(lambda eng, ct2=ct2: eng.activation(
                out=W0[:, ct2 * 1024 + 512: ct2 * 1024 + 1024], in_=FWT[4 + ct2],
                func=AF.Copy))
            SC.op(lambda eng, ct2=ct2: eng.activation(
                out=W0[:, (4 + ct2) * 1024: (4 + ct2) * 1024 + 512],
                in_=FWT[4 + ct2], func=AF.Copy, scale=-1.0))
            SC.op(lambda eng, ct2=ct2: eng.activation(
                out=W0[:, (4 + ct2) * 1024 + 512: (4 + ct2) * 1024 + 1024],
                in_=FWT[ct2], func=AF.Copy), inc=sAC)
        barrier()
        make_svec(W0, lambda ct: cOn[:, 0:1], AUGC, 0)
        barrier()
        SY.dma(lambda eng: eng.dma_start(out=wstash, in_=W0), sIN)

    def ffn(reload=False):
        if reload:
            barrier()
            SY.wait(sIN)
            SY.dma(lambda eng: eng.dma_start(out=W0, in_=wstash), sIN)
        ln_stats()
        if reload:
            PE.wait(sIN)
        res_done = [0, 0]
        for c in range(NCH):
            t0 = c * CH
            DV.wait(sPE)
            make_xs(c)
            xs_v = sDV.n
            PE.wait(sDV, xs_v)
            for dp in range(CT):
                psf = PS[dp % 2]
                if c * CT + dp >= 2:
                    PE.wait(sDV, res_done[dp % 2])
                proj_group(psf, W0, dp,
                           lambda ct: XS[:, ct * (CH + 1) + 1:
                                         ct * (CH + 1) + 1 + CH],
                           (AUGC[0:1, dp * 128:(dp + 1) * 128],
                            MRB[0:1, 1 + t0:1 + t0 + CH]),
                           (AUGC[32:33, dp * 128:(dp + 1) * 128],
                            cOn[32:33, 0:CH]), inc=sPE)
                DV.wait(sPE)
                DV.op(lambda eng, dp=dp, t0=t0, psf=psf: eng.tensor_tensor(
                    out=X[:, dp * T + t0: dp * T + t0 + CH],
                    in0=X[:, dp * T + t0: dp * T + t0 + CH], in1=psf,
                    op=ALU.add), inc=sDV)
                res_done[dp % 2] = sDV.n
        barrier()

    if n_sublayers >= 2:
        ffn_build()
        ffn()

    import os as _os_chk
    def load_win(src_name):
        win_v = WINB.rearrange("p (ct d) -> p ct d", d=1024)
        src_t = din[src_name].rearrange("(r p) cc -> p r cc", p=128)
        for half in range(2):
            GP.wait(sPE)
            GP.dma(lambda eng, half=half, src_t=src_t: eng.dma_start(
                out=WNAT2.rearrange("p (r cc) -> p r cc", cc=1024),
                in_=src_t[:, half * 4:(half + 1) * 4, :]), sGPD)
            PE.wait(sGPD)
            for r4 in range(4):
                r = half * 4 + r4
                PE.wait(sDV)
                PE.wait(sAC)
                for dblk in range(8):
                    PE.op(lambda eng, r4=r4, dblk=dblk: eng.transpose(
                        PST[:, dblk * 128:(dblk + 1) * 128],
                        WNAT2[:, r4 * 1024 + dblk * 128: r4 * 1024 + (dblk + 1) * 128],
                        cIdB), inc=(sPE if dblk == 7 else None))
                evq, evsem = (DV, sDV) if (r % 2 == 0) else (SC, sAC)
                evq.wait(sPE)
                dst = win_v[:, :, r * 128:(r + 1) * 128]
                evq.op(lambda eng, dst=dst: cpy(
                    eng, dst, PST.rearrange("p (a r2) -> p a r2", a=8)), inc=evsem)
        barrier()

    def kTs(dt):
        slot = W0 if dt < 4 else W1
        return slot[:, (dt % 4) * 2048:(dt % 4) * 2048 + 2048]

    def attn_b():
        ln_stats()
        GP.dma(lambda eng: eng.dma_start(
            out=AUGB[0:1, 0:1024], in_=din["b_out_b"]), sGPD)
        DV.wait(sGPD)
        for c in range(NCH):
            t0 = c * CH
            for ct in range(CT):
                DV.op(lambda eng, ct=ct, t0=t0: eng.tensor_tensor(
                    out=QTXS[:, ct * T + t0: ct * T + t0 + CH],
                    in0=X[:, ct * T + t0: ct * T + t0 + CH],
                    in1=RSTD[:, t0:t0 + CH], op=ALU.mult),
                    inc=(sDV if ct == CT - 1 else None))
        for j in range(16):
            oc = VN[j].rearrange("p (h d) -> p h d", d=65)[:, :, 64:65]
            DV.op(lambda eng, oc=oc: eng.tensor_copy(
                out=oc, in_=cOn[:, 0:16].unsqueeze(2)),
                inc=(sDV if j == 15 else None))
        barrier()
        load_win("b_wv")
        make_svec(WINB, lambda ct: cOn[:, 0:1], AUGA, 0)
        barrier()
        ev_done = [0, 0]
        for tt in range(16):
            for hf in range(2):
                idx = tt * 2 + hf
                ps_ = PS[idx % 2]
                if idx >= 2:
                    PE.wait(sAC, ev_done[idx % 2])
                for ct in range(CT):
                    mm(ps_, QTXS[:, ct * T + tt * 128: ct * T + (tt + 1) * 128],
                       WINB[:, ct * 1024 + hf * 512: ct * 1024 + hf * 512 + 512],
                       start=(ct == 0), stop=False)
                mm(ps_, MRB[0:1, 1 + tt * 128: 1 + (tt + 1) * 128],
                   AUGA[0:1, hf * 512: hf * 512 + 512],
                   start=False, stop=True, inc=sPE)
                SC.wait(sPE)
                dst = VN[tt].rearrange("p (h d) -> p h d",
                                       d=65)[:, hf * 8:(hf + 1) * 8, 0:64]
                SC.op(lambda eng, dst=dst, ps_=ps_: eng.activation(
                    out=dst, in_=ps_.rearrange("p (a b) -> p a b", a=8),
                    func=AF.Copy), inc=sAC)
                ev_done[idx % 2] = sAC.n
        barrier()
        load_win("b_wk")
        make_svec(WINB, lambda ct: cOn[:, 0:1], AUGA, 32)
        barrier()
        kv_ev = [0, 0, 0, 0]
        for c in range(NCH):
            t0 = c * CH
            for dt in range(CT):
                ps_ = PS[dt % 4]
                if kv_ev[dt % 4]:
                    PE.wait(sAC, kv_ev[dt % 4])
                for ct in range(CT):
                    mm(ps_, WINB[:, ct * 1024 + dt * 128: ct * 1024 + (dt + 1) * 128],
                       QTXS[:, ct * T + t0: ct * T + t0 + CH],
                       start=(ct == 0), stop=False)
                mm(ps_, AUGA[32:33, dt * 128:(dt + 1) * 128],
                   MRB[32:33, 1 + t0:1 + t0 + CH], start=False, stop=True, inc=sPE)
                SC.wait(sPE)
                SC.op(lambda eng, dt=dt, t0=t0, ps_=ps_: eng.activation(
                    out=kTs(dt)[:, t0:t0 + CH], in_=ps_, func=AF.Copy), inc=sAC)
                kv_ev[dt % 4] = sAC.n
            DV.wait(sAC)
            for hc in range(2):
                th = t0 + hc * 256
                GP.wait(sDV)
                GP.dma(lambda eng, th=th: eng.dma_start(
                    out=ROPE.rearrange("p (dt j) -> p dt j", j=256),
                    in_=din["cCos"].rearrange("p (dt t) -> p dt t", t=T)[:, :, th:th + 256]), sGPD)
                GP.dma(lambda eng, th=th: eng.dma_start(
                    out=RTS.rearrange("p (dt j) -> p dt j", j=256),
                    in_=din["cSin"].rearrange("p (dt t) -> p dt t", t=T)[:, :, th:th + 256]), sGPD)
                DV.wait(sGPD)
                for dt in range(4):
                    top = kTs(dt)[:, th:th + 256]
                    bot = kTs(dt + 4)[:, th:th + 256]
                    cosb = ROPE[:, dt * 256:(dt + 1) * 256]
                    sinb = RTS[:, dt * 256:(dt + 1) * 256]
                    DV.op(lambda eng, top=top, cosb=cosb: eng.tensor_tensor(
                        out=TMPC, in0=top, in1=cosb, op=ALU.mult))
                    DV.op(lambda eng, top=top, sinb=sinb: eng.tensor_tensor(
                        out=TMPD, in0=top, in1=sinb, op=ALU.mult))
                    DV.op(lambda eng, top=top, bot=bot, sinb=sinb: eng.tensor_tensor(
                        out=top, in0=bot, in1=sinb, op=ALU.mult))
                    DV.op(lambda eng, top=top: eng.tensor_tensor(
                        out=top, in0=TMPC, in1=top, op=ALU.subtract))
                    DV.op(lambda eng, bot=bot, cosb=cosb: eng.tensor_tensor(
                        out=TMPC, in0=bot, in1=cosb, op=ALU.mult))
                    DV.op(lambda eng, bot=bot: eng.tensor_tensor(
                        out=bot, in0=TMPD, in1=TMPC, op=ALU.add),
                        inc=(sDV if dt == 3 else None))
        barrier()
        load_win("b_wq")
        make_svec(WINB, lambda ct: cOn[:, 0:1], AUGA, 64)
        barrier()
        for c in range(NCH):
            t0 = c * CH
            for hc in range(2):
                th = t0 + hc * 256
                GP.wait(sDV)
                GP.dma(lambda eng, th=th: eng.dma_start(
                    out=ROPE.rearrange("p (dt j) -> p dt j", j=256),
                    in_=din["cCos"].rearrange("p (dt t) -> p dt t", t=T)[:, :, th:th + 256]), sGPD)
                GP.dma(lambda eng, th=th: eng.dma_start(
                    out=RTS.rearrange("p (dt j) -> p dt j", j=256),
                    in_=din["cSin"].rearrange("p (dt t) -> p dt t", t=T)[:, :, th:th + 256]), sGPD)
                qev = [0, 0]
                for dt in range(CT):
                    ps_ = PS[dt % 2][:, 0:256]
                    PE.wait(sDV)
                    if qev[dt % 2]:
                        PE.wait(sAC, qev[dt % 2])
                    for ct in range(CT):
                        mm(ps_, WINB[:, ct * 1024 + dt * 128: ct * 1024 + (dt + 1) * 128],
                           QTXS[:, ct * T + th: ct * T + th + 256],
                           start=(ct == 0), stop=False)
                    mm(ps_, AUGA[64:65, dt * 128:(dt + 1) * 128],
                       MRB[64:65, 1 + th:1 + th + 256], start=False, stop=True,
                       inc=sPE)
                    SC.wait(sPE)
                    SC.op(lambda eng, dt=dt, ps_=ps_: eng.activation(
                        out=QP8[:, dt * 256:(dt + 1) * 256], in_=ps_,
                        func=AF.Copy), inc=sAC)
                    qev[dt % 2] = sAC.n
                DV.wait(sAC)
                DV.wait(sGPD)
                for dt in range(4):
                    topo = QTXS[:, dt * T + th: dt * T + th + 256]
                    boto = QTXS[:, (dt + 4) * T + th:(dt + 4) * T + th + 256]
                    qpa = QP8[:, dt * 256:(dt + 1) * 256]
                    qpb = QP8[:, (dt + 4) * 256:(dt + 5) * 256]
                    cosb = ROPE[:, dt * 256:(dt + 1) * 256]
                    sinb = RTS[:, dt * 256:(dt + 1) * 256]
                    DV.op(lambda eng, qpa=qpa, cosb=cosb: eng.tensor_tensor(
                        out=TMPC, in0=qpa, in1=cosb, op=ALU.mult))
                    DV.op(lambda eng, qpb=qpb, sinb=sinb: eng.tensor_tensor(
                        out=TMPD, in0=qpb, in1=sinb, op=ALU.mult))
                    DV.op(lambda eng, topo=topo: eng.tensor_tensor(
                        out=topo, in0=TMPC, in1=TMPD, op=ALU.subtract))
                    DV.op(lambda eng, qpa=qpa, sinb=sinb: eng.tensor_tensor(
                        out=TMPC, in0=qpa, in1=sinb, op=ALU.mult))
                    DV.op(lambda eng, qpb=qpb, cosb=cosb: eng.tensor_tensor(
                        out=TMPD, in0=qpb, in1=cosb, op=ALU.mult))
                    DV.op(lambda eng, boto=boto: eng.tensor_tensor(
                        out=boto, in0=TMPC, in1=TMPD, op=ALU.add),
                        inc=(sDV if dt == 3 else None))
        barrier()
        if os.environ.get("KDUMP") == "qt":
            SY.dma(lambda eng: eng.dma_start(out=dbg, in_=QTXS), sIN)
            barrier()
        if os.environ.get("KDUMP") == "kt":
            SY.dma(lambda eng: eng.dma_start(out=dbg[:, 0:8192], in_=W0), sIN)
            SY.dma(lambda eng: eng.dma_start(out=dbg[:, 8192:16384], in_=W1), sIN)
            barrier()
        if os.environ.get("KDUMP") == "vn":
            for j in range(16):
                SY.dma(lambda eng, j=j: eng.dma_start(
                    out=dbg[:, j * 1024:(j + 1) * 1024], in_=VN[j][:, 0:1024]), sIN)
            barrier()
        load_win("b_out_w")
        barrier()
        for g in range(NCH):
            t1c = g * 512
            tr_pe = [0]
            ts_dv = [0]
            pending_norm = [None]
            for h in range(H):
                dtq = h // 2
                pbase = (h % 2) * 64
                qsl = QTXS[pbase:pbase + 64, dtq * T + t1c: dtq * T + t1c + 512]
                gran = ALIBI_GRAN[h]
                dmin = -(16 - gran // 128)
                nj = 4 * g + 4
                exp_ac = {}
                mask_dv = {}
                pv_arr = {}

                def emit_qk(j):
                    psc = PS[j % 2]
                    if j >= 2:
                        PE.wait(sAC, exp_ac[j - 2])
                    else:
                        PE.wait(sAC)
                    ksl = kTs(dtq)[pbase:pbase + 64, j * 128:(j + 1) * 128]
                    mm(psc, ksl, qsl, start=True, stop=True, inc=sPE)
                    return sPE.n

                def emit_exp(j, qk_v):
                    psc = PS[j % 2]
                    PT = PTD[j % 2]
                    SC.wait(sPE, qk_v)
                    if j >= 2:
                        SC.wait(sPE, pv_arr[j - 2])
                    if j - 1 in mask_dv:
                        SC.wait(sDV, mask_dv[j - 1])
                    nsub = 512 // gran
                    for s2 in range(nsub):
                        c0 = s2 * gran
                        c1 = c0 + gran
                        dlo = j * 128 - t1c
                        if c1 <= dlo:
                            SC.op(lambda eng, c0=c0, c1=c1, PT=PT: eng.memzero(
                                PT[:, c0:c1]))
                            continue
                        blk0 = (t1c + s2 * gran) // gran
                        didx = j - (gran // 128) * blk0 - dmin
                        bias = cBt[:, h * 16 + didx: h * 16 + didx + 1]
                        if c0 <= dlo < c1:
                            if dlo > c0:
                                SC.op(lambda eng, c0=c0, dlo=dlo, PT=PT:
                                      eng.memzero(PT[:, c0:dlo]))
                            SC.op(lambda eng, psc=psc, dlo=dlo, bias=bias:
                                  eng.activation(
                                      out=PSC, in_=psc[:, dlo:dlo + 128],
                                      func=AF.Exp, scale=0.125, bias=bias),
                                  inc=sAC)
                            DV.wait(sAC)
                            DV.op(lambda eng, dlo=dlo, PT=PT: eng.tensor_tensor(
                                out=PT[:, dlo:dlo + 128], in0=PSC, in1=cDg,
                                op=ALU.mult), inc=sDV)
                            if dlo + 128 < c1:
                                SC.op(lambda eng, psc=psc, dlo=dlo, c1=c1,
                                      bias=bias, PT=PT: eng.activation(
                                    out=PT[:, dlo + 128:c1],
                                    in_=psc[:, dlo + 128:c1],
                                    func=AF.Exp, scale=0.125, bias=bias))
                        elif c0 > dlo:
                            SC.op(lambda eng, psc=psc, c0=c0, c1=c1, bias=bias,
                                  PT=PT: eng.activation(
                                      out=PT[:, c0:c1], in_=psc[:, c0:c1],
                                      func=AF.Exp, scale=0.125, bias=bias))
                    exp_ac[j] = sAC.n
                    mask_dv[j] = sDV.n

                def emit_pv(j):
                    PT = PTD[j % 2]
                    if j == 0 and ts_dv[0]:
                        PE.wait(sDV, ts_dv[0])
                    PE.wait(sAC, exp_ac[j])
                    PE.wait(sDV, mask_dv[j])
                    for i2 in range(4):
                        mm(PS[6][:, i2 * 65:(i2 + 1) * 65],
                           PT[:, i2 * 128:(i2 + 1) * 128],
                           VN[j][:, 65 * h:65 * h + 65],
                           start=(j == 0 and i2 == 0),
                           stop=(j == nj - 1 and i2 == 3),
                           inc=(sPE if i2 == 3 else None))
                    pv_arr[j] = sPE.n

                for j in range(nj):
                    qkv = emit_qk(j)
                    emit_exp(j, qkv)
                    if j == 1 and pending_norm[0] is not None:
                        pending_norm[0]()
                        pending_norm[0] = None
                    if j >= 1:
                        emit_pv(j - 1)
                if pending_norm[0] is not None:
                    pending_norm[0]()
                    pending_norm[0] = None
                emit_pv(nj - 1)
                pv_pe = pv_arr[nj - 1]
                def make_norm(h=h, dtq=dtq, pbase=pbase,
                              pv_arr=pv_arr, nj=nj):
                    pv_pe = pv_arr[nj - 1]
                    DV.wait(sPE, pv_pe)
                    if ts_dv[0]:
                        DV.wait(sDV, ts_dv[0])
                    zcols = PS[6][:, 0:260].rearrange(
                        "p (i c) -> p i c", c=65)[:, :, 64:65]
                    DV.op(lambda eng, zcols=zcols, h=h: eng.tensor_tensor(
                        out=RCB[:, 0:4].unsqueeze(2), in0=zcols,
                        in1=cCV[:, h * 4:h * 4 + 4].unsqueeze(2), op=ALU.mult))
                    DV.op(lambda eng: eng.drain(fusable=False), inc=sDV)
                    SC.wait(sDV)
                    SC.op(lambda eng: eng.activation(
                        out=RCB[:, 4:8], in_=RCB[:, 0:4], func=AF.Ln))
                    SC.op(lambda eng: eng.activation(
                        out=RCB[:, 8:12], in_=RCB[:, 4:8], func=AF.Exp, scale=-1.0))
                    SC.op(lambda eng: eng.drain(fusable=False), inc=sAC)
                    DV.wait(sAC)
                    DV.op(lambda eng, h=h: eng.tensor_tensor(
                        out=RCB[:, 8:12], in0=RCB[:, 8:12],
                        in1=cCV[:, h * 4:h * 4 + 4], op=ALU.mult), inc=sDV)
                    DV.wait(sDV)
                    for i2 in range(4):
                        DV.op(lambda eng, i2=i2: eng.tensor_scalar(
                            out=STG64[:, i2 * 64:(i2 + 1) * 64],
                            in0=PS[6][:, i2 * 65: i2 * 65 + 64],
                            scalar1=RCB[:, 8 + i2:9 + i2], scalar2=None,
                            op0=ALU.mult), inc=(sDV if i2 == 3 else None))
                    if (os.environ.get("KDUMP") == "stg" and g == 0
                            and h == int(os.environ.get("KH", "2"))):
                        barrier()
                        SY.dma(lambda eng: eng.dma_start(
                            out=dbg[:, 0:256], in_=STG64), sIN)
                        SY.dma(lambda eng: eng.dma_start(
                            out=dbgf[:, 0:8], in_=RCB), sIN)
                        barrier()
                    PE.wait(sDV)
                    for i2 in range(4):
                        PE.op(lambda eng, i2=i2: eng.transpose(
                            PST[0:64, i2 * 128:(i2 + 1) * 128],
                            STG64[:, i2 * 64:(i2 + 1) * 64], cIdB),
                            inc=(sPE if i2 == 3 else None))
                    tr_pe[0] = sPE.n
                    SC.wait(sPE)
                    SC.op(lambda eng, pbase=pbase, dtq=dtq: eng.activation(
                        out=OTB[pbase:pbase + 64, dtq * 512:(dtq + 1) * 512],
                        in_=PST[0:64, 0:512], func=AF.Copy), inc=sAC)
                    PE.wait(sAC)
                pending_norm[0] = make_norm

            if pending_norm[0] is not None:
                pending_norm[0]()
                pending_norm[0] = None
            if g == 0 and os.environ.get("KDUMP") == "otb":
                barrier()
                SY.dma(lambda eng: eng.dma_start(out=dbg[:, 0:4096], in_=OTB), sIN)
                barrier()
            op_res = [0, 0]
            for dp in range(CT):
                pso = PS[2 + dp % 2]
                if dp >= 2:
                    PE.wait(sDV, op_res[dp % 2])
                for ct in range(CT):
                    mm(pso, WINB[:, ct * 1024 + dp * 128: ct * 1024 + (dp + 1) * 128],
                       OTB[:, ct * 512:(ct + 1) * 512],
                       start=(ct == 0), stop=False)
                mm(pso, AUGB[0:1, dp * 128:(dp + 1) * 128], cOn[0:1, 0:512],
                   start=False, stop=True, inc=sPE)
                DV.wait(sPE)
                DV.op(lambda eng, dp=dp, t1c=t1c, pso=pso: eng.tensor_tensor(
                    out=X[:, dp * T + t1c: dp * T + t1c + 512],
                    in0=X[:, dp * T + t1c: dp * T + t1c + 512], in1=pso,
                    op=ALU.add), inc=sDV)
                op_res[dp % 2] = sDV.n
        barrier()

    if n_sublayers >= 3:
        attn_b()
    if n_sublayers >= 4:
        ffn(reload=True)

    barrier()
    for tt in range(16):
        par = tt % 2
        sS = sSA if par == 0 else sSB
        if tt >= 2:
            PE.wait(sS, 16 * (tt // 2))
        PE.wait(sAC)
        for hf in range(2):
            bank = PS[hf]
            for cq in range(4):
                cb = hf * 4 + cq
                PE.op(lambda eng, cb=cb, cq=cq, tt=tt, bank=bank: eng.transpose(
                    bank[:, cq * 128:(cq + 1) * 128],
                    X[:, cb * T + tt * 128: cb * T + (tt + 1) * 128],
                    cIdF), inc=(sPE if cq == 3 else None))
            SC.wait(sPE)
            SC.op(lambda eng, hf=hf, par=par, bank=bank: cpy(eng,
                OST[:, par * 1024 + hf * 512: par * 1024 + hf * 512 + 512],
                bank), inc=sAC)
        SY.wait(sAC)
        SY.dma(lambda eng, tt=tt, par=par: eng.dma_start(
            out=out[tt * 128:(tt + 1) * 128, :],
            in_=OST[:, par * 1024:(par + 1) * 1024]), sS)
    SY.wait(sSA)
    SY.wait(sSB)

    def body(qname):
        def run(eng):
            for item in q[qname].items:
                if item[0] == "wait":
                    eng.wait_ge(item[1].h, item[2])
                else:
                    inst = item[1](eng)
                    for s, n in item[2]:
                        inst.then_inc(s.h, n)
        return run
    blk.sync(body("sync"))
    blk.gpsimd(body("gpsimd"))
    blk.vector(body("vector"))
    blk.scalar(body("scalar"))
    blk.tensor(body("tensor"))
    blk_ctx.__exit__(None, None, None)
    for ctx in sem_ctx:
        ctx.__exit__(None, None, None)
    return nc


_consts_cache = None

def kernel(L0_attn_a_x, L0_attn_a_time_decay, L0_attn_a_time_first,
           L0_attn_a_time_mix_k, L0_attn_a_time_mix_v, L0_attn_a_time_mix_r,
           L0_attn_a_Wk, L0_attn_a_Wv, L0_attn_a_Wr, L0_attn_a_Wo,
           L0_attn_b_wq, L0_attn_b_wk, L0_attn_b_wv, L0_attn_b_out_w,
           L0_attn_b_out_b, L0_ffn_G1_R, L0_ffn_G2_R, L0_ffn_G1_I, L0_ffn_G2_I,
           L0_ffn_bias_real, L0_ffn_bias_imag, _sublayers=4):
    global _consts_cache
    if _consts_cache is None:
        _consts_cache = host_constants()
    f32 = np.float32
    x = np.ascontiguousarray(np.asarray(L0_attn_a_x, f32))
    par = np.ascontiguousarray(np.stack([
        np.asarray(L0_attn_a_time_decay, f32).reshape(8, 128),
        np.asarray(L0_attn_a_time_first, f32).reshape(8, 128),
        np.asarray(L0_attn_a_time_mix_k, f32).reshape(8, 128),
        np.asarray(L0_attn_a_time_mix_v, f32).reshape(8, 128),
        np.asarray(L0_attn_a_time_mix_r, f32).reshape(8, 128)],
        axis=0).reshape(40, 128))
    g2r = np.ascontiguousarray(np.asarray(L0_ffn_G2_R, f32).reshape(4, 16, 16)
                               .transpose(1, 0, 2).reshape(16, 64))
    g2i = np.ascontiguousarray(np.asarray(L0_ffn_G2_I, f32).reshape(4, 16, 16)
                               .transpose(1, 0, 2).reshape(16, 64))
    base = {
        "a_Wk": np.ascontiguousarray(np.asarray(L0_attn_a_Wk, f32)),
        "a_Wv": np.ascontiguousarray(np.asarray(L0_attn_a_Wv, f32)),
        "a_Wr": np.ascontiguousarray(np.asarray(L0_attn_a_Wr, f32)),
        "a_Wo": np.ascontiguousarray(np.asarray(L0_attn_a_Wo, f32)),
        "b_wq": np.ascontiguousarray(np.asarray(L0_attn_b_wq, f32)),
        "b_wk": np.ascontiguousarray(np.asarray(L0_attn_b_wk, f32)),
        "b_wv": np.ascontiguousarray(np.asarray(L0_attn_b_wv, f32)),
        "b_out_w": np.ascontiguousarray(np.asarray(L0_attn_b_out_w, f32)),
        "a_par": par,
        "b_out_b": np.ascontiguousarray(np.asarray(L0_attn_b_out_b, f32))[None, :],
        "f_G1R": np.ascontiguousarray(np.asarray(L0_ffn_G1_R, f32).reshape(32, 128)),
        "f_G2R": g2r,
        "f_G1I": np.ascontiguousarray(np.asarray(L0_ffn_G1_I, f32).reshape(32, 128)),
        "f_G2I": g2i,
        "f_bias": np.concatenate([np.asarray(L0_ffn_bias_real, f32),
                                  np.asarray(L0_ffn_bias_imag, f32)])[None, :],
    }
    base.update(_consts_cache)
    nc = build(n_sublayers=_sublayers)
    in_maps = []
    for b in range(8):
        m = dict(base)
        m["xin"] = np.ascontiguousarray(x[b])
        in_maps.append(m)
    res = run_bass_kernel_spmd(nc, in_maps, core_ids=list(range(8)))
    return np.stack([np.asarray(res.results[b]["out"]) for b in range(8)], axis=0)
